# version 05
# baseline (speedup 1.0000x reference)
import contextlib
import numpy as np
import ml_dtypes
import concourse.bass as bass
import concourse.mybir as mybir
from concourse.bass_utils import run_bass_kernel_spmd

F32 = mybir.dt.float32
BF16 = mybir.dt.bfloat16
AF = mybir.ActivationFunctionType
ALU = mybir.AluOpType
AX = mybir.AxisListType

D = 1024
S = 8192
HEADS, HD = 16, 64
H = 1408
FBn = 8
HBn = 11
NT = 64
RING_X = 4
RING_XN = 20
RUN = 16
NGRP, NCH, CHT = 4, 2, 512
NEXP = 9

ENGINES = ("t", "v", "s", "g", "y")


class Sem:
    def __init__(self, handle, name):
        self.h = handle
        self.name = name
        self.n = 0


class Plan:

    def __init__(self):
        self.q = {e: [] for e in ENGINES}

    def wait(self, eng, sem, val=None):
        v = sem.n if val is None else val
        if v > 0:
            self.q[eng].append(("w", sem, v))

    def op(self, eng, fn, inc=None, inc_by=1):
        if inc is not None:
            inc.n += inc_by
        self.q[eng].append(("o", fn, inc, inc_by))

    def validate(self):
        counts = {}
        idx = {e: 0 for e in ENGINES}
        progress = True
        while progress:
            progress = False
            for e in ENGINES:
                while idx[e] < len(self.q[e]):
                    item = self.q[e][idx[e]]
                    if item[0] == "w":
                        if counts.get(item[1].name, 0) < item[2]:
                            break
                    else:
                        if item[2] is not None:
                            counts[item[2].name] = counts.get(item[2].name, 0) + item[3]
                    idx[e] += 1
                    progress = True
        stuck = {e: (idx[e], len(self.q[e])) for e in ENGINES if idx[e] < len(self.q[e])}
        if stuck:
            detail = {}
            for e in stuck:
                item = self.q[e][idx[e]]
                detail[e] = (item[1].name, item[2], counts.get(item[1].name, 0))
            raise RuntimeError(f"plan deadlock: {stuck} waiting on {detail}")

    def replay(self, block):
        deco = {"t": block.tensor, "v": block.vector, "s": block.scalar,
                "g": block.gpsimd, "y": block.sync}
        for e in ENGINES:
            items = self.q[e]
            if not items:
                continue

            def run(engobj, items=items):
                for item in items:
                    if item[0] == "w":
                        engobj.wait_ge(item[1].h, item[2])
                    else:
                        _, fn, inc, by = item
                        ins = fn(engobj)
                        if inc is not None:
                            ins.then_inc(inc.h, by)
            deco[e](run)


def build_nc(dbg=(), NT=NT, RUN=RUN, RING_XN=RING_XN, RING_X=RING_X, serial=0, nexp=NEXP, tap_e=0, tap_hb=0, experts=None):
    nc = bass.Bass("TRN2", target_bir_lowering=False, detect_race_conditions=False)
    dbg = set(dbg)

    din = {}
    def dt_in(name, shape):
        din[name] = nc.dram_tensor(name, list(shape), F32, kind="ExternalInput")
        return din[name]

    x_tm = dt_in("x_tm", (NT * 128, D))
    x_fm_V = dt_in("x_fm_V", (D, 4096))
    x_fm_F = dt_in("x_fm_F", (D, 4096))
    wq = dt_in("wq", (D, D)); wk = dt_in("wk", (D, D)); wv = dt_in("wv", (D, D))
    wo = dt_in("wo", (128, 16 * D))
    bo_p = dt_in("bo_p", (128, 8))
    gwt = dt_in("gwt", (D, 8)); gbr = dt_in("gbr", (1, 8))
    w1d = dt_in("w1d", (8, D, H)); w3d = dt_in("w3d", (8, D, H)); w2d = dt_in("w2d", (8, H, D))
    b1p = dt_in("b1p", (128, 88)); b3p = dt_in("b3p", (128, 88)); b2a = dt_in("b2a", (8, D))
    sw1 = dt_in("sw1", (D, H)); sw3 = dt_in("sw3", (D, H)); sw2 = dt_in("sw2", (H, D))
    sb1p = dt_in("sb1p", (128, 11)); sb3p = dt_in("sb3p", (128, 11)); sb2r = dt_in("sb2r", (1, D))
    idn = dt_in("idn", (128, 128))
    onesr = dt_in("onesr", (1, 512))
    onec = dt_in("onec", (1, 128))
    sel2 = dt_in("sel2", (2, 128))
    pFT = dt_in("pFT", (NT, 32)); pVT = dt_in("pVT", (NT, 32))
    selbk = dt_in("selbk", (16, 8 * 128))
    p0d = dt_in("p0d", (128, 512))

    out_d = nc.dram_tensor("out", [D, 4096], F32, kind="ExternalOutput")
    g_scr = nc.dram_tensor("g_scr", [D, 4096], BF16, kind="Internal")

    dbg_d = {}
    def dbg_out(name, shape, dtype=F32):
        if name in dbg:
            dbg_d[name] = nc.dram_tensor("dbg_" + name, list(shape), dtype, kind="ExternalOutput")

    dbg_out("stats_m", (128, NT)); dbg_out("stats_r", (128, NT))
    dbg_out("stats_s2", (128, NT)); dbg_out("stats_var", (128, NT)); dbg_out("stats_std", (128, NT))
    dbg_out("G", (128, FBn * D)); dbg_out("E", (16, 16)); dbg_out("att", (16, 16))
    dbg_out("xn0", (128, D), BF16)
    dbg_out("v_jq", (128, FBn * 4096), BF16)
    dbg_out("attv", (128, 4096), BF16)
    dbg_out("g_grp0", (128, FBn * 1024), BF16)
    dbg_out("rw0", (8, 512), BF16)
    dbg_out("lg0", (8, 512)); dbg_out("lgtm0", (128, 32)); dbg_out("sctm0", (128, 32))
    dbg_out("gbrep", (128, 8)); dbg_out("biastm0", (128, 32)); dbg_out("mskdtm0", (128, 32))
    dbg_out("h00", (128, 11 * 512), BF16); dbg_out("y0", (128, FBn * D))
    dbg_out("ye0", (128, FBn * D)); dbg_out("rep0", (128, 512), BF16)
    dbg_out("w13dump", (128, 2 * 8 * H), BF16)
    dbg_out("t100", (128, 512)); dbg_out("t300", (128, 512))
    dbg_out("rwtm0", (128, 32))
    dbg_out("s_head0", (128, 8 * 256), BF16)

    es0 = contextlib.ExitStack()
    sem = lambda n: Sem(es0.enter_context(nc.semaphore(n)), n)
    sINIT = sem("sINIT")
    sQK = sem("sQK")
    sX = [sem(f"sX{i}") for i in range(4)]
    sW13 = [sem(f"sW13{i}") for i in range(2)]
    sW2 = sem("sW2")
    sGS = sem("sGS")
    sDBG = sem("sDBG")
    sOUT = [sem(f"sOUT{i}") for i in range(2)]
    sGR = [sem(f"sGR{i}") for i in range(2)]
    sXR = [sem(f"sXR{i}") for i in range(2)]
    sT = sem("sT"); sV = sem("sV"); sS = sem("sS"); sG = sem("sG")

    sb0 = lambda n, sh, d=F32: es0.enter_context(nc.sbuf_tensor(n, list(sh), d))
    stats_m = sb0("stats_m", (128, NT))
    stats_r = sb0("stats_r", (128, NT))
    stats_mr = sb0("stats_mr", (128, NT))
    r_row = sb0("r_row", (NT, 128), BF16)
    mr_row = sb0("mr_row", (NT, 128), BF16)
    attT = sb0("attT", (16, 16), BF16)
    rF_row = sb0("rF_row", (32, 128), BF16)
    mrF_row = sb0("mrF_row", (32, 128), BF16)
    rV_row = sb0("rV_row", (32, 128), BF16)
    mrV_row = sb0("mrV_row", (32, 128), BF16)
    pF_sb = sb0("pF_sb", (NT, 32), BF16)
    pV_sb = sb0("pV_sb", (NT, 32), BF16)
    selbk_sb = sb0("selbk_sb", (16, 8 * 128), BF16)
    id_b = sb0("id_b", (128, 128), BF16)
    id_f = sb0("id_f", (128, 128))
    ones_b = sb0("ones_b", (1, 512), BF16)
    onec_b = sb0("onec_b", (1, 128), BF16)
    sel2_b = sb0("sel2_b", (2, 128), BF16)
    p0d_b = sb0("p0d_b", (128, 512), BF16)
    bo_sb = sb0("bo_sb", (128, 8))
    gwt_sb = sb0("gwt_sb", (128, 64), BF16)
    gb_rep = sb0("gb_rep", (128, 8))
    b1_sb = sb0("b1_sb", (128, 88)); b3_sb = sb0("b3_sb", (128, 88))
    sb1_sb = sb0("sb1_sb", (128, 11)); sb3_sb = sb0("sb3_sb", (128, 11))
    b2_sb = sb0("b2_sb", (8, D), BF16)
    sb2_sb = sb0("sb2_sb", (1, D), BF16)
    c_inv1024 = sb0("c_inv1024", (128, 1))
    c_eps = sb0("c_eps", (128, 1))
    c_0125 = sb0("c_0125", (128, 1))
    c_big = sb0("c_big", (128, 1))
    c_m0125 = sb0("c_m0125", (128, 1))
    junk = sb0("junk", (128, 1))
    c_m1 = sb0("c_m1", (128, 1))

    P = Plan()

    def vgap():
        P.op("v", lambda e: e.tensor_scalar(junk[:, :], c_eps[:, :], 1.0, None, ALU.mult))

    def join_all():
        for eng in ("t", "v", "s"):
            P.wait(eng, sT, sT.n)
            P.wait(eng, sV, sV.n)
            P.wait(eng, sS, sS.n)

    def dma(eng_letter, sem_, dst, src):
        P.op(eng_letter, lambda e, d=dst, s=src: e.dma_start(out=d, in_=s), inc=sem_, inc_by=16)
        return sem_.n

    dma("g", sINIT, id_b[:, :], idn[:, :])
    dma("g", sINIT, id_f[:, :], idn[:, :])
    dma("g", sINIT, ones_b[:, :], onesr[:, :])
    dma("g", sINIT, onec_b[:, :], onec[:, :])
    dma("g", sINIT, sel2_b[:, :], sel2[:, :])
    dma("g", sINIT, p0d_b[:, :], p0d[:, :])
    dma("g", sINIT, bo_sb[:, :], bo_p[:, :])
    dma("g", sINIT, gwt_sb[:, :].rearrange("p (kb e) -> p kb e", kb=8),
        gwt[:, :].rearrange("(kb p) e -> p kb e", p=128))
    dma("g", sINIT, b1_sb[:, :], b1p[:, :])
    dma("g", sINIT, b3_sb[:, :], b3p[:, :])
    dma("g", sINIT, sb1_sb[:, :], sb1p[:, :])
    dma("g", sINIT, sb3_sb[:, :], sb3p[:, :])
    dma("g", sINIT, b2_sb[:, :], b2a[:, :])
    dma("g", sINIT, sb2_sb[:, :], sb2r[:, :])
    init_wq = sINIT.n

    P.op("v", lambda e: e.memset(c_inv1024[:, :], 1.0 / 1024.0))
    P.op("v", lambda e: e.memset(c_eps[:, :], 1e-5))
    P.op("v", lambda e: e.memset(c_0125[:, :], 0.125))
    P.op("v", lambda e: e.memset(c_big[:, :], 1e9))
    P.op("v", lambda e: e.memset(c_m0125[:, :], -0.125))
    P.op("v", lambda e: e.memset(c_m1[:, :], -1.0), inc=sV)
    init_v = sV.n

    gb_row = sb0("gb_row", (1, 8), BF16)
    dma("g", sINIT, gb_row[:, :], gbr[:, :])
    dma("g", sINIT, pF_sb[:, :], pFT[:, :])
    dma("g", sINIT, pV_sb[:, :], pVT[:, :])
    dma("g", sINIT, selbk_sb[:, :], selbk[:, :])
    init_wq = sINIT.n

    es1 = contextlib.ExitStack()
    sb1 = lambda n, sh, d=F32: es1.enter_context(nc.sbuf_tensor(n, list(sh), d))
    xt_ring = [sb1(f"xt{i}", (128, D)) for i in range(RING_X)]
    stats_s2 = sb1("stats_s2", (128, NT))
    stats_s1 = sb1("stats_s1", (128, NT))
    stats_var = sb1("stats_var", (128, NT))
    stats_std = sb1("stats_std", (128, NT))
    sq_buf = sb1("sq", (128, D))
    xn_ring = [sb1(f"xn{i}", (128, D), BF16) for i in range(RING_XN)]
    G_sb = sb1("G_sb", (128, FBn * D))
    s1c = sb1("s1c", (128, 1)); mc = sb1("mc", (128, 1))
    s2c = sb1("s2c", (128, 1)); m2c = sb1("m2c", (128, 1))
    varc = sb1("varc", (128, 1)); stdc = sb1("stdc", (128, 1))
    pg_cms = [nc.psum_tensor(f"pg{i}", [128, D], F32) for i in range(4)]
    pg = [cm.__enter__() for cm in pg_cms]

    P.op("v", lambda e: e.memset(stats_s2[:, :], 0.0))
    P.op("v", lambda e: e.memset(stats_s1[:, :], 0.0), inc=sV)
    s2_zeroed = sV.n

    apply_done = [0] * NT
    bmm_done = [0] * NT
    ghalf_evac = {}

    for i in range(NT):
        run, pos = divmod(i, RUN)
        if i >= RING_X:
            P.wait("y", sV, apply_done[i - RING_X])
        sXs = sX[i % RING_X]
        ld_tgt = dma("y", sXs, xt_ring[i % RING_X][:, :], x_tm[128 * i:128 * (i + 1), :])
        xt = xt_ring[i % RING_X]

        P.wait("s", sXs, ld_tgt)
        P.wait("s", sV, init_v)
        P.wait("s", sV, s2_zeroed)
        P.op("s", lambda e, xt=xt, i=i: e.activation(sq_buf[:, :], xt[:, :], AF.Square,
                                                     accum_out=stats_s2[:, i:i + 1]))
        P.op("s", lambda e, xt=xt, i=i: e.activation(sq_buf[:, :], xt[:, :], AF.Identity,
                                                     accum_out=stats_s1[:, i:i + 1]), inc=sS)
        sq_done = sS.n
        P.wait("v", sS, sq_done)
        P.op("v", lambda e, i=i: e.tensor_scalar(stats_m[:, i:i + 1], stats_s1[:, i:i + 1],
                                                 c_inv1024[:, :], None, ALU.mult))
        P.op("v", lambda e, i=i: e.tensor_scalar(s2c[:, :], stats_s2[:, i:i + 1],
                                                 c_inv1024[:, :], None, ALU.mult))
        P.op("v", lambda e, i=i: e.tensor_mul(m2c[:, :], stats_m[:, i:i + 1], stats_m[:, i:i + 1]))
        vgap()
        P.op("v", lambda e, i=i: e.tensor_sub(stats_var[:, i:i + 1], s2c[:, :], m2c[:, :]), inc=sV)
        var_done = sV.n
        P.wait("s", sV, var_done)
        P.op("s", lambda e, i=i: e.activation(stats_std[:, i:i + 1], stats_var[:, i:i + 1],
                                              AF.Sqrt, bias=c_eps[:, :]), inc=sS)
        std_done = sS.n
        P.wait("v", sS, std_done)
        P.op("v", lambda e, i=i: e.reciprocal(stats_r[:, i:i + 1], stats_std[:, i:i + 1]))
        vgap()
        xn = xn_ring[i % RING_XN]
        if i >= RING_XN:
            P.wait("v", sT, bmm_done[i - RING_XN])
        P.op("v", lambda e, xt=xt, xn=xn, i=i: e.tensor_scalar(
            xn[:, :], xt[:, :], stats_m[:, i:i + 1], stats_r[:, i:i + 1],
            ALU.subtract, ALU.mult), inc=sV)
        apply_done[i] = sV.n
        if i == 0 and "xn0" in dbg:
            P.wait("y", sV, apply_done[0])
            dma("y", sDBG, dbg_d["xn0"][:, :], xn_ring[0][:, :])

        P.wait("t", sV, apply_done[i])
        if pos == 0 and run > 0:
            P.wait("t", sV, ghalf_evac[(run - 1, 1)])
        for mt_loc in range(4):
            mt = mt_loc
            for nch in range(2):
                P.op("t", lambda e, xn=xn, mt=mt, nch=nch, pos=pos, mt_loc=mt_loc:
                     e.matmul(pg[mt_loc][:, nch * 512:(nch + 1) * 512],
                              xn[:, mt * 128:(mt + 1) * 128],
                              xn[:, nch * 512:(nch + 1) * 512],
                              start=(pos == 0), stop=(pos == RUN - 1)),
                     inc=(sT if (mt_loc == 3 and nch == 1 and pos == RUN - 1) else None))
        if pos == RUN - 1:
            amms_done = sT.n
            P.wait("v", sT, amms_done)
            for mt_loc in range(4):
                if run == 0:
                    P.op("v", lambda e, mt_loc=mt_loc: e.tensor_copy(
                        G_sb[:, mt_loc * D:(mt_loc + 1) * D], pg[mt_loc][:, :]),
                        inc=(sV if mt_loc == 3 else None))
                else:
                    P.op("v", lambda e, mt_loc=mt_loc: e.tensor_add(
                        G_sb[:, mt_loc * D:(mt_loc + 1) * D],
                        G_sb[:, mt_loc * D:(mt_loc + 1) * D], pg[mt_loc][:, :]),
                        inc=(sV if mt_loc == 3 else None))
            ghalf_evac[(run, 0)] = sV.n
            P.wait("t", sV, ghalf_evac[(run, 0)])
            for j in range(run * RUN, run * RUN + RUN):
                xnj = xn_ring[j % RING_XN]
                for mt_loc in range(4):
                    mt = 4 + mt_loc
                    for nch in range(2):
                        last = (j == run * RUN + RUN - 1 and mt_loc == 3 and nch == 1)
                        P.op("t", lambda e, xn=xnj, mt=mt, nch=nch, j=j, mt_loc=mt_loc, run=run:
                             e.matmul(pg[mt_loc][:, nch * 512:(nch + 1) * 512],
                                      xn[:, mt * 128:(mt + 1) * 128],
                                      xn[:, nch * 512:(nch + 1) * 512],
                                      start=(j == run * RUN), stop=(j == run * RUN + RUN - 1)),
                             inc=(sT if last else None))
            bmms_done = sT.n
            for j in range(run * RUN, run * RUN + RUN):
                bmm_done[j] = bmms_done
            P.wait("v", sT, bmms_done)
            for mt_loc in range(4):
                if run == 0:
                    P.op("v", lambda e, mt_loc=mt_loc: e.tensor_copy(
                        G_sb[:, (4 + mt_loc) * D:(5 + mt_loc) * D], pg[mt_loc][:, :]),
                        inc=(sV if mt_loc == 3 else None))
                else:
                    P.op("v", lambda e, mt_loc=mt_loc: e.tensor_add(
                        G_sb[:, (4 + mt_loc) * D:(5 + mt_loc) * D],
                        G_sb[:, (4 + mt_loc) * D:(5 + mt_loc) * D], pg[mt_loc][:, :]),
                        inc=(sV if mt_loc == 3 else None))
            ghalf_evac[(run, 1)] = sV.n

    g_done = sV.n

    P.op("v", lambda e: e.tensor_mul(stats_mr[:, :], stats_m[:, :], stats_r[:, :]))
    r_bf = sb1("r_bf", (128, NT), BF16)
    mr_bf = sb1("mr_bf", (128, NT), BF16)
    P.op("v", lambda e: e.tensor_copy(r_bf[:, :], stats_r[:, :]))
    P.op("v", lambda e: e.tensor_copy(mr_bf[:, :], stats_mr[:, :]), inc=sV)
    rows_src = sV.n
    pg_cms[3].__exit__(None, None, None)
    pg_bf = es1.enter_context(nc.psum_tensor("pg_bf", [128, 1024], BF16))
    P.wait("t", sV, rows_src)
    P.op("t", lambda e: e.transpose(pg_bf[0:NT, 0:128], r_bf[:, :], id_b[:, :]), inc=sT)
    P.op("t", lambda e: e.transpose(pg_bf[0:NT, 128:256], mr_bf[:, :], id_b[:, :]), inc=sT)
    rows_t = sT.n
    P.wait("v", sT, rows_t)
    P.op("v", lambda e: e.tensor_copy(r_row[:, :], pg_bf[0:NT, 0:128]))
    P.op("v", lambda e: e.tensor_copy(mr_row[:, :], pg_bf[0:NT, 128:256]), inc=sV)
    rows_done = sV.n

    P.wait("t", sINIT, init_wq)
    P.op("t", lambda e: e.matmul(pg[0][0:128, 0:8], onec_b[:, :], gb_row[:, :],
                                 start=True, stop=True), inc=sT)
    gbrep_t = sT.n
    P.wait("v", sT, gbrep_t)
    P.op("v", lambda e: e.tensor_copy(gb_rep[:, :], pg[0][0:128, 0:8]), inc=sV)
    gbrep_done = sV.n

    if "stats_m" in dbg:
        P.wait("y", sV, g_done)
        dma("y", sDBG, dbg_d["stats_m"][:, :], stats_m[:, :])
        dma("y", sDBG, dbg_d["stats_r"][:, :], stats_r[:, :])
    if "stats_s2" in dbg:
        P.wait("y", sV, g_done)
        P.wait("y", sS, sS.n)
        dma("y", sDBG, dbg_d["stats_s2"][:, :], stats_s2[:, :])
        dma("y", sDBG, dbg_d["stats_var"][:, :], stats_var[:, :])
        dma("y", sDBG, dbg_d["stats_std"][:, :], stats_std[:, :])
    if "G" in dbg:
        P.wait("y", sV, g_done)
        dma("y", sDBG, dbg_d["G"][:, :], G_sb[:, :])

    G_hi = sb1("G_hi", (128, FBn * D), BF16)
    G_lo = sb1("G_lo", (128, FBn * D), BF16)
    wq_sb = sb1("wq_sb", (128, FBn * D), BF16)
    wk_sb = sb1("wk_sb", (128, FBn * D), BF16)
    m1_hi = sb1("m1_hi", (128, FBn * D), BF16)
    m1_lo = sb1("m1_lo", (128, FBn * D), BF16)

    dma("g", sQK, wq_sb[:, :].rearrange("p (kb c) -> p kb c", kb=8),
        wq[:, :].rearrange("(kb p) c -> p kb c", p=128))
    dma("g", sQK, wk_sb[:, :].rearrange("p (kb c) -> p kb c", kb=8),
        wk[:, :].rearrange("(kb p) c -> p kb c", p=128))
    wqk_ld = sQK.n

    P.wait("s", sV, g_done)
    P.wait("v", sV, g_done)
    for fb in range(FBn):
        P.op("s", lambda e, fb=fb: e.activation(G_hi[:, fb * D:(fb + 1) * D],
                                                G_sb[:, fb * D:(fb + 1) * D], AF.Identity),
             inc=sS)
    ghi_done = sS.n
    P.wait("v", sS, ghi_done)
    for fb in range(FBn):
        P.op("v", lambda e, fb=fb: e.tensor_sub(G_lo[:, fb * D:(fb + 1) * D],
                                                G_sb[:, fb * D:(fb + 1) * D],
                                                G_hi[:, fb * D:(fb + 1) * D]),
             inc=(sV if fb == FBn - 1 else None))
    glo_done = sV.n

    m1_evac = {}
    P.wait("t", sV, glo_done)
    P.wait("t", sQK, wqk_ld)
    for it in range(FBn):
        pgm = pg[0] if it % 2 == 0 else pg[2]
        if it >= 2:
            P.wait("t", sV, m1_evac[it - 2])
        for nch in range(2):
            n_mm = 0
            for srcbuf in (G_hi, G_lo):
                for jt in range(FBn):
                    n_mm += 1
                    last = (n_mm == 16)
                    P.op("t", lambda e, srcbuf=srcbuf, jt=jt, it=it, nch=nch, pgm=pgm,
                         st_=(n_mm == 1), sp_=last:
                         e.matmul(pgm[:, nch * 512:(nch + 1) * 512],
                                  srcbuf[:, jt * D + it * 128: jt * D + (it + 1) * 128],
                                  wk_sb[:, jt * D + nch * 512: jt * D + (nch + 1) * 512],
                                  start=st_, stop=sp_),
                         inc=(sT if last else None))
        m1_t = sT.n
        P.wait("s", sT, m1_t)
        P.op("s", lambda e, it=it, pgm=pgm: e.activation(
            m1_hi[:, it * D:(it + 1) * D], pgm[:, :], AF.Identity), inc=sS)
        m1hi_s = sS.n
        P.wait("v", sS, m1hi_s)
        P.op("v", lambda e, it=it, pgm=pgm: e.tensor_sub(
            m1_lo[:, it * D:(it + 1) * D], pgm[:, :], m1_hi[:, it * D:(it + 1) * D]),
            inc=sV)
        m1_evac[it] = sV.n
    m1_done = sV.n

    pe = pg[0]
    P.wait("t", sV, m1_done)
    n_mm = 0
    total_mm = FBn * HD * 2
    for it in range(FBn):
        for d in range(HD):
            for srcbuf in (m1_hi, m1_lo):
                n_mm += 1
                P.op("t", lambda e, it=it, d=d, srcbuf=srcbuf,
                     st_=(n_mm == 1), sp_=(n_mm == total_mm):
                     e.matmul(pe[0:16, 0:16],
                              wq_sb[:, it * D + d * 16: it * D + (d + 1) * 16],
                              srcbuf[:, it * D + d * 16: it * D + (d + 1) * 16],
                              start=st_, stop=sp_),
                     inc=(sT if n_mm == total_mm else None))
    e_done_t = sT.n

    negmax = sb1("negmax", (16, 1)); nb = sb1("nb", (16, 1))
    exp_sb = sb1("exp_sb", (16, 16)); ssum = sb1("ssum", (16, 1)); rcs = sb1("rcs", (16, 1))
    att_b = sb1("att_b", (16, 16), BF16)
    if "E" in dbg:
        e_cp = sb1("e_cp", (16, 16))
        P.wait("v", sT, e_done_t)
        P.op("v", lambda e: e.tensor_copy(e_cp[:, :], pe[0:16, 0:16]), inc=sV)
        P.wait("y", sV, sV.n)
        dma("y", sDBG, dbg_d["E"][:, :], e_cp[:, :])
    P.wait("v", sT, e_done_t)
    P.op("v", lambda e: e.reduce_max(negmax[:, :], pe[0:16, 0:16], AX.X))
    vgap()
    P.op("v", lambda e: e.tensor_scalar(nb[:, :], negmax[:, :], c_m0125[0:16, :], None, ALU.mult),
         inc=sV)
    nb_done = sV.n
    P.wait("s", sV, nb_done)
    P.op("s", lambda e: e.activation(exp_sb[:, :], pe[0:16, 0:16], AF.Exp,
                                     bias=nb[:, :], scale=0.125), inc=sS)
    exp_done = sS.n
    P.wait("v", sS, exp_done)
    P.op("v", lambda e: e.reduce_sum(ssum[:, :], exp_sb[:, :], AX.X))
    vgap()
    P.op("v", lambda e: e.reciprocal(rcs[:, :], ssum[:, :]))
    vgap()
    P.op("v", lambda e: e.tensor_scalar(att_b[:, :], exp_sb[:, :], rcs[:, :], None, ALU.mult),
         inc=sV)
    att_done = sV.n
    P.wait("t", sV, att_done)
    P.op("t", lambda e: e.transpose(pg_bf[0:16, 128:144], att_b[:, :], id_b[0:16, 0:16]),
         inc=sT)
    attT_t = sT.n
    P.wait("v", sT, attT_t)
    P.op("v", lambda e: e.tensor_copy(attT[:, :], pg_bf[0:16, 128:144]), inc=sV)
    attT_done = sV.n

    if "att" in dbg:
        att_f = sb1("att_f", (16, 16))
        P.wait("v", sV, att_done)
        P.op("v", lambda e: e.tensor_copy(att_f[:, :], att_b[:, :]), inc=sV)
        P.wait("y", sV, sV.n)
        dma("y", sDBG, dbg_d["att"][:, :], att_f[:, :])


    P.wait("t", sV, rows_done)
    P.wait("t", sV, m1_done)
    P.wait("t", sINIT, init_wq)
    psel = pg[2]
    P.op("t", lambda e: e.matmul(psel[0:32, 0:128], pF_sb[:, :], r_row[:, :], start=True, stop=True))
    P.op("t", lambda e: e.matmul(psel[0:32, 128:256], pF_sb[:, :], mr_row[:, :], start=True, stop=True))
    P.op("t", lambda e: e.matmul(psel[0:32, 256:384], pV_sb[:, :], r_row[:, :], start=True, stop=True))
    P.op("t", lambda e: e.matmul(psel[0:32, 384:512], pV_sb[:, :], mr_row[:, :], start=True, stop=True),
         inc=sT)
    sel_t = sT.n
    P.wait("v", sT, sel_t)
    P.op("v", lambda e: e.tensor_copy(rF_row[:, :], psel[0:32, 0:128]))
    P.op("v", lambda e: e.tensor_copy(mrF_row[:, :], psel[0:32, 128:256]))
    P.op("v", lambda e: e.tensor_copy(rV_row[:, :], psel[0:32, 256:384]))
    P.op("v", lambda e: e.tensor_copy(mrV_row[:, :], psel[0:32, 384:512]), inc=sV)
    selrows_done = sV.n

    ph1 = {sT: sT.n, sV: sV.n, sS: sS.n}
    for eng in ENGINES:
        for s_, v_ in ph1.items():
            P.wait(eng, s_, v_)
    es1.close()
    for cm in (pg_cms[2], pg_cms[1], pg_cms[0]):
        cm.__exit__(None, None, None)

    es2 = contextlib.ExitStack()
    sb2 = lambda n, sh, d=F32: es2.enter_context(nc.sbuf_tensor(n, list(sh), d))
    w13 = [sb2(f"w13_{i}", (128, 2 * 8 * H), BF16) for i in range(2)]
    W3OFF = 8 * H
    w2b = sb2("w2b", (128, HBn * D), BF16)
    xng = sb2("xng", (128, FBn * D), BF16)
    yg = sb2("yg", (128, FBn * D))
    xf_ring = [sb2(f"xf{i}", (128, 512)) for i in range(4)]
    t1b = [sb2(f"t1b{i}", (128, 512)) for i in range(2)]
    t3b = [sb2(f"t3b{i}", (128, 512)) for i in range(2)]
    hbuf = sb2("hbuf", (128, HBn * 512), BF16)
    rep_b = [sb2(f"rep{i}", (128, 512), BF16) for i in range(2)]
    rw_fm = [sb2(f"rwfm{i}", (8, 512), BF16) for i in range(2)]
    lg_sb = sb2("lg_sb", (8, 512))
    lg_tm = sb2("lg_tm", (128, 32))
    ex_tm = sb2("ex_tm", (128, 32))
    sc_tm = sb2("sc_tm", (128, 32))
    bias_tm = sb2("bias_tm", (128, 32))
    mskd_tm = sb2("mskd_tm", (128, 32))
    rw_tm = sb2("rw_tm", (128, 32))
    mxc = sb2("mxc", (128, 1)); nmc = sb2("nmc", (128, 1))
    smc = sb2("smc", (128, 1)); rpc = sb2("rpc", (128, 1))
    m1c2 = sb2("m1c2", (128, 1)); m2c2 = sb2("m2c2", (128, 1))
    mskc = sb2("mskc", (128, 8)); mkc = sb2("mkc", (128, 8)); keepc = sb2("keepc", (128, 8))
    pa_cms = [nc.psum_tensor(f"pa{i}", [128, 512], F32) for i in range(4)]
    pa = [cm.__enter__() for cm in pa_cms]
    pw_cms = [nc.psum_tensor(f"pw{i}", [128, 512], F32) for i in range(2)]
    pw = [cm.__enter__() for cm in pw_cms]
    pm_cms = [nc.psum_tensor(f"pm{i}", [128, 512], F32) for i in range(2)]
    pm = [cm.__enter__() for cm in pm_cms]

    xf_free_v = [0] * 4
    xn_ready = {}
    rwfm_s = {}
    rep_s = [0, 0]
    gdma_y = 0
    gsilu_s = 0
    a_end_t = {}
    w2_end_t = {}
    w1_ld = {}; w3_ld = {}; w2_ld = {}
    pa_free = [("sS", 0)] * 4
    pw_free = [0, 0]
    pm0_free = ("sS", 0)
    pm1_free = ("sV", 0)
    hmul_last = 0
    t1_free = [0, 0]
    t3_free = [("sS", 0), ("sS", 0)]

    def wsrc(eng, tag_cnt):
        tag, cnt = tag_cnt
        P.wait(eng, {"sS": sS, "sV": sV, "sT": sT}[tag], cnt)

    for g in range(NGRP):
        for ch in range(NCH):
            c0 = g * 1024 + ch * 512
            wsrc("t", pm0_free); wsrc("t", pm1_free)
            P.wait("t", sV, selrows_done)
            for k4 in range(4):
                row = 8 * g + 4 * ch + k4
                P.op("t", lambda e, row=row, k4=k4: e.matmul(
                    pm[0][:, k4 * 128:(k4 + 1) * 128],
                    bass.AP(id_b, row, [[128, 32], [0, 128]]), rF_row[0:32, :],
                    start=True, stop=True))
                P.op("t", lambda e, row=row, k4=k4: e.matmul(
                    pm[1][:, k4 * 128:(k4 + 1) * 128],
                    bass.AP(id_b, row, [[128, 32], [0, 128]]), mrF_row[0:32, :],
                    start=True, stop=True),
                    inc=(sT if k4 == 3 else None))
            ab_t = sT.n
            if g > 0 and ch == 0:
                P.wait("v", sGS, gdma_y)
            for fb in range(FBn):
                slot = (ch * FBn + fb) % 4
                P.wait("y", sV, xf_free_v[slot])
                ld_t = dma("y", sX[slot], xf_ring[slot][:, :],
                           x_fm_F[fb * 128:(fb + 1) * 128, c0:c0 + 512])
                P.wait("v", sX[slot], ld_t)
                P.wait("v", sT, ab_t)
                xf = xf_ring[slot]
                P.op("v", lambda e, xf=xf: e.tensor_mul(xf[:, :], xf[:, :], pm[0][:, :]))
                P.op("v", lambda e, xf=xf, fb=fb, ch=ch: e.tensor_sub(
                    xng[:, fb * D + ch * 512: fb * D + ch * 512 + 512], xf[:, :], pm[1][:, :]),
                    inc=sV)
                xf_free_v[slot] = sV.n
            xn_ready[(g, ch)] = sV.n
            P.wait("t", sV, xn_ready[(g, ch)])
            for kb in range(FBn):
                P.op("t", lambda e, kb=kb, ch=ch: e.matmul(
                    pm[0][0:8, 0:512], gwt_sb[:, kb * 8:(kb + 1) * 8],
                    xng[:, kb * D + ch * 512: kb * D + ch * 512 + 512],
                    start=(kb == 0), stop=(kb == FBn - 1)),
                    inc=(sT if kb == FBn - 1 else None))
            gate_t = sT.n
            P.wait("s", sT, gate_t)
            P.op("s", lambda e: e.activation(lg_sb[:, :], pm[0][0:8, 0:512], AF.Identity), inc=sS)
            lg_s = sS.n
            if g == 0 and ch == 0 and "lg0" in dbg:
                P.wait("y", sS, lg_s)
                dma("y", sDBG, dbg_d["lg0"][:, :], lg_sb[:, :])
            pm0_free = ("sS", lg_s)
            P.wait("t", sS, lg_s)
            P.wait("t", sV, xn_ready[(g, ch)])
            for k4 in range(4):
                P.op("t", lambda e, k4=k4: e.transpose(
                    pm[1][0:128, k4 * 8:(k4 + 1) * 8], lg_sb[0:8, k4 * 128:(k4 + 1) * 128],
                    id_f[0:8, 0:8]),
                    inc=(sT if k4 == 3 else None))
            lgT_t = sT.n
            P.wait("v", sT, lgT_t)
            P.op("v", lambda e: e.tensor_copy(lg_tm[:, :], pm[1][:, 0:32]), inc=sV)
            if g == 0 and ch == 0 and "lgtm0" in dbg:
                P.wait("y", sV, sV.n)
                dma("y", sDBG, dbg_d["lgtm0"][:, :], lg_tm[:, :])
            for b4 in range(4):
                sl = slice(b4 * 8, (b4 + 1) * 8)
                P.op("v", lambda e, sl=sl: e.reduce_max(mxc[:, :], lg_tm[:, sl], AX.X))
                vgap()
                P.op("v", lambda e: e.tensor_scalar(nmc[:, :], mxc[:, :], c_m1[:, :], None, ALU.mult),
                     inc=sV)
                nm_v = sV.n
                P.wait("s", sV, nm_v)
                P.op("s", lambda e, sl=sl: e.activation(ex_tm[:, sl], lg_tm[:, sl], AF.Exp,
                                                        bias=nmc[:, :]), inc=sS)
                ex_s = sS.n
                P.wait("v", sS, ex_s)
                P.op("v", lambda e, sl=sl: e.reduce_sum(smc[:, :], ex_tm[:, sl], AX.X))
                vgap()
                P.op("v", lambda e: e.reciprocal(rpc[:, :], smc[:, :]))
                vgap()
                P.op("v", lambda e, sl=sl: e.tensor_scalar(sc_tm[:, sl], ex_tm[:, sl],
                                                           rpc[:, :], None, ALU.mult))
                vgap()
                P.op("v", lambda e, sl=sl: e.tensor_add(bias_tm[:, sl], sc_tm[:, sl], gb_rep[:, :]))
                vgap()
                P.op("v", lambda e, sl=sl: e.reduce_max(m1c2[:, :], bias_tm[:, sl], AX.X))
                vgap()
                P.op("v", lambda e, sl=sl: e.tensor_scalar(mskc[:, :], bias_tm[:, sl],
                                                           m1c2[:, :], None, ALU.is_ge))
                vgap()
                P.op("v", lambda e: e.tensor_scalar(mkc[:, :], mskc[:, :], c_big[:, :], None, ALU.mult))
                vgap()
                P.op("v", lambda e, sl=sl: e.tensor_sub(mskd_tm[:, sl], bias_tm[:, sl], mkc[:, :]))
                vgap()
                P.op("v", lambda e, sl=sl: e.reduce_max(m2c2[:, :], mskd_tm[:, sl], AX.X))
                vgap()
                P.op("v", lambda e, sl=sl: e.tensor_scalar(keepc[:, :], bias_tm[:, sl],
                                                           m2c2[:, :], None, ALU.is_ge))
                vgap()
                P.op("v", lambda e, sl=sl: e.tensor_mul(rw_tm[:, sl], sc_tm[:, sl], keepc[:, :]),
                     inc=(sV if b4 == 3 else None))
            rw_v = sV.n
            if g == 0 and ch == 0 and "sctm0" in dbg:
                P.wait("y", sV, rw_v)
                dma("y", sDBG, dbg_d["sctm0"][:, :], sc_tm[:, :])
                dma("y", sDBG, dbg_d["gbrep"][:, :], gb_rep[:, :])
                dma("y", sDBG, dbg_d["biastm0"][:, :], bias_tm[:, :])
                dma("y", sDBG, dbg_d["mskdtm0"][:, :], mskd_tm[:, :])
                dma("y", sDBG, dbg_d["rwtm0"][:, :], rw_tm[:, :])
            P.wait("t", sV, rw_v)
            for k4 in range(4):
                P.op("t", lambda e, k4=k4: e.transpose(
                    pm[1][0:8, k4 * 128:(k4 + 1) * 128], rw_tm[:, k4 * 8:(k4 + 1) * 8],
                    id_f[:, :]),
                    inc=(sT if k4 == 3 else None))
            rwT_t = sT.n
            P.wait("s", sT, rwT_t)
            P.op("s", lambda e, ch=ch: e.activation(rw_fm[ch][:, :], pm[1][0:8, 0:512], AF.Identity),
                 inc=sS)
            rwfm_s[(g, ch)] = sS.n
            pm1_free = ("sS", sS.n)
            if g == 0 and ch == 0 and "rw0" in dbg:
                P.wait("y", sS, rwfm_s[(g, ch)])
                dma("y", sDBG, dbg_d["rw0"][:, :], rw_fm[0][:, :])

        elist = list(range(nexp)) if experts is None else experts
        for eidx in elist:
            L = g * NEXP + eidx
            shared = (eidx == 8)
            if L >= 2:
                P.wait("g", sT, a_end_t.get(L - 2, max(list(a_end_t.values()) or [0])))
            srcs = (w1d[eidx], w3d[eidx]) if not shared else (sw1[:, :], sw3[:, :])
            for wi, srcw in enumerate(srcs):
                ap = srcw[:, :] if not shared else srcw
                ap = ap.rearrange("(kb p) h -> p kb h", p=128)
                dstbuf = w13[L % 2]
                off = 0 if wi == 0 else W3OFF
                P.op("g", lambda e, ap=ap, dstbuf=dstbuf, off=off: e.dma_start(
                    out=dstbuf[:, off:off + 8 * H].rearrange("p (kb h) -> p kb h", kb=8),
                    in_=ap), inc=sW13[L % 2], inc_by=16)
            w3_ld[L] = sW13[L % 2].n
            if L >= 1:
                P.wait("g", sT, w2_end_t.get(L - 1, max(list(w2_end_t.values()) or [0])))
            srcw2 = w2d[eidx][:, :] if not shared else sw2[:, :]
            ap2 = srcw2.rearrange("(kb p) dd -> p kb dd", p=128)
            P.op("g", lambda e, ap2=ap2: e.dma_start(
                out=w2b[:, :].rearrange("p (kb dd) -> p kb dd", kb=HBn), in_=ap2),
                inc=sW2, inc_by=16)
            w2_ld[L] = sW2.n
            if not shared:
                for ch in range(NCH):
                    P.wait("t", sS, rwfm_s[(g, ch)])
                    wsrc("t", pm0_free)
                    P.op("t", lambda e, ch=ch, eidx=eidx: e.matmul(
                        pm[0][:, :], bass.AP(id_b, eidx, [[128, 8], [0, 128]]),
                        rw_fm[ch][0:8, :], start=True, stop=True), inc=sT)
                    rept = sT.n
                    P.wait("s", sT, rept)
                    P.wait("s", sV, hmul_last)
                    P.op("s", lambda e, ch=ch: e.activation(rep_b[ch][:, :], pm[0][:, :],
                                                            AF.Identity), inc=sS)
                    rep_s[ch] = sS.n
                    pm0_free = ("sS", sS.n)
            for ch in range(NCH):
                if g == 0 and eidx == tap_e and ch == 0 and "w13dump" in dbg:
                    P.wait("y", sW13[L % 2], w3_ld[L])
                    dma("y", sDBG, dbg_d["w13dump"][:, :], w13[L % 2][:, :])
                    P.wait("t", sDBG, sDBG.n)
                for hb in range(HBn):
                    p2 = hb % 2
                    if serial >= 1:
                        join_all()
                    P.wait("t", sW13[L % 2], w3_ld[L])
                    P.wait("t", sV, xn_ready[(g, ch)])
                    wsrc("t", pa_free[2 * p2])
                    wsrc("t", pa_free[2 * p2 + 1])
                    for kb in range(FBn):
                        P.op("t", lambda e, kb=kb, hb=hb, ch=ch, p2=p2, L=L: e.matmul(
                            pa[2 * p2][:, :], w13[L % 2][:, kb * H + hb * 128: kb * H + (hb + 1) * 128],
                            xng[:, kb * D + ch * 512: kb * D + ch * 512 + 512],
                            start=(kb == 0), stop=(kb == FBn - 1)),
                            inc=(sT if kb == FBn - 1 else None))
                    a1_t = sT.n
                    for kb in range(FBn):
                        P.op("t", lambda e, kb=kb, hb=hb, ch=ch, p2=p2, L=L: e.matmul(
                            pa[2 * p2 + 1][:, :],
                            w13[L % 2][:, W3OFF + kb * H + hb * 128: W3OFF + kb * H + (hb + 1) * 128],
                            xng[:, kb * D + ch * 512: kb * D + ch * 512 + 512],
                            start=(kb == 0), stop=(kb == FBn - 1)),
                            inc=(sT if kb == FBn - 1 else None))
                    a3_t = sT.n
                    bcol = (sb1_sb if shared else b1_sb)[:, (hb if shared else eidx * HBn + hb):
                                                         (hb if shared else eidx * HBn + hb) + 1]
                    bcol3 = (sb3_sb if shared else b3_sb)[:, (hb if shared else eidx * HBn + hb):
                                                          (hb if shared else eidx * HBn + hb) + 1]
                    P.wait("s", sT, a1_t)
                    P.wait("s", sV, t1_free[p2])
                    P.op("s", lambda e, p2=p2, bcol=bcol: e.activation(
                        t1b[p2][:, :], pa[2 * p2][:, :], AF.Identity, bias=bcol), inc=sS)
                    t1_s = sS.n
                    pa_free[2 * p2] = ("sS", t1_s)
                    P.wait("s", sT, a3_t)
                    P.op("s", lambda e, p2=p2, bcol3=bcol3: e.activation(
                        pa[2 * p2 + 1][:, :], pa[2 * p2 + 1][:, :], AF.Identity, bias=bcol3),
                        inc=sS)
                    b3_s = sS.n
                    P.wait("v", sS, b3_s)
                    wsrc("v", t3_free[p2])
                    P.op("v", lambda e, p2=p2: e.tensor_mul(t3b[p2][:, :], t1b[p2][:, :],
                                                            pa[2 * p2 + 1][:, :]), inc=sV)
                    prod_v = sV.n
                    if g == 0 and eidx == tap_e and ch == 0 and hb == tap_hb and "t100" in dbg:
                        P.wait("y", sV, prod_v)
                        dma("y", sDBG, dbg_d["t100"][:, :], t1b[0][:, :])
                        dma("y", sDBG, dbg_d["t300"][:, :], t3b[0][:, :])
                        P.wait("s", sDBG, sDBG.n)
                    t1_free[p2] = prod_v
                    pa_free[2 * p2 + 1] = ("sV", prod_v)
                    P.wait("s", sV, prod_v)
                    if shared:
                        P.op("s", lambda e, p2=p2, hb=hb: e.activation(
                            hbuf[:, hb * 512:(hb + 1) * 512], t3b[p2][:, :], AF.Silu), inc=sS)
                        t3_free[p2] = ("sS", sS.n)
                    else:
                        P.op("s", lambda e, p2=p2: e.activation(t3b[p2][:, :], t3b[p2][:, :],
                                                                AF.Silu), inc=sS)
                        silu_s = sS.n
                        P.wait("v", sS, silu_s)
                        P.wait("v", sS, rep_s[ch])
                        P.op("v", lambda e, p2=p2, hb=hb, ch=ch: e.tensor_mul(
                            hbuf[:, hb * 512:(hb + 1) * 512], t3b[p2][:, :], rep_b[ch][:, :]),
                            inc=sV)
                        t3_free[p2] = ("sV", sV.n)
                        hmul_last = sV.n
                if g == 0 and eidx == tap_e and ch == 0 and "h00" in dbg:
                    P.wait("y", sV, hmul_last)
                    P.wait("y", sS, sS.n)
                    dma("y", sDBG, dbg_d["h00"][:, :], hbuf[:, :])
                    if "rep0" in dbg:
                        dma("y", sDBG, dbg_d["rep0"][:, :], rep_b[0][:, :])
                    P.wait("v", sDBG, sDBG.n)
                    P.wait("s", sDBG, sDBG.n)
                if ch == NCH - 1:
                    a_end_t[L] = sT.n
                P.wait("t", sW2, w2_ld[L])
                if shared:
                    P.wait("t", sS, sS.n)
                else:
                    P.wait("t", sV, hmul_last)
                for db in range(FBn):
                    pwb = pw[db % 2]
                    if serial >= 1:
                        join_all()
                    P.wait("t", sV, pw_free[db % 2])
                    first = True
                    if eidx == 0:
                        P.op("t", lambda e, db=db, ch=ch: e.matmul(
                            pw[db % 2][:, :], b2_sb[:, db * 128:(db + 1) * 128],
                            rw_fm[ch][:, :], start=True, stop=False))
                        first = False
                    if shared:
                        P.op("t", lambda e, db=db: e.matmul(
                            pw[db % 2][:, :], sb2_sb[:, db * 128:(db + 1) * 128],
                            ones_b[:, :], start=True, stop=False))
                        first = False
                    for kb in range(HBn):
                        P.op("t", lambda e, kb=kb, db=db, first=first: e.matmul(
                            pw[db % 2][:, :], w2b[:, kb * D + db * 128: kb * D + (db + 1) * 128],
                            hbuf[:, kb * 512:(kb + 1) * 512],
                            start=(first and kb == 0), stop=(kb == HBn - 1)),
                            inc=(sT if kb == HBn - 1 else None))
                    w2db_t = sT.n
                    P.wait("v", sT, w2db_t)
                    if eidx == elist[0]:
                        if g > 0 and db == 0 and ch == 0:
                            P.wait("v", sS, gsilu_s)
                        P.op("v", lambda e, db=db, ch=ch: e.tensor_copy(
                            yg[:, db * D + ch * 512: db * D + ch * 512 + 512], pw[db % 2][:, :]),
                            inc=sV)
                    else:
                        P.op("v", lambda e, db=db, ch=ch: e.tensor_add(
                            yg[:, db * D + ch * 512: db * D + ch * 512 + 512],
                            yg[:, db * D + ch * 512: db * D + ch * 512 + 512], pw[db % 2][:, :]),
                            inc=sV)
                    pw_free[db % 2] = sV.n
                if g == 0 and eidx == tap_e and ch == NCH - 1 and "ye0" in dbg:
                    P.wait("y", sV, sV.n)
                    dma("y", sDBG, dbg_d["ye0"][:, :], yg[:, :])
                    P.wait("t", sDBG, sDBG.n)
                if ch == NCH - 1:
                    w2_end_t[L] = sT.n
        if g == 0 and "y0" in dbg:
            P.wait("y", sV, sV.n)
            dma("y", sDBG, dbg_d["y0"][:, :], yg[:, :])
            P.wait("s", sDBG, sDBG.n)
        P.wait("s", sV, sV.n)
        P.wait("s", sT, w2_end_t[g * NEXP + elist[-1]])
        for fb in range(FBn):
            P.op("s", lambda e, fb=fb: e.activation(
                xng[:, fb * D:(fb + 1) * D], yg[:, fb * D:(fb + 1) * D], AF.Silu),
                inc=(sS if fb == FBn - 1 else None))
        gsilu_s = sS.n
        P.wait("y", sS, gsilu_s)
        P.op("y", lambda e, g=g: e.dma_start(
            out=g_scr[:, :].rearrange("(fb p) c -> p fb c", p=128)[:, :, g * 1024:(g + 1) * 1024],
            in_=xng[:, :].rearrange("p (fb c) -> p fb c", fb=FBn)),
            inc=sGS, inc_by=16)
        gdma_y = sGS.n
        if g == 0 and "g_grp0" in dbg:
            P.wait("y", sS, gsilu_s)
            dma("y", sDBG, dbg_d["g_grp0"][:, :], xng[:, :])

    ffn_done = {"sT": sT.n, "sV": sV.n, "sS": sS.n}


    for eng in ENGINES:
        for s_, v_ in ffn_done.items():
            P.wait(eng, {"sT": sT, "sV": sV, "sS": sS}[s_], v_)
    es2.close()
    for cm in (pm_cms[1], pm_cms[0], pw_cms[1], pw_cms[0],
               pa_cms[3], pa_cms[2], pa_cms[1], pa_cms[0]):
        cm.__exit__(None, None, None)

    es3 = contextlib.ExitStack()
    sb3 = lambda n, sh, d=F32: es3.enter_context(nc.sbuf_tensor(n, list(sh), d))
    wv_sb = sb3("wv_sb", (128, FBn * D), BF16)
    wo_sb = sb3("wo_sb", (128, 16 * D), BF16)
    v_jq = sb3("v_jq", (128, FBn * 4096), BF16)
    xnv = [sb3(f"xnv{i}", (128, FBn * 512), BF16) for i in range(2)]
    xv_ring = [sb3(f"xv{i}", (128, 512)) for i in range(2)]
    out_jq = [sb3(f"ojq{i}", (128, 4096), BF16) for i in range(2)]
    bd_sb = [sb3(f"bd{i}", (128, FBn * 128), BF16) for i in range(2)]
    s_tile = [sb3(f"stl{i}", (128, FBn * 256), BF16) for i in range(2)]
    g_tile = [sb3(f"gtl{i}", (128, FBn * 256), BF16) for i in range(2)]
    xr_tile = [sb3(f"xrt{i}", (128, FBn * 256)) for i in range(2)]
    pscr = sb3("pscr", (128, FBn * 256))
    pv_cms = [nc.psum_tensor(f"pv{i}", [128, 512], F32) for i in range(2)]
    pv = [cm.__enter__() for cm in pv_cms]
    pav_cms = [nc.psum_tensor(f"pav{i}", [128, 512], F32) for i in range(2)]
    pav = [cm.__enter__() for cm in pav_cms]
    ps_cms = [nc.psum_tensor(f"ps{i}", [128, 512], F32) for i in range(4)]
    ps_ = [cm.__enter__() for cm in ps_cms]

    dma("g", sW13[0], wv_sb[:, :].rearrange("p (kb c) -> p kb c", kb=FBn),
        wv[:, :].rearrange("(kb p) c -> p kb c", p=128))
    wv_ld = sW13[0].n
    dma("g", sW13[1], wo_sb[:, :], wo[:, :])
    wo_ld = sW13[1].n

    xnv_free_t = [0, 0]
    xv_free_v = [0] * 4
    vp_free = [0, 0]
    vch_evac = {}
    comb_v_free = {}
    stores_prev = {}
    for c in range(8):
        par = c % 2
        P.wait("t", sV, selrows_done)
        if c >= 1:
            P.wait("t", sV, vch_evac[c - 1])
        for k4 in range(4):
            row = 4 * c + k4
            P.op("t", lambda e, row=row, k4=k4: e.matmul(
                ps_[0][:, k4 * 128:(k4 + 1) * 128],
                bass.AP(id_b, row, [[128, 32], [0, 128]]), rV_row[0:32, :],
                start=True, stop=True))
            P.op("t", lambda e, row=row, k4=k4: e.matmul(
                ps_[1][:, k4 * 128:(k4 + 1) * 128],
                bass.AP(id_b, row, [[128, 32], [0, 128]]), mrV_row[0:32, :],
                start=True, stop=True),
                inc=(sT if k4 == 3 else None))
        ab_t = sT.n
        P.wait("v", sT, ab_t)
        if c >= 2:
            P.wait("v", sT, xnv_free_t[par])
        for fb in range(FBn):
            slot = (c * FBn + fb) % 2
            P.wait("y", sV, xv_free_v[slot])
            ld_t = dma("y", sX[slot], xv_ring[slot][:, :],
                       x_fm_V[fb * 128:(fb + 1) * 128, c * 512:(c + 1) * 512])
            P.wait("v", sX[slot], ld_t)
            xv = xv_ring[slot]
            P.op("v", lambda e, xv=xv: e.tensor_mul(xv[:, :], xv[:, :], ps_[0][:, :]))
            P.op("v", lambda e, xv=xv, fb=fb, par=par: e.tensor_sub(
                xnv[par][:, fb * 512:(fb + 1) * 512], xv[:, :], ps_[1][:, :]), inc=sV)
            xv_free_v[slot] = sV.n
        xn_v = sV.n
        P.wait("t", sV, xn_v)
        P.wait("t", sW13[0], wv_ld)
        for mb in range(FBn):
            pb = pv[mb % 2]
            if c * FBn + mb >= 2:
                P.wait("t", sV, vp_free[mb % 2])
            for kb in range(FBn):
                P.op("t", lambda e, kb=kb, mb=mb, par=par, pb=pb: e.matmul(
                    pb[:, :], wv_sb[:, kb * D + mb * 128: kb * D + (mb + 1) * 128],
                    xnv[par][:, kb * 512:(kb + 1) * 512],
                    start=(kb == 0), stop=(kb == FBn - 1)),
                    inc=(sT if kb == FBn - 1 else None))
            mm_t = sT.n
            P.wait("v", sT, mm_t)
            dst = bass.AP(v_jq, mb * 4096 + c * 32, [[FBn * 4096, 128], [256, 16], [1, 32]])
            srcp = bass.AP(pb, 0, [[512, 128], [1, 16], [16, 32]])
            P.op("v", lambda e, dst=dst, srcp=srcp: e.tensor_copy(dst, srcp), inc=sV)
            vp_free[mb % 2] = sV.n
        vch_evac[c] = sV.n
        xnv_free_t[par] = sT.n
    v_done = sV.n
    if "v_jq" in dbg:
        P.wait("y", sV, v_done)
        dma("y", sDBG, dbg_d["v_jq"][:, :], v_jq[:, :])

    bd_free_t = [0, 0]
    ojq_free_t = [0, 0]
    st_done = {}
    comb_free = {}
    ps_free = {}
    gx_ld = {}
    pav_free_v = [0]
    bd_v_map = {}

    def emit_bd2(tp):
        par_t = tp % 2
        P.wait("t", sV, attT_done)
        P.wait("t", sV, pav_free_v[0])
        P.wait("t", sINIT, init_wq)
        rhs = bass.AP(attT, 2 * tp, [[16, 16], [1, 2], [0, 64]])
        for t in range(8):
            P.op("t", lambda e, t=t, rhs=rhs: e.matmul(
                pav[t // 4][:, (t % 4) * 128:(t % 4 + 1) * 128],
                selbk_sb[:, t * 128:(t + 1) * 128], rhs,
                start=True, stop=True),
                inc=(sT if t % 4 == 3 else None))
        rep_t = sT.n
        P.wait("v", sT, rep_t)
        P.op("v", lambda e, par_t=par_t: e.tensor_mul(
            bd_sb[par_t][:, 0:512], pav[0][:, :], p0d_b[:, :]))
        P.op("v", lambda e, par_t=par_t: e.tensor_mul(
            bd_sb[par_t][:, 512:1024], pav[1][:, :], p0d_b[:, :]), inc=sV)
        bd_v_map[tp] = sV.n

    ps_free[(0, 0)] = 0
    emit_bd2(0)
    for tp in range(8):
        par_t = tp % 2
        for par in range(2):
            a = 2 * tp + par
            gx_ld[(tp, par, "g")] = dma(
                "y", sGR[par], g_tile[par][:, :].rearrange("p (fb c) -> p fb c", fb=FBn),
                g_scr[:, :].rearrange("(fb p) c -> p fb c", p=128)[:, :, a * 256:(a + 1) * 256])
            gx_ld[(tp, par, "x")] = dma(
                "y", sXR[par], xr_tile[par][:, :].rearrange("p (fb c) -> p fb c", fb=FBn),
                x_fm_F[:, :].rearrange("(fb p) c -> p fb c", p=128)[:, :, a * 256:(a + 1) * 256])
        avevac = {}
        P.wait("t", sV, bd_v_map[tp])
        P.wait("t", sV, v_done)
        P.wait("t", sT, ojq_free_t[par_t])
        for cc in range(8):
            pb = pav[cc % 2]
            if cc >= 2:
                P.wait("t", sV, avevac[cc - 2])
            for t in range(8):
                P.op("t", lambda e, t=t, cc=cc, pb=pb, par_t=par_t: e.matmul(
                    pb[:, :], bd_sb[par_t][:, t * 128:(t + 1) * 128],
                    v_jq[:, t * 4096 + cc * 512: t * 4096 + (cc + 1) * 512],
                    start=(t == 0), stop=(t == 7)),
                    inc=(sT if t == 7 else None))
            avt = sT.n
            P.wait("v", sT, avt)
            P.op("v", lambda e, cc=cc, pb=pb, par_t=par_t: e.tensor_copy(
                out_jq[par_t][:, cc * 512:(cc + 1) * 512], pb[:, :]), inc=sV)
            avevac[cc] = sV.n
        ojq_v = sV.n
        pav_free_v[0] = ojq_v
        if tp == 0 and "attv" in dbg:
            P.wait("y", sV, ojq_v)
            dma("y", sDBG, dbg_d["attv"][:, :], out_jq[0][:, :])
        P.wait("t", sV, ojq_v)
        P.wait("t", sW13[1], wo_ld)
        for half in range(2):
            for par in range(2):
                a_loc = par
                for ob in range(half * 4, half * 4 + 4):
                    P.wait("t", sS, ps_free.get((ob % 4, par), 0))
                    for j in range(16):
                        P.op("t", lambda e, j=j, ob=ob, par=par, par_t=par_t: e.matmul(
                            ps_[ob % 4][:, par * 256:(par + 1) * 256],
                            wo_sb[par * 64:(par + 1) * 64, j * D + ob * 128:(j * D + (ob + 1) * 128)],
                            out_jq[par_t][par * 64:(par + 1) * 64, j * 256:(j + 1) * 256],
                            start=(j == 0), stop=(j == 15),
                            tile_position=(par * 64, 0)),
                            inc=(sT if j == 15 else None))
                    st_done[(par, ob)] = sT.n
            for par in range(2):
                for ob in range(half * 4, half * 4 + 4):
                    P.wait("s", sT, st_done[(par, ob)])
                    P.wait("s", sG, comb_v_free.get(par, 0))
                    P.op("s", lambda e, ob=ob, par=par: e.activation(
                        s_tile[par][:, ob * 256:(ob + 1) * 256],
                        ps_[ob % 4][:, par * 256:(par + 1) * 256],
                        AF.Identity, bias=bo_sb[:, ob:ob + 1]),
                        inc=sS)
                    ps_free[(ob % 4, par)] = sS.n
        s_done = sS.n
        if tp + 1 < 8:
            emit_bd2(tp + 1)
        for par in range(2):
            a = 2 * tp + par
            gld = gx_ld[(tp, par, "g")]
            xld = gx_ld[(tp, par, "x")]
            P.wait("g", sGR[par], gld)
            P.wait("g", sXR[par], xld)
            P.wait("g", sS, s_done)
            P.wait("g", sOUT[par], stores_prev.get(par, 0))
            for ob in range(FBn):
                P.op("g", lambda e, ob=ob, par=par: e.tensor_mul(
                    pscr[:, ob * 256:(ob + 1) * 256],
                    s_tile[par][:, ob * 256:(ob + 1) * 256],
                    g_tile[par][:, ob * 256:(ob + 1) * 256]))
            for ob in range(FBn):
                P.op("g", lambda e, ob=ob, par=par: e.tensor_add(
                    xr_tile[par][:, ob * 256:(ob + 1) * 256],
                    xr_tile[par][:, ob * 256:(ob + 1) * 256],
                    pscr[:, ob * 256:(ob + 1) * 256]),
                    inc=(sG if ob == FBn - 1 else None))
            comb_v = sG.n
            comb_v_free[par] = comb_v
            P.wait("y", sG, comb_v)
            st = dma("y", sOUT[par],
                     out_d[:, :].rearrange("(fb p) c -> p fb c", p=128)[:, :, a * 256:(a + 1) * 256],
                     xr_tile[par][:, :].rearrange("p (fb c) -> p fb c", fb=FBn))
            stores_prev[par] = st
            if a == 0 and "s_head0" in dbg:
                P.wait("y", sS, s_done)
                dma("y", sDBG, dbg_d["s_head0"][:, :], s_tile[0][:, :])
        bd_free_t[par_t] = sT.n
        ojq_free_t[par_t] = sT.n

    P.wait("y", sDBG, sDBG.n)
    P.wait("y", sGS, sGS.n)
    for s_ in sOUT:
        P.wait("y", s_, s_.n)
    P.wait("y", sV, attT_done)
    P.wait("y", sV, rows_done)
    P.wait("y", sV, gbrep_done)
    P.validate()
    with nc.Block() as block:
        P.replay(block)
    es3.close()
    es0.close()
    return nc


BF_ML = ml_dtypes.bfloat16


def make_consts():
    idn = np.eye(128, dtype=np.float32)
    onesr = np.ones((1, 512), np.float32)
    onec = np.ones((1, 128), np.float32)
    sel2 = np.zeros((2, 128), np.float32)
    sel2[0, 0:64] = 1.0
    sel2[1, 64:128] = 1.0
    p0d = np.zeros((128, 128), np.float32)
    for p in range(128):
        for f in range(p % 64, 128, 64):
            p0d[p, f] = 1.0
    p0d = np.tile(p0d, (1, 4))
    return idn, onesr, onec, sel2, p0d


def host_inputs(inp, core):
    n, h = core // 2, core % 2
    x = np.ascontiguousarray(np.asarray(inp["L0_attn_x"], np.float32)[n])
    F_rows = np.array([a * 512 + h * 256 + q for a in range(16) for q in range(256)])
    perm_da = np.array([(i % 16) * 64 + i // 16 for i in range(D)])
    Wq = np.asarray(inp["L0_attn_Wq"], np.float32)
    Wk = np.asarray(inp["L0_attn_Wk"], np.float32)
    Wo = np.asarray(inp["L0_attn_Wo"], np.float32)
    wo_dup = np.empty((128, 16 * D), np.float32)
    for j in range(16):
        blk = Wo[64 * j:64 * (j + 1), :]
        wo_dup[0:64, j * D:(j + 1) * D] = blk
        wo_dup[64:128, j * D:(j + 1) * D] = blk
    bo = np.asarray(inp["L0_attn_bo"], np.float32)
    bo_p = np.ascontiguousarray(bo.reshape(8, 128).T)
    gate_w = np.asarray(inp["L0_ffn_gate_w"], np.float32)
    b1 = np.asarray(inp["L0_ffn_b1"], np.float32)
    b3 = np.asarray(inp["L0_ffn_b3"], np.float32)
    b1p = np.ascontiguousarray(b1.reshape(8, 11, 128).transpose(2, 0, 1).reshape(128, 88))
    b3p = np.ascontiguousarray(b3.reshape(8, 11, 128).transpose(2, 0, 1).reshape(128, 88))
    sb1_ = np.asarray(inp["L0_ffn_sb1"], np.float32).reshape(11, 128).T
    sb3_ = np.asarray(inp["L0_ffn_sb3"], np.float32).reshape(11, 128).T
    idn, onesr, onec, sel2, p0d = make_consts()
    pFT = np.zeros((64, 32), np.float32)
    for k in range(32):
        pFT[4 * (k // 2) + 2 * h + (k % 2), k] = 1.0
    pVT = np.zeros((64, 32), np.float32)
    for k in range(32):
        pVT[32 * h + k, k] = 1.0
    selbk = np.zeros((16, 8 * 128), np.float32)
    for t in range(8):
        for p in range(128):
            selbk[2 * t + p // 64, t * 128 + p] = 1.0
    d = {
        "x_tm": x,
        "x_fm_V": np.ascontiguousarray(x[h * 4096:(h + 1) * 4096].T),
        "x_fm_F": np.ascontiguousarray(x[F_rows].T),
        "wq": np.ascontiguousarray(Wq[:, perm_da]),
        "wk": np.ascontiguousarray(Wk[:, perm_da]),
        "wv": np.asarray(inp["L0_attn_Wv"], np.float32),
        "wo": wo_dup,
        "bo_p": np.ascontiguousarray(bo_p),
        "gwt": np.ascontiguousarray(gate_w.T),
        "gbr": np.asarray(inp["L0_ffn_gate_b"], np.float32).reshape(1, 8),
        "w1d": np.asarray(inp["L0_ffn_w1"], np.float32),
        "w3d": np.asarray(inp["L0_ffn_w3"], np.float32),
        "w2d": np.asarray(inp["L0_ffn_w2"], np.float32),
        "b1p": b1p, "b3p": b3p,
        "b2a": np.asarray(inp["L0_ffn_b2"], np.float32),
        "sw1": np.asarray(inp["L0_ffn_sw1"], np.float32),
        "sw3": np.asarray(inp["L0_ffn_sw3"], np.float32),
        "sw2": np.asarray(inp["L0_ffn_sw2"], np.float32),
        "sb1p": np.ascontiguousarray(sb1_), "sb3p": np.ascontiguousarray(sb3_),
        "sb2r": np.asarray(inp["L0_ffn_sb2"], np.float32).reshape(1, D),
        "idn": idn, "onesr": onesr, "onec": onec, "sel2": sel2, "p0d": p0d,
        "pFT": pFT, "pVT": pVT, "selbk": selbk,
    }
    return d, F_rows


def kernel(**inputs):
    nc = build_nc()
    in_maps = []
    frows = []
    for c in range(8):
        m, fr = host_inputs(inputs, c)
        in_maps.append(m)
        frows.append(fr)
    res = run_bass_kernel_spmd(nc, in_maps, core_ids=list(range(8)))
    out = np.empty((4, S, D), np.float32)
    for c in range(8):
        n = c // 2
        out[n, frows[c], :] = res.results[c]["out"].T
    return out


# version 06 (final)
# speedup vs baseline: 1.2547x; 1.2547x over previous
import contextlib
import numpy as np
import ml_dtypes
import concourse.bass as bass
import concourse.mybir as mybir
from concourse.bass_utils import run_bass_kernel_spmd

F32 = mybir.dt.float32
BF16 = mybir.dt.bfloat16
AF = mybir.ActivationFunctionType
ALU = mybir.AluOpType
AX = mybir.AxisListType

D = 1024
S = 8192
HEADS, HD = 16, 64
H = 1408
FBn = 8
HBn = 11
NT = 64
RING_X = 4
RING_XN = 20
RUN = 16
NGRP, NCH, CHT = 4, 2, 512
NEXP = 9

ENGINES = ("t", "v", "s", "g", "y")


class Sem:
    def __init__(self, handle, name):
        self.h = handle
        self.name = name
        self.n = 0


class Plan:

    def __init__(self):
        self.q = {e: [] for e in ENGINES}

    def wait(self, eng, sem, val=None):
        v = sem.n if val is None else val
        if v > 0:
            self.q[eng].append(("w", sem, v))

    def op(self, eng, fn, inc=None, inc_by=1):
        if inc is not None:
            inc.n += inc_by
        self.q[eng].append(("o", fn, inc, inc_by))

    def validate(self):
        counts = {}
        idx = {e: 0 for e in ENGINES}
        progress = True
        while progress:
            progress = False
            for e in ENGINES:
                while idx[e] < len(self.q[e]):
                    item = self.q[e][idx[e]]
                    if item[0] == "w":
                        if counts.get(item[1].name, 0) < item[2]:
                            break
                    else:
                        if item[2] is not None:
                            counts[item[2].name] = counts.get(item[2].name, 0) + item[3]
                    idx[e] += 1
                    progress = True
        stuck = {e: (idx[e], len(self.q[e])) for e in ENGINES if idx[e] < len(self.q[e])}
        if stuck:
            detail = {}
            for e in stuck:
                item = self.q[e][idx[e]]
                detail[e] = (item[1].name, item[2], counts.get(item[1].name, 0))
            raise RuntimeError(f"plan deadlock: {stuck} waiting on {detail}")

    def replay(self, block):
        deco = {"t": block.tensor, "v": block.vector, "s": block.scalar,
                "g": block.gpsimd, "y": block.sync}
        for e in ENGINES:
            items = self.q[e]
            if not items:
                continue

            def run(engobj, items=items):
                for item in items:
                    if item[0] == "w":
                        engobj.wait_ge(item[1].h, item[2])
                    else:
                        _, fn, inc, by = item
                        ins = fn(engobj)
                        if inc is not None:
                            ins.then_inc(inc.h, by)
            deco[e](run)


def build_nc(dbg=(), NT=NT, RUN=RUN, RING_XN=RING_XN, RING_X=RING_X, serial=0, nexp=NEXP, tap_e=0, tap_hb=0, experts=None):
    nc = bass.Bass("TRN2", target_bir_lowering=False, detect_race_conditions=False)
    dbg = set(dbg)

    din = {}
    def dt_in(name, shape):
        din[name] = nc.dram_tensor(name, list(shape), F32, kind="ExternalInput")
        return din[name]

    x_tm = dt_in("x_tm", (NT * 128, D))
    x_fm_V = dt_in("x_fm_V", (D, 4096))
    x_fm_F = dt_in("x_fm_F", (D, 4096))
    wq = dt_in("wq", (D, D)); wk = dt_in("wk", (D, D)); wv = dt_in("wv", (D, D))
    wo = dt_in("wo", (128, 16 * D))
    bo_p = dt_in("bo_p", (128, 8))
    gwt = dt_in("gwt", (D, 8)); gbr = dt_in("gbr", (1, 8))
    w1d = dt_in("w1d", (8, D, H)); w3d = dt_in("w3d", (8, D, H)); w2d = dt_in("w2d", (8, H, D))
    b1p = dt_in("b1p", (128, 88)); b3p = dt_in("b3p", (128, 88)); b2a = dt_in("b2a", (8, D))
    sw1 = dt_in("sw1", (D, H)); sw3 = dt_in("sw3", (D, H)); sw2 = dt_in("sw2", (H, D))
    sb1p = dt_in("sb1p", (128, 11)); sb3p = dt_in("sb3p", (128, 11)); sb2r = dt_in("sb2r", (1, D))
    idn = dt_in("idn", (128, 128))
    onesr = dt_in("onesr", (1, 512))
    onec = dt_in("onec", (1, 128))
    sel2 = dt_in("sel2", (2, 128))
    pFT = dt_in("pFT", (NT, 32)); pVT = dt_in("pVT", (NT, 32))
    selbk = dt_in("selbk", (16, 8 * 128))
    p0d = dt_in("p0d", (128, 512))

    out_d = nc.dram_tensor("out", [D, 4096], F32, kind="ExternalOutput")
    g_scr = nc.dram_tensor("g_scr", [D, 4096], BF16, kind="Internal")

    dbg_d = {}
    def dbg_out(name, shape, dtype=F32):
        if name in dbg:
            dbg_d[name] = nc.dram_tensor("dbg_" + name, list(shape), dtype, kind="ExternalOutput")

    dbg_out("stats_m", (128, NT)); dbg_out("stats_r", (128, NT))
    dbg_out("stats_s2", (128, NT)); dbg_out("stats_var", (128, NT)); dbg_out("stats_std", (128, NT))
    dbg_out("G", (128, FBn * D)); dbg_out("E", (16, 16)); dbg_out("att", (16, 16))
    dbg_out("xn0", (128, D), BF16)
    dbg_out("v_jq", (128, FBn * 4096), BF16)
    dbg_out("attv", (128, 4096), BF16)
    dbg_out("g_grp0", (128, FBn * 1024), BF16)
    dbg_out("rw0", (8, 512), BF16)
    dbg_out("lg0", (8, 512)); dbg_out("lgtm0", (128, 32)); dbg_out("sctm0", (128, 32))
    dbg_out("gbrep", (128, 8)); dbg_out("biastm0", (128, 32)); dbg_out("mskdtm0", (128, 32))
    dbg_out("h00", (128, 11 * 512), BF16); dbg_out("y0", (128, FBn * D))
    dbg_out("ye0", (128, FBn * D)); dbg_out("rep0", (128, 512), BF16)
    dbg_out("w13dump", (128, 2 * 8 * H), BF16)
    dbg_out("t100", (128, 512)); dbg_out("t300", (128, 512))
    dbg_out("rwtm0", (128, 32))
    dbg_out("s_head0", (128, 8 * 256), BF16)

    es0 = contextlib.ExitStack()
    sem = lambda n: Sem(es0.enter_context(nc.semaphore(n)), n)
    sINIT = sem("sINIT")
    sQK = sem("sQK")
    sX = [sem(f"sX{i}") for i in range(4)]
    sW13 = [sem(f"sW13{i}") for i in range(2)]
    sW2 = sem("sW2")
    sGS = sem("sGS")
    sDBG = sem("sDBG")
    sOUT = [sem(f"sOUT{i}") for i in range(2)]
    sGR = [sem(f"sGR{i}") for i in range(2)]
    sXR = [sem(f"sXR{i}") for i in range(2)]
    sT = sem("sT"); sV = sem("sV"); sS = sem("sS"); sG = sem("sG")

    sb0 = lambda n, sh, d=F32: es0.enter_context(nc.sbuf_tensor(n, list(sh), d))
    stats_m = sb0("stats_m", (128, NT))
    stats_r = sb0("stats_r", (128, NT))
    stats_mr = sb0("stats_mr", (128, NT))
    r_row = sb0("r_row", (NT, 128), BF16)
    mr_row = sb0("mr_row", (NT, 128), BF16)
    attT = sb0("attT", (16, 16), BF16)
    rF_row = sb0("rF_row", (32, 128), BF16)
    mrF_row = sb0("mrF_row", (32, 128), BF16)
    rV_row = sb0("rV_row", (32, 128), BF16)
    mrV_row = sb0("mrV_row", (32, 128), BF16)
    pF_sb = sb0("pF_sb", (NT, 32), BF16)
    pV_sb = sb0("pV_sb", (NT, 32), BF16)
    selbk_sb = sb0("selbk_sb", (16, 8 * 128), BF16)
    id_b = sb0("id_b", (128, 128), BF16)
    id_f = sb0("id_f", (128, 128))
    ones_b = sb0("ones_b", (1, 512), BF16)
    onec_b = sb0("onec_b", (1, 128), BF16)
    sel2_b = sb0("sel2_b", (2, 128), BF16)
    p0d_b = sb0("p0d_b", (128, 512), BF16)
    bo_sb = sb0("bo_sb", (128, 8))
    gwt_sb = sb0("gwt_sb", (128, 64), BF16)
    gb_rep = sb0("gb_rep", (128, 8))
    b1_sb = sb0("b1_sb", (128, 88)); b3_sb = sb0("b3_sb", (128, 88))
    sb1_sb = sb0("sb1_sb", (128, 11)); sb3_sb = sb0("sb3_sb", (128, 11))
    b2_sb = sb0("b2_sb", (8, D), BF16)
    sb2_sb = sb0("sb2_sb", (1, D), BF16)
    c_inv1024 = sb0("c_inv1024", (128, 1))
    c_eps = sb0("c_eps", (128, 1))
    c_0125 = sb0("c_0125", (128, 1))
    c_big = sb0("c_big", (128, 1))
    c_m0125 = sb0("c_m0125", (128, 1))
    junk = sb0("junk", (128, 1))
    c_m1 = sb0("c_m1", (128, 1))

    P = Plan()

    def vgap():
        P.op("v", lambda e: e.tensor_scalar(junk[:, :], c_eps[:, :], 1.0, None, ALU.mult))

    def join_all():
        for eng in ("t", "v", "s"):
            P.wait(eng, sT, sT.n)
            P.wait(eng, sV, sV.n)
            P.wait(eng, sS, sS.n)

    def dma(eng_letter, sem_, dst, src):
        P.op(eng_letter, lambda e, d=dst, s=src: e.dma_start(out=d, in_=s), inc=sem_, inc_by=16)
        return sem_.n

    dma("g", sINIT, id_b[:, :], idn[:, :])
    dma("g", sINIT, id_f[:, :], idn[:, :])
    dma("g", sINIT, ones_b[:, :], onesr[:, :])
    dma("g", sINIT, onec_b[:, :], onec[:, :])
    dma("g", sINIT, sel2_b[:, :], sel2[:, :])
    dma("g", sINIT, p0d_b[:, :], p0d[:, :])
    dma("g", sINIT, bo_sb[:, :], bo_p[:, :])
    dma("g", sINIT, gwt_sb[:, :].rearrange("p (kb e) -> p kb e", kb=8),
        gwt[:, :].rearrange("(kb p) e -> p kb e", p=128))
    dma("g", sINIT, b1_sb[:, :], b1p[:, :])
    dma("g", sINIT, b3_sb[:, :], b3p[:, :])
    dma("g", sINIT, sb1_sb[:, :], sb1p[:, :])
    dma("g", sINIT, sb3_sb[:, :], sb3p[:, :])
    dma("g", sINIT, b2_sb[:, :], b2a[:, :])
    dma("g", sINIT, sb2_sb[:, :], sb2r[:, :])
    init_wq = sINIT.n

    P.op("v", lambda e: e.memset(c_inv1024[:, :], 1.0 / 1024.0))
    P.op("v", lambda e: e.memset(c_eps[:, :], 1e-5))
    P.op("v", lambda e: e.memset(c_0125[:, :], 0.125))
    P.op("v", lambda e: e.memset(c_big[:, :], 1e9))
    P.op("v", lambda e: e.memset(c_m0125[:, :], -0.125))
    P.op("v", lambda e: e.memset(c_m1[:, :], -1.0), inc=sV)
    init_v = sV.n

    gb_row = sb0("gb_row", (1, 8), BF16)
    dma("g", sINIT, gb_row[:, :], gbr[:, :])
    dma("g", sINIT, pF_sb[:, :], pFT[:, :])
    dma("g", sINIT, pV_sb[:, :], pVT[:, :])
    dma("g", sINIT, selbk_sb[:, :], selbk[:, :])
    init_wq = sINIT.n

    es1 = contextlib.ExitStack()
    sb1 = lambda n, sh, d=F32: es1.enter_context(nc.sbuf_tensor(n, list(sh), d))
    xt_ring = [sb1(f"xt{i}", (128, D)) for i in range(RING_X)]
    stats_s2 = sb1("stats_s2", (128, NT))
    stats_s1 = sb1("stats_s1", (128, NT))
    stats_var = sb1("stats_var", (128, NT))
    stats_std = sb1("stats_std", (128, NT))
    sq_buf = sb1("sq", (128, D))
    xn_ring = [sb1(f"xn{i}", (128, D), BF16) for i in range(RING_XN)]
    G_sb = sb1("G_sb", (128, FBn * D))
    s1c = sb1("s1c", (128, 1)); mc = sb1("mc", (128, 1))
    s2c = sb1("s2c", (128, 1)); m2c = sb1("m2c", (128, 1))
    varc = sb1("varc", (128, 1)); stdc = sb1("stdc", (128, 1))
    pg_cms = [nc.psum_tensor(f"pg{i}", [128, D], F32) for i in range(4)]
    pg = [cm.__enter__() for cm in pg_cms]

    P.op("v", lambda e: e.memset(stats_s2[:, :], 0.0))
    P.op("v", lambda e: e.memset(stats_s1[:, :], 0.0), inc=sV)
    s2_zeroed = sV.n

    apply_done = [0] * NT
    bmm_done = [0] * NT
    ghalf_evac = {}

    for i in range(NT):
        run, pos = divmod(i, RUN)
        if i >= RING_X:
            P.wait("y", sV, apply_done[i - RING_X])
        sXs = sX[i % RING_X]
        ld_tgt = dma("y", sXs, xt_ring[i % RING_X][:, :], x_tm[128 * i:128 * (i + 1), :])
        xt = xt_ring[i % RING_X]

        P.wait("s", sXs, ld_tgt)
        P.wait("s", sV, init_v)
        P.wait("s", sV, s2_zeroed)
        P.op("s", lambda e, xt=xt, i=i: e.activation(sq_buf[:, :], xt[:, :], AF.Square,
                                                     accum_out=stats_s2[:, i:i + 1]))
        P.op("s", lambda e, xt=xt, i=i: e.activation(sq_buf[:, :], xt[:, :], AF.Identity,
                                                     accum_out=stats_s1[:, i:i + 1]), inc=sS)
        sq_done = sS.n
        P.wait("v", sS, sq_done)
        P.op("v", lambda e, i=i: e.tensor_scalar(stats_m[:, i:i + 1], stats_s1[:, i:i + 1],
                                                 c_inv1024[:, :], None, ALU.mult))
        P.op("v", lambda e, i=i: e.tensor_scalar(s2c[:, :], stats_s2[:, i:i + 1],
                                                 c_inv1024[:, :], None, ALU.mult))
        P.op("v", lambda e, i=i: e.tensor_mul(m2c[:, :], stats_m[:, i:i + 1], stats_m[:, i:i + 1]))
        vgap()
        P.op("v", lambda e, i=i: e.tensor_sub(stats_var[:, i:i + 1], s2c[:, :], m2c[:, :]), inc=sV)
        var_done = sV.n
        P.wait("s", sV, var_done)
        P.op("s", lambda e, i=i: e.activation(stats_std[:, i:i + 1], stats_var[:, i:i + 1],
                                              AF.Sqrt, bias=c_eps[:, :]), inc=sS)
        std_done = sS.n
        P.wait("v", sS, std_done)
        P.op("v", lambda e, i=i: e.reciprocal(stats_r[:, i:i + 1], stats_std[:, i:i + 1]))
        vgap()
        xn = xn_ring[i % RING_XN]
        if i >= RING_XN:
            P.wait("v", sT, bmm_done[i - RING_XN])
        P.op("v", lambda e, xt=xt, xn=xn, i=i: e.tensor_scalar(
            xn[:, :], xt[:, :], stats_m[:, i:i + 1], stats_r[:, i:i + 1],
            ALU.subtract, ALU.mult), inc=sV)
        apply_done[i] = sV.n
        if i == 0 and "xn0" in dbg:
            P.wait("y", sV, apply_done[0])
            dma("y", sDBG, dbg_d["xn0"][:, :], xn_ring[0][:, :])

        P.wait("t", sV, apply_done[i])
        if pos == 0 and run > 0:
            P.wait("t", sV, ghalf_evac[(run - 1, 1)])
        for mt_loc in range(4):
            mt = mt_loc
            for nch in range(2):
                P.op("t", lambda e, xn=xn, mt=mt, nch=nch, pos=pos, mt_loc=mt_loc:
                     e.matmul(pg[mt_loc][:, nch * 512:(nch + 1) * 512],
                              xn[:, mt * 128:(mt + 1) * 128],
                              xn[:, nch * 512:(nch + 1) * 512],
                              start=(pos == 0), stop=(pos == RUN - 1)),
                     inc=(sT if (mt_loc == 3 and nch == 1 and pos == RUN - 1) else None))
        if pos == RUN - 1:
            amms_done = sT.n
            P.wait("v", sT, amms_done)
            for mt_loc in range(4):
                if run == 0:
                    P.op("v", lambda e, mt_loc=mt_loc: e.tensor_copy(
                        G_sb[:, mt_loc * D:(mt_loc + 1) * D], pg[mt_loc][:, :]),
                        inc=(sV if mt_loc == 3 else None))
                else:
                    P.op("v", lambda e, mt_loc=mt_loc: e.tensor_add(
                        G_sb[:, mt_loc * D:(mt_loc + 1) * D],
                        G_sb[:, mt_loc * D:(mt_loc + 1) * D], pg[mt_loc][:, :]),
                        inc=(sV if mt_loc == 3 else None))
            ghalf_evac[(run, 0)] = sV.n
            P.wait("t", sV, ghalf_evac[(run, 0)])
            for j in range(run * RUN, run * RUN + RUN):
                xnj = xn_ring[j % RING_XN]
                for mt_loc in range(4):
                    mt = 4 + mt_loc
                    for nch in range(2):
                        last = (j == run * RUN + RUN - 1 and mt_loc == 3 and nch == 1)
                        P.op("t", lambda e, xn=xnj, mt=mt, nch=nch, j=j, mt_loc=mt_loc, run=run:
                             e.matmul(pg[mt_loc][:, nch * 512:(nch + 1) * 512],
                                      xn[:, mt * 128:(mt + 1) * 128],
                                      xn[:, nch * 512:(nch + 1) * 512],
                                      start=(j == run * RUN), stop=(j == run * RUN + RUN - 1)),
                             inc=(sT if last else None))
            bmms_done = sT.n
            for j in range(run * RUN, run * RUN + RUN):
                bmm_done[j] = bmms_done
            P.wait("v", sT, bmms_done)
            for mt_loc in range(4):
                if run == 0:
                    P.op("v", lambda e, mt_loc=mt_loc: e.tensor_copy(
                        G_sb[:, (4 + mt_loc) * D:(5 + mt_loc) * D], pg[mt_loc][:, :]),
                        inc=(sV if mt_loc == 3 else None))
                else:
                    P.op("v", lambda e, mt_loc=mt_loc: e.tensor_add(
                        G_sb[:, (4 + mt_loc) * D:(5 + mt_loc) * D],
                        G_sb[:, (4 + mt_loc) * D:(5 + mt_loc) * D], pg[mt_loc][:, :]),
                        inc=(sV if mt_loc == 3 else None))
            ghalf_evac[(run, 1)] = sV.n

    g_done = sV.n

    P.op("v", lambda e: e.tensor_mul(stats_mr[:, :], stats_m[:, :], stats_r[:, :]))
    r_bf = sb1("r_bf", (128, NT), BF16)
    mr_bf = sb1("mr_bf", (128, NT), BF16)
    P.op("v", lambda e: e.tensor_copy(r_bf[:, :], stats_r[:, :]))
    P.op("v", lambda e: e.tensor_copy(mr_bf[:, :], stats_mr[:, :]), inc=sV)
    rows_src = sV.n
    pg_cms[3].__exit__(None, None, None)
    pg_bf = es1.enter_context(nc.psum_tensor("pg_bf", [128, 1024], BF16))
    P.wait("t", sV, rows_src)
    P.op("t", lambda e: e.transpose(pg_bf[0:NT, 0:128], r_bf[:, :], id_b[:, :]), inc=sT)
    P.op("t", lambda e: e.transpose(pg_bf[0:NT, 128:256], mr_bf[:, :], id_b[:, :]), inc=sT)
    rows_t = sT.n
    P.wait("v", sT, rows_t)
    P.op("v", lambda e: e.tensor_copy(r_row[:, :], pg_bf[0:NT, 0:128]))
    P.op("v", lambda e: e.tensor_copy(mr_row[:, :], pg_bf[0:NT, 128:256]), inc=sV)
    rows_done = sV.n

    P.wait("t", sINIT, init_wq)
    P.op("t", lambda e: e.matmul(pg[0][0:128, 0:8], onec_b[:, :], gb_row[:, :],
                                 start=True, stop=True), inc=sT)
    gbrep_t = sT.n
    P.wait("v", sT, gbrep_t)
    P.op("v", lambda e: e.tensor_copy(gb_rep[:, :], pg[0][0:128, 0:8]), inc=sV)
    gbrep_done = sV.n

    if "stats_m" in dbg:
        P.wait("y", sV, g_done)
        dma("y", sDBG, dbg_d["stats_m"][:, :], stats_m[:, :])
        dma("y", sDBG, dbg_d["stats_r"][:, :], stats_r[:, :])
    if "stats_s2" in dbg:
        P.wait("y", sV, g_done)
        P.wait("y", sS, sS.n)
        dma("y", sDBG, dbg_d["stats_s2"][:, :], stats_s2[:, :])
        dma("y", sDBG, dbg_d["stats_var"][:, :], stats_var[:, :])
        dma("y", sDBG, dbg_d["stats_std"][:, :], stats_std[:, :])
    if "G" in dbg:
        P.wait("y", sV, g_done)
        dma("y", sDBG, dbg_d["G"][:, :], G_sb[:, :])

    G_hi = sb1("G_hi", (128, FBn * D), BF16)
    G_lo = sb1("G_lo", (128, FBn * D), BF16)
    wq_sb = sb1("wq_sb", (128, FBn * D), BF16)
    wk_sb = sb1("wk_sb", (128, FBn * D), BF16)
    m1_hi = sb1("m1_hi", (128, FBn * D), BF16)
    m1_lo = sb1("m1_lo", (128, FBn * D), BF16)

    dma("g", sQK, wq_sb[:, :].rearrange("p (kb c) -> p kb c", kb=8),
        wq[:, :].rearrange("(kb p) c -> p kb c", p=128))
    dma("g", sQK, wk_sb[:, :].rearrange("p (kb c) -> p kb c", kb=8),
        wk[:, :].rearrange("(kb p) c -> p kb c", p=128))
    wqk_ld = sQK.n

    P.wait("s", sV, g_done)
    P.wait("v", sV, g_done)
    for fb in range(FBn):
        P.op("s", lambda e, fb=fb: e.activation(G_hi[:, fb * D:(fb + 1) * D],
                                                G_sb[:, fb * D:(fb + 1) * D], AF.Identity),
             inc=sS)
    ghi_done = sS.n
    P.wait("v", sS, ghi_done)
    for fb in range(FBn):
        P.op("v", lambda e, fb=fb: e.tensor_sub(G_lo[:, fb * D:(fb + 1) * D],
                                                G_sb[:, fb * D:(fb + 1) * D],
                                                G_hi[:, fb * D:(fb + 1) * D]),
             inc=(sV if fb == FBn - 1 else None))
    glo_done = sV.n

    m1_evac = {}
    P.wait("t", sV, glo_done)
    P.wait("t", sQK, wqk_ld)
    for it in range(FBn):
        pgm = pg[0] if it % 2 == 0 else pg[2]
        if it >= 2:
            P.wait("t", sV, m1_evac[it - 2])
        for nch in range(2):
            n_mm = 0
            for srcbuf in (G_hi, G_lo):
                for jt in range(FBn):
                    n_mm += 1
                    last = (n_mm == 16)
                    P.op("t", lambda e, srcbuf=srcbuf, jt=jt, it=it, nch=nch, pgm=pgm,
                         st_=(n_mm == 1), sp_=last:
                         e.matmul(pgm[:, nch * 512:(nch + 1) * 512],
                                  srcbuf[:, jt * D + it * 128: jt * D + (it + 1) * 128],
                                  wk_sb[:, jt * D + nch * 512: jt * D + (nch + 1) * 512],
                                  start=st_, stop=sp_),
                         inc=(sT if last else None))
        m1_t = sT.n
        P.wait("s", sT, m1_t)
        P.op("s", lambda e, it=it, pgm=pgm: e.activation(
            m1_hi[:, it * D:(it + 1) * D], pgm[:, :], AF.Identity), inc=sS)
        m1hi_s = sS.n
        P.wait("v", sS, m1hi_s)
        P.op("v", lambda e, it=it, pgm=pgm: e.tensor_sub(
            m1_lo[:, it * D:(it + 1) * D], pgm[:, :], m1_hi[:, it * D:(it + 1) * D]),
            inc=sV)
        m1_evac[it] = sV.n
    m1_done = sV.n

    pe = pg[0]
    P.wait("t", sV, m1_done)
    n_mm = 0
    total_mm = FBn * HD * 2
    for it in range(FBn):
        for d in range(HD):
            for srcbuf in (m1_hi, m1_lo):
                n_mm += 1
                P.op("t", lambda e, it=it, d=d, srcbuf=srcbuf,
                     st_=(n_mm == 1), sp_=(n_mm == total_mm):
                     e.matmul(pe[0:16, 0:16],
                              wq_sb[:, it * D + d * 16: it * D + (d + 1) * 16],
                              srcbuf[:, it * D + d * 16: it * D + (d + 1) * 16],
                              start=st_, stop=sp_),
                     inc=(sT if n_mm == total_mm else None))
    e_done_t = sT.n

    negmax = sb1("negmax", (16, 1)); nb = sb1("nb", (16, 1))
    exp_sb = sb1("exp_sb", (16, 16)); ssum = sb1("ssum", (16, 1)); rcs = sb1("rcs", (16, 1))
    att_b = sb1("att_b", (16, 16), BF16)
    if "E" in dbg:
        e_cp = sb1("e_cp", (16, 16))
        P.wait("v", sT, e_done_t)
        P.op("v", lambda e: e.tensor_copy(e_cp[:, :], pe[0:16, 0:16]), inc=sV)
        P.wait("y", sV, sV.n)
        dma("y", sDBG, dbg_d["E"][:, :], e_cp[:, :])
    P.wait("v", sT, e_done_t)
    P.op("v", lambda e: e.reduce_max(negmax[:, :], pe[0:16, 0:16], AX.X))
    vgap()
    P.op("v", lambda e: e.tensor_scalar(nb[:, :], negmax[:, :], c_m0125[0:16, :], None, ALU.mult),
         inc=sV)
    nb_done = sV.n
    P.wait("s", sV, nb_done)
    P.op("s", lambda e: e.activation(exp_sb[:, :], pe[0:16, 0:16], AF.Exp,
                                     bias=nb[:, :], scale=0.125), inc=sS)
    exp_done = sS.n
    P.wait("v", sS, exp_done)
    P.op("v", lambda e: e.reduce_sum(ssum[:, :], exp_sb[:, :], AX.X))
    vgap()
    P.op("v", lambda e: e.reciprocal(rcs[:, :], ssum[:, :]))
    vgap()
    P.op("v", lambda e: e.tensor_scalar(att_b[:, :], exp_sb[:, :], rcs[:, :], None, ALU.mult),
         inc=sV)
    att_done = sV.n
    P.wait("t", sV, att_done)
    P.op("t", lambda e: e.transpose(pg_bf[0:16, 128:144], att_b[:, :], id_b[0:16, 0:16]),
         inc=sT)
    attT_t = sT.n
    P.wait("v", sT, attT_t)
    P.op("v", lambda e: e.tensor_copy(attT[:, :], pg_bf[0:16, 128:144]), inc=sV)
    attT_done = sV.n

    if "att" in dbg:
        att_f = sb1("att_f", (16, 16))
        P.wait("v", sV, att_done)
        P.op("v", lambda e: e.tensor_copy(att_f[:, :], att_b[:, :]), inc=sV)
        P.wait("y", sV, sV.n)
        dma("y", sDBG, dbg_d["att"][:, :], att_f[:, :])


    P.wait("t", sV, rows_done)
    P.wait("t", sV, m1_done)
    P.wait("t", sINIT, init_wq)
    psel = pg[2]
    P.op("t", lambda e: e.matmul(psel[0:32, 0:128], pF_sb[:, :], r_row[:, :], start=True, stop=True))
    P.op("t", lambda e: e.matmul(psel[0:32, 128:256], pF_sb[:, :], mr_row[:, :], start=True, stop=True))
    P.op("t", lambda e: e.matmul(psel[0:32, 256:384], pV_sb[:, :], r_row[:, :], start=True, stop=True))
    P.op("t", lambda e: e.matmul(psel[0:32, 384:512], pV_sb[:, :], mr_row[:, :], start=True, stop=True),
         inc=sT)
    sel_t = sT.n
    P.wait("v", sT, sel_t)
    P.op("v", lambda e: e.tensor_copy(rF_row[:, :], psel[0:32, 0:128]))
    P.op("v", lambda e: e.tensor_copy(mrF_row[:, :], psel[0:32, 128:256]))
    P.op("v", lambda e: e.tensor_copy(rV_row[:, :], psel[0:32, 256:384]))
    P.op("v", lambda e: e.tensor_copy(mrV_row[:, :], psel[0:32, 384:512]), inc=sV)
    selrows_done = sV.n

    ph1 = {sT: sT.n, sV: sV.n, sS: sS.n}
    for eng in ENGINES:
        for s_, v_ in ph1.items():
            P.wait(eng, s_, v_)
    es1.close()
    for cm in (pg_cms[2], pg_cms[1], pg_cms[0]):
        cm.__exit__(None, None, None)

    es2 = contextlib.ExitStack()
    sb2 = lambda n, sh, d=F32: es2.enter_context(nc.sbuf_tensor(n, list(sh), d))
    w13 = [sb2(f"w13_{i}", (128, 2 * 8 * H), BF16) for i in range(2)]
    W3OFF = 8 * H
    w2b = sb2("w2b", (128, HBn * D), BF16)
    xng = sb2("xng", (128, FBn * D), BF16)
    yg = sb2("yg", (128, FBn * D))
    xf_ring = [sb2(f"xf{i}", (128, 512)) for i in range(4)]
    t1b = [sb2(f"t1b{i}", (128, 512)) for i in range(2)]
    t3b = [sb2(f"t3b{i}", (128, 512)) for i in range(2)]
    hbuf = sb2("hbuf", (128, HBn * 512), BF16)
    rep_b = [sb2(f"rep{i}", (128, 512), BF16) for i in range(2)]
    rw_fm = [sb2(f"rwfm{i}", (8, 512), BF16) for i in range(2)]
    lg_sb = sb2("lg_sb", (8, 512))
    lg_tm = sb2("lg_tm", (128, 32))
    ex_tm = sb2("ex_tm", (128, 32))
    sc_tm = sb2("sc_tm", (128, 32))
    bias_tm = sb2("bias_tm", (128, 32))
    mskd_tm = sb2("mskd_tm", (128, 32))
    rw_tm = sb2("rw_tm", (128, 32))
    mxc = sb2("mxc", (128, 1)); nmc = sb2("nmc", (128, 1))
    smc = sb2("smc", (128, 1)); rpc = sb2("rpc", (128, 1))
    m1c2 = sb2("m1c2", (128, 1)); m2c2 = sb2("m2c2", (128, 1))
    mskc = sb2("mskc", (128, 8)); mkc = sb2("mkc", (128, 8)); keepc = sb2("keepc", (128, 8))
    pa_cms = [nc.psum_tensor(f"pa{i}", [128, 512], F32) for i in range(4)]
    pa = [cm.__enter__() for cm in pa_cms]
    pw_cms = [nc.psum_tensor(f"pw{i}", [128, 512], F32) for i in range(2)]
    pw = [cm.__enter__() for cm in pw_cms]
    pm_cms = [nc.psum_tensor(f"pm{i}", [128, 512], F32) for i in range(2)]
    pm = [cm.__enter__() for cm in pm_cms]

    xf_free_v = [0] * 4
    xn_ready = {}
    rwfm_s = {}
    rep_s = [0, 0]
    gdma_y = 0
    gsilu_s = 0
    a_end_t = {}
    w2_end_t = {}
    w1_ld = {}; w3_ld = {}; w2_ld = {}
    pa_free = [("sS", 0)] * 4
    pw_free = [0, 0]
    pm0_free = ("sS", 0)
    pm1_free = ("sV", 0)
    hmul_last = 0
    t1_free = [0, 0]
    t3_free = [("sS", 0), ("sS", 0)]

    def wsrc(eng, tag_cnt):
        tag, cnt = tag_cnt
        P.wait(eng, {"sS": sS, "sV": sV, "sT": sT}[tag], cnt)

    for g in range(NGRP):
        for ch in range(NCH):
            c0 = g * 1024 + ch * 512
            wsrc("t", pm0_free); wsrc("t", pm1_free)
            P.wait("t", sV, selrows_done)
            for k4 in range(4):
                row = 8 * g + 4 * ch + k4
                P.op("t", lambda e, row=row, k4=k4: e.matmul(
                    pm[0][:, k4 * 128:(k4 + 1) * 128],
                    bass.AP(id_b, row, [[128, 32], [0, 128]]), rF_row[0:32, :],
                    start=True, stop=True))
                P.op("t", lambda e, row=row, k4=k4: e.matmul(
                    pm[1][:, k4 * 128:(k4 + 1) * 128],
                    bass.AP(id_b, row, [[128, 32], [0, 128]]), mrF_row[0:32, :],
                    start=True, stop=True),
                    inc=(sT if k4 == 3 else None))
            ab_t = sT.n
            if g > 0 and ch == 0:
                P.wait("v", sGS, gdma_y)
            for fb in range(FBn):
                slot = (ch * FBn + fb) % 4
                P.wait("y", sV, xf_free_v[slot])
                ld_t = dma("y", sX[slot], xf_ring[slot][:, :],
                           x_fm_F[fb * 128:(fb + 1) * 128, c0:c0 + 512])
                P.wait("v", sX[slot], ld_t)
                P.wait("v", sT, ab_t)
                xf = xf_ring[slot]
                P.op("v", lambda e, xf=xf: e.tensor_mul(xf[:, :], xf[:, :], pm[0][:, :]))
                P.op("v", lambda e, xf=xf, fb=fb, ch=ch: e.tensor_sub(
                    xng[:, fb * D + ch * 512: fb * D + ch * 512 + 512], xf[:, :], pm[1][:, :]),
                    inc=sV)
                xf_free_v[slot] = sV.n
            xn_ready[(g, ch)] = sV.n
            P.wait("t", sV, xn_ready[(g, ch)])
            for kb in range(FBn):
                P.op("t", lambda e, kb=kb, ch=ch: e.matmul(
                    pm[0][0:8, 0:512], gwt_sb[:, kb * 8:(kb + 1) * 8],
                    xng[:, kb * D + ch * 512: kb * D + ch * 512 + 512],
                    start=(kb == 0), stop=(kb == FBn - 1)),
                    inc=(sT if kb == FBn - 1 else None))
            gate_t = sT.n
            P.wait("s", sT, gate_t)
            P.op("s", lambda e: e.activation(lg_sb[:, :], pm[0][0:8, 0:512], AF.Identity), inc=sS)
            lg_s = sS.n
            if g == 0 and ch == 0 and "lg0" in dbg:
                P.wait("y", sS, lg_s)
                dma("y", sDBG, dbg_d["lg0"][:, :], lg_sb[:, :])
            pm0_free = ("sS", lg_s)
            P.wait("t", sS, lg_s)
            P.wait("t", sV, xn_ready[(g, ch)])
            for k4 in range(4):
                P.op("t", lambda e, k4=k4: e.transpose(
                    pm[1][0:128, k4 * 8:(k4 + 1) * 8], lg_sb[0:8, k4 * 128:(k4 + 1) * 128],
                    id_f[0:8, 0:8]),
                    inc=(sT if k4 == 3 else None))
            lgT_t = sT.n
            P.wait("v", sT, lgT_t)
            P.op("v", lambda e: e.tensor_copy(lg_tm[:, :], pm[1][:, 0:32]), inc=sV)
            if g == 0 and ch == 0 and "lgtm0" in dbg:
                P.wait("y", sV, sV.n)
                dma("y", sDBG, dbg_d["lgtm0"][:, :], lg_tm[:, :])
            for b4 in range(4):
                sl = slice(b4 * 8, (b4 + 1) * 8)
                P.op("v", lambda e, sl=sl: e.reduce_max(mxc[:, :], lg_tm[:, sl], AX.X))
                vgap()
                P.op("v", lambda e: e.tensor_scalar(nmc[:, :], mxc[:, :], c_m1[:, :], None, ALU.mult),
                     inc=sV)
                nm_v = sV.n
                P.wait("s", sV, nm_v)
                P.op("s", lambda e, sl=sl: e.activation(ex_tm[:, sl], lg_tm[:, sl], AF.Exp,
                                                        bias=nmc[:, :]), inc=sS)
                ex_s = sS.n
                P.wait("v", sS, ex_s)
                P.op("v", lambda e, sl=sl: e.reduce_sum(smc[:, :], ex_tm[:, sl], AX.X))
                vgap()
                P.op("v", lambda e: e.reciprocal(rpc[:, :], smc[:, :]))
                vgap()
                P.op("v", lambda e, sl=sl: e.tensor_scalar(sc_tm[:, sl], ex_tm[:, sl],
                                                           rpc[:, :], None, ALU.mult))
                vgap()
                P.op("v", lambda e, sl=sl: e.tensor_add(bias_tm[:, sl], sc_tm[:, sl], gb_rep[:, :]))
                vgap()
                P.op("v", lambda e, sl=sl: e.reduce_max(m1c2[:, :], bias_tm[:, sl], AX.X))
                vgap()
                P.op("v", lambda e, sl=sl: e.tensor_scalar(mskc[:, :], bias_tm[:, sl],
                                                           m1c2[:, :], None, ALU.is_ge))
                vgap()
                P.op("v", lambda e: e.tensor_scalar(mkc[:, :], mskc[:, :], c_big[:, :], None, ALU.mult))
                vgap()
                P.op("v", lambda e, sl=sl: e.tensor_sub(mskd_tm[:, sl], bias_tm[:, sl], mkc[:, :]))
                vgap()
                P.op("v", lambda e, sl=sl: e.reduce_max(m2c2[:, :], mskd_tm[:, sl], AX.X))
                vgap()
                P.op("v", lambda e, sl=sl: e.tensor_scalar(keepc[:, :], bias_tm[:, sl],
                                                           m2c2[:, :], None, ALU.is_ge))
                vgap()
                P.op("v", lambda e, sl=sl: e.tensor_mul(rw_tm[:, sl], sc_tm[:, sl], keepc[:, :]),
                     inc=(sV if b4 == 3 else None))
            rw_v = sV.n
            if g == 0 and ch == 0 and "sctm0" in dbg:
                P.wait("y", sV, rw_v)
                dma("y", sDBG, dbg_d["sctm0"][:, :], sc_tm[:, :])
                dma("y", sDBG, dbg_d["gbrep"][:, :], gb_rep[:, :])
                dma("y", sDBG, dbg_d["biastm0"][:, :], bias_tm[:, :])
                dma("y", sDBG, dbg_d["mskdtm0"][:, :], mskd_tm[:, :])
                dma("y", sDBG, dbg_d["rwtm0"][:, :], rw_tm[:, :])
            P.wait("t", sV, rw_v)
            for k4 in range(4):
                P.op("t", lambda e, k4=k4: e.transpose(
                    pm[1][0:8, k4 * 128:(k4 + 1) * 128], rw_tm[:, k4 * 8:(k4 + 1) * 8],
                    id_f[:, :]),
                    inc=(sT if k4 == 3 else None))
            rwT_t = sT.n
            P.wait("s", sT, rwT_t)
            P.op("s", lambda e, ch=ch: e.activation(rw_fm[ch][:, :], pm[1][0:8, 0:512], AF.Identity),
                 inc=sS)
            rwfm_s[(g, ch)] = sS.n
            pm1_free = ("sS", sS.n)
            if g == 0 and ch == 0 and "rw0" in dbg:
                P.wait("y", sS, rwfm_s[(g, ch)])
                dma("y", sDBG, dbg_d["rw0"][:, :], rw_fm[0][:, :])

        elist = list(range(nexp)) if experts is None else experts
        for eidx in elist:
            L = g * NEXP + eidx
            shared = (eidx == 8)
            if L >= 2:
                P.wait("g", sT, a_end_t.get(L - 2, max(list(a_end_t.values()) or [0])))
            srcs = (w1d[eidx], w3d[eidx]) if not shared else (sw1[:, :], sw3[:, :])
            for wi, srcw in enumerate(srcs):
                ap = srcw[:, :] if not shared else srcw
                ap = ap.rearrange("(kb p) h -> p kb h", p=128)
                dstbuf = w13[L % 2]
                off = 0 if wi == 0 else W3OFF
                P.op("g", lambda e, ap=ap, dstbuf=dstbuf, off=off: e.dma_start(
                    out=dstbuf[:, off:off + 8 * H].rearrange("p (kb h) -> p kb h", kb=8),
                    in_=ap), inc=sW13[L % 2], inc_by=16)
            w3_ld[L] = sW13[L % 2].n
            if L >= 1:
                P.wait("g", sT, w2_end_t.get(L - 1, max(list(w2_end_t.values()) or [0])))
            srcw2 = w2d[eidx][:, :] if not shared else sw2[:, :]
            ap2 = srcw2.rearrange("(kb p) dd -> p kb dd", p=128)
            P.op("g", lambda e, ap2=ap2: e.dma_start(
                out=w2b[:, :].rearrange("p (kb dd) -> p kb dd", kb=HBn), in_=ap2),
                inc=sW2, inc_by=16)
            w2_ld[L] = sW2.n
            if not shared:
                for ch in range(NCH):
                    P.wait("t", sS, rwfm_s[(g, ch)])
                    wsrc("t", pm0_free)
                    P.op("t", lambda e, ch=ch, eidx=eidx: e.matmul(
                        pm[0][:, :], bass.AP(id_b, eidx, [[128, 8], [0, 128]]),
                        rw_fm[ch][0:8, :], start=True, stop=True), inc=sT)
                    rept = sT.n
                    P.wait("s", sT, rept)
                    P.wait("s", sV, hmul_last)
                    P.op("s", lambda e, ch=ch: e.activation(rep_b[ch][:, :], pm[0][:, :],
                                                            AF.Identity), inc=sS)
                    rep_s[ch] = sS.n
                    pm0_free = ("sS", sS.n)
            for ch in range(NCH):
                if g == 0 and eidx == tap_e and ch == 0 and "w13dump" in dbg:
                    P.wait("y", sW13[L % 2], w3_ld[L])
                    dma("y", sDBG, dbg_d["w13dump"][:, :], w13[L % 2][:, :])
                    P.wait("t", sDBG, sDBG.n)
                for hb in range(HBn):
                    p2 = hb % 2
                    if serial >= 1:
                        join_all()
                    P.wait("t", sW13[L % 2], w3_ld[L])
                    P.wait("t", sV, xn_ready[(g, ch)])
                    wsrc("t", pa_free[2 * p2])
                    wsrc("t", pa_free[2 * p2 + 1])
                    for kb in range(FBn):
                        P.op("t", lambda e, kb=kb, hb=hb, ch=ch, p2=p2, L=L: e.matmul(
                            pa[2 * p2][:, :], w13[L % 2][:, kb * H + hb * 128: kb * H + (hb + 1) * 128],
                            xng[:, kb * D + ch * 512: kb * D + ch * 512 + 512],
                            start=(kb == 0), stop=(kb == FBn - 1)),
                            inc=(sT if kb == FBn - 1 else None))
                    a1_t = sT.n
                    for kb in range(FBn):
                        P.op("t", lambda e, kb=kb, hb=hb, ch=ch, p2=p2, L=L: e.matmul(
                            pa[2 * p2 + 1][:, :],
                            w13[L % 2][:, W3OFF + kb * H + hb * 128: W3OFF + kb * H + (hb + 1) * 128],
                            xng[:, kb * D + ch * 512: kb * D + ch * 512 + 512],
                            start=(kb == 0), stop=(kb == FBn - 1)),
                            inc=(sT if kb == FBn - 1 else None))
                    a3_t = sT.n
                    bcol = (sb1_sb if shared else b1_sb)[:, (hb if shared else eidx * HBn + hb):
                                                         (hb if shared else eidx * HBn + hb) + 1]
                    bcol3 = (sb3_sb if shared else b3_sb)[:, (hb if shared else eidx * HBn + hb):
                                                          (hb if shared else eidx * HBn + hb) + 1]
                    P.wait("s", sT, a1_t)
                    P.wait("s", sV, t1_free[p2])
                    P.op("s", lambda e, p2=p2, bcol=bcol: e.activation(
                        t1b[p2][:, :], pa[2 * p2][:, :], AF.Identity, bias=bcol), inc=sS)
                    t1_s = sS.n
                    pa_free[2 * p2] = ("sS", t1_s)
                    P.wait("v", sS, t1_s)
                    P.wait("v", sT, a3_t)
                    wsrc("v", t3_free[p2])
                    P.op("v", lambda e, p2=p2, bcol3=bcol3: e.scalar_tensor_tensor(
                        t3b[p2][:, :], pa[2 * p2 + 1][:, :], bcol3, t1b[p2][:, :],
                        ALU.add, ALU.mult), inc=sV)
                    prod_v = sV.n
                    if g == 0 and eidx == tap_e and ch == 0 and hb == tap_hb and "t100" in dbg:
                        P.wait("y", sV, prod_v)
                        dma("y", sDBG, dbg_d["t100"][:, :], t1b[0][:, :])
                        dma("y", sDBG, dbg_d["t300"][:, :], t3b[0][:, :])
                        P.wait("s", sDBG, sDBG.n)
                    t1_free[p2] = prod_v
                    pa_free[2 * p2 + 1] = ("sV", prod_v)
                    P.wait("s", sV, prod_v)
                    if shared:
                        P.op("s", lambda e, p2=p2, hb=hb: e.activation(
                            hbuf[:, hb * 512:(hb + 1) * 512], t3b[p2][:, :], AF.Silu), inc=sS)
                        t3_free[p2] = ("sS", sS.n)
                    else:
                        P.op("s", lambda e, p2=p2: e.activation(t3b[p2][:, :], t3b[p2][:, :],
                                                                AF.Silu), inc=sS)
                        silu_s = sS.n
                        P.wait("v", sS, silu_s)
                        P.wait("v", sS, rep_s[ch])
                        P.op("v", lambda e, p2=p2, hb=hb, ch=ch: e.tensor_mul(
                            hbuf[:, hb * 512:(hb + 1) * 512], t3b[p2][:, :], rep_b[ch][:, :]),
                            inc=sV)
                        t3_free[p2] = ("sV", sV.n)
                        hmul_last = sV.n
                if g == 0 and eidx == tap_e and ch == 0 and "h00" in dbg:
                    P.wait("y", sV, hmul_last)
                    P.wait("y", sS, sS.n)
                    dma("y", sDBG, dbg_d["h00"][:, :], hbuf[:, :])
                    if "rep0" in dbg:
                        dma("y", sDBG, dbg_d["rep0"][:, :], rep_b[0][:, :])
                    P.wait("v", sDBG, sDBG.n)
                    P.wait("s", sDBG, sDBG.n)
                if ch == NCH - 1:
                    a_end_t[L] = sT.n
                P.wait("t", sW2, w2_ld[L])
                if shared:
                    P.wait("t", sS, sS.n)
                else:
                    P.wait("t", sV, hmul_last)
                for db in range(FBn):
                    pwb = pw[db % 2]
                    if serial >= 1:
                        join_all()
                    P.wait("t", sV, pw_free[db % 2])
                    first = True
                    if eidx == 0:
                        P.op("t", lambda e, db=db, ch=ch: e.matmul(
                            pw[db % 2][:, :], b2_sb[:, db * 128:(db + 1) * 128],
                            rw_fm[ch][:, :], start=True, stop=False))
                        first = False
                    if shared:
                        P.op("t", lambda e, db=db: e.matmul(
                            pw[db % 2][:, :], sb2_sb[:, db * 128:(db + 1) * 128],
                            ones_b[:, :], start=True, stop=False))
                        first = False
                    for kb in range(HBn):
                        P.op("t", lambda e, kb=kb, db=db, first=first: e.matmul(
                            pw[db % 2][:, :], w2b[:, kb * D + db * 128: kb * D + (db + 1) * 128],
                            hbuf[:, kb * 512:(kb + 1) * 512],
                            start=(first and kb == 0), stop=(kb == HBn - 1)),
                            inc=(sT if kb == HBn - 1 else None))
                    w2db_t = sT.n
                    P.wait("v", sT, w2db_t)
                    if eidx == elist[0]:
                        if g > 0 and db == 0 and ch == 0:
                            P.wait("v", sS, gsilu_s)
                        P.op("v", lambda e, db=db, ch=ch: e.tensor_copy(
                            yg[:, db * D + ch * 512: db * D + ch * 512 + 512], pw[db % 2][:, :]),
                            inc=sV)
                    else:
                        P.op("v", lambda e, db=db, ch=ch: e.tensor_add(
                            yg[:, db * D + ch * 512: db * D + ch * 512 + 512],
                            yg[:, db * D + ch * 512: db * D + ch * 512 + 512], pw[db % 2][:, :]),
                            inc=sV)
                    pw_free[db % 2] = sV.n
                if g == 0 and eidx == tap_e and ch == NCH - 1 and "ye0" in dbg:
                    P.wait("y", sV, sV.n)
                    dma("y", sDBG, dbg_d["ye0"][:, :], yg[:, :])
                    P.wait("t", sDBG, sDBG.n)
                if ch == NCH - 1:
                    w2_end_t[L] = sT.n
        if g == 0 and "y0" in dbg:
            P.wait("y", sV, sV.n)
            dma("y", sDBG, dbg_d["y0"][:, :], yg[:, :])
            P.wait("s", sDBG, sDBG.n)
        P.wait("s", sV, sV.n)
        P.wait("s", sT, w2_end_t[g * NEXP + elist[-1]])
        for fb in range(FBn):
            P.op("s", lambda e, fb=fb: e.activation(
                xng[:, fb * D:(fb + 1) * D], yg[:, fb * D:(fb + 1) * D], AF.Silu),
                inc=(sS if fb == FBn - 1 else None))
        gsilu_s = sS.n
        P.wait("y", sS, gsilu_s)
        P.op("y", lambda e, g=g: e.dma_start(
            out=g_scr[:, :].rearrange("(fb p) c -> p fb c", p=128)[:, :, g * 1024:(g + 1) * 1024],
            in_=xng[:, :].rearrange("p (fb c) -> p fb c", fb=FBn)),
            inc=sGS, inc_by=16)
        gdma_y = sGS.n
        if g == 0 and "g_grp0" in dbg:
            P.wait("y", sS, gsilu_s)
            dma("y", sDBG, dbg_d["g_grp0"][:, :], xng[:, :])

    ffn_done = {"sT": sT.n, "sV": sV.n, "sS": sS.n}


    for eng in ENGINES:
        for s_, v_ in ffn_done.items():
            P.wait(eng, {"sT": sT, "sV": sV, "sS": sS}[s_], v_)
    es2.close()
    for cm in (pm_cms[1], pm_cms[0], pw_cms[1], pw_cms[0],
               pa_cms[3], pa_cms[2], pa_cms[1], pa_cms[0]):
        cm.__exit__(None, None, None)

    es3 = contextlib.ExitStack()
    sb3 = lambda n, sh, d=F32: es3.enter_context(nc.sbuf_tensor(n, list(sh), d))
    wv_sb = sb3("wv_sb", (128, FBn * D), BF16)
    wo_sb = sb3("wo_sb", (128, 16 * D), BF16)
    v_jq = sb3("v_jq", (128, FBn * 4096), BF16)
    xnv = [sb3(f"xnv{i}", (128, FBn * 512), BF16) for i in range(2)]
    xv_ring = [sb3(f"xv{i}", (128, 512)) for i in range(2)]
    out_jq = [sb3(f"ojq{i}", (128, 4096), BF16) for i in range(2)]
    bd_sb = [sb3(f"bd{i}", (128, FBn * 128), BF16) for i in range(2)]
    s_tile = [sb3(f"stl{i}", (128, FBn * 256), BF16) for i in range(2)]
    g_tile = [sb3(f"gtl{i}", (128, FBn * 256), BF16) for i in range(2)]
    xr_tile = [sb3(f"xrt{i}", (128, FBn * 256)) for i in range(2)]
    pscr = sb3("pscr", (128, FBn * 256))
    pv_cms = [nc.psum_tensor(f"pv{i}", [128, 512], F32) for i in range(2)]
    pv = [cm.__enter__() for cm in pv_cms]
    pav_cms = [nc.psum_tensor(f"pav{i}", [128, 512], F32) for i in range(2)]
    pav = [cm.__enter__() for cm in pav_cms]
    ps_cms = [nc.psum_tensor(f"ps{i}", [128, 512], F32) for i in range(4)]
    ps_ = [cm.__enter__() for cm in ps_cms]

    dma("g", sW13[0], wv_sb[:, :].rearrange("p (kb c) -> p kb c", kb=FBn),
        wv[:, :].rearrange("(kb p) c -> p kb c", p=128))
    wv_ld = sW13[0].n
    dma("g", sW13[1], wo_sb[:, :], wo[:, :])
    wo_ld = sW13[1].n

    xnv_free_t = [0, 0]
    xv_free_v = [0] * 4
    vp_free = [0, 0]
    vch_evac = {}
    comb_v_free = {}
    stores_prev = {}
    for c in range(8):
        par = c % 2
        P.wait("t", sV, selrows_done)
        if c >= 1:
            P.wait("t", sV, vch_evac[c - 1])
        for k4 in range(4):
            row = 4 * c + k4
            P.op("t", lambda e, row=row, k4=k4: e.matmul(
                ps_[0][:, k4 * 128:(k4 + 1) * 128],
                bass.AP(id_b, row, [[128, 32], [0, 128]]), rV_row[0:32, :],
                start=True, stop=True))
            P.op("t", lambda e, row=row, k4=k4: e.matmul(
                ps_[1][:, k4 * 128:(k4 + 1) * 128],
                bass.AP(id_b, row, [[128, 32], [0, 128]]), mrV_row[0:32, :],
                start=True, stop=True),
                inc=(sT if k4 == 3 else None))
        ab_t = sT.n
        P.wait("v", sT, ab_t)
        if c >= 2:
            P.wait("v", sT, xnv_free_t[par])
        for fb in range(FBn):
            slot = (c * FBn + fb) % 2
            P.wait("y", sV, xv_free_v[slot])
            ld_t = dma("y", sX[slot], xv_ring[slot][:, :],
                       x_fm_V[fb * 128:(fb + 1) * 128, c * 512:(c + 1) * 512])
            P.wait("v", sX[slot], ld_t)
            xv = xv_ring[slot]
            P.op("v", lambda e, xv=xv: e.tensor_mul(xv[:, :], xv[:, :], ps_[0][:, :]))
            P.op("v", lambda e, xv=xv, fb=fb, par=par: e.tensor_sub(
                xnv[par][:, fb * 512:(fb + 1) * 512], xv[:, :], ps_[1][:, :]), inc=sV)
            xv_free_v[slot] = sV.n
        xn_v = sV.n
        P.wait("t", sV, xn_v)
        P.wait("t", sW13[0], wv_ld)
        for mb in range(FBn):
            pb = pv[mb % 2]
            if c * FBn + mb >= 2:
                P.wait("t", sV, vp_free[mb % 2])
            for kb in range(FBn):
                P.op("t", lambda e, kb=kb, mb=mb, par=par, pb=pb: e.matmul(
                    pb[:, :], wv_sb[:, kb * D + mb * 128: kb * D + (mb + 1) * 128],
                    xnv[par][:, kb * 512:(kb + 1) * 512],
                    start=(kb == 0), stop=(kb == FBn - 1)),
                    inc=(sT if kb == FBn - 1 else None))
            mm_t = sT.n
            P.wait("v", sT, mm_t)
            dst = bass.AP(v_jq, mb * 4096 + c * 32, [[FBn * 4096, 128], [256, 16], [1, 32]])
            srcp = bass.AP(pb, 0, [[512, 128], [1, 16], [16, 32]])
            P.op("v", lambda e, dst=dst, srcp=srcp: e.tensor_copy(dst, srcp), inc=sV)
            vp_free[mb % 2] = sV.n
        vch_evac[c] = sV.n
        xnv_free_t[par] = sT.n
    v_done = sV.n
    if "v_jq" in dbg:
        P.wait("y", sV, v_done)
        dma("y", sDBG, dbg_d["v_jq"][:, :], v_jq[:, :])

    bd_free_t = [0, 0]
    ojq_free_t = [0, 0]
    st_done = {}
    comb_free = {}
    gx_ld = {}
    pav_free_v = 0
    for tp in range(8):
        par_t = tp % 2
        for par in range(2):
            a = 2 * tp + par
            gx_ld[(tp, par, "g")] = dma(
                "y", sGR[par], g_tile[par][:, :].rearrange("p (fb c) -> p fb c", fb=FBn),
                g_scr[:, :].rearrange("(fb p) c -> p fb c", p=128)[:, :, a * 256:(a + 1) * 256])
            gx_ld[(tp, par, "x")] = dma(
                "y", sXR[par], xr_tile[par][:, :].rearrange("p (fb c) -> p fb c", fb=FBn),
                x_fm_F[:, :].rearrange("(fb p) c -> p fb c", p=128)[:, :, a * 256:(a + 1) * 256])
        P.wait("t", sV, attT_done)
        P.wait("t", sT, bd_free_t[par_t])
        P.wait("t", sV, pav_free_v)
        P.wait("t", sINIT, init_wq)
        rhs = bass.AP(attT, 2 * tp, [[16, 16], [1, 2], [0, 64]])
        for t in range(8):
            P.op("t", lambda e, t=t, rhs=rhs: e.matmul(
                pav[t // 4][:, (t % 4) * 128:(t % 4 + 1) * 128],
                selbk_sb[:, t * 128:(t + 1) * 128], rhs,
                start=True, stop=True),
                inc=(sT if t % 4 == 3 else None))
        rep_t = sT.n
        P.wait("v", sT, rep_t)
        P.op("v", lambda e, par_t=par_t: e.tensor_mul(
            bd_sb[par_t][:, 0:512], pav[0][:, :], p0d_b[:, :]))
        P.op("v", lambda e, par_t=par_t: e.tensor_mul(
            bd_sb[par_t][:, 512:1024], pav[1][:, :], p0d_b[:, :]), inc=sV)
        bd_v = sV.n
        P.wait("t", sV, bd_v)
        P.wait("t", sV, v_done)
        P.wait("t", sT, ojq_free_t[par_t])
        for cc in range(8):
            pb = pav[cc % 2]
            for t in range(8):
                P.op("t", lambda e, t=t, cc=cc, pb=pb, par_t=par_t: e.matmul(
                    pb[:, :], bd_sb[par_t][:, t * 128:(t + 1) * 128],
                    v_jq[:, t * 4096 + cc * 512: t * 4096 + (cc + 1) * 512],
                    start=(t == 0), stop=(t == 7)),
                    inc=(sT if t == 7 else None))
            avt = sT.n
            P.wait("v", sT, avt)
            P.op("v", lambda e, cc=cc, pb=pb, par_t=par_t: e.tensor_copy(
                out_jq[par_t][:, cc * 512:(cc + 1) * 512], pb[:, :]), inc=sV)
        ojq_v = sV.n
        pav_free_v = ojq_v
        if tp == 0 and "attv" in dbg:
            P.wait("y", sV, ojq_v)
            dma("y", sDBG, dbg_d["attv"][:, :], out_jq[0][:, :])
        P.wait("t", sV, ojq_v)
        P.wait("t", sW13[1], wo_ld)
        for half in range(2):
            for par in range(2):
                a_loc = par
                for ob in range(half * 4, half * 4 + 4):
                    P.wait("t", sS, comb_free.get((tp * 2 + par - 4, ob), 0))
                    for j in range(16):
                        P.op("t", lambda e, j=j, ob=ob, par=par, par_t=par_t: e.matmul(
                            ps_[ob % 4][:, par * 256:(par + 1) * 256],
                            wo_sb[par * 64:(par + 1) * 64, j * D + ob * 128:(j * D + (ob + 1) * 128)],
                            out_jq[par_t][par * 64:(par + 1) * 64, j * 256:(j + 1) * 256],
                            start=(j == 0), stop=(j == 15),
                            tile_position=(par * 64, 0)),
                            inc=(sT if j == 15 else None))
                    st_done[(par, ob)] = sT.n
            for par in range(2):
                for ob in range(half * 4, half * 4 + 4):
                    P.wait("s", sT, st_done[(par, ob)])
                    P.wait("s", sV, comb_v_free.get(par, 0))
                    P.op("s", lambda e, ob=ob, par=par: e.activation(
                        s_tile[par][:, ob * 256:(ob + 1) * 256],
                        ps_[ob % 4][:, par * 256:(par + 1) * 256],
                        AF.Identity, bias=bo_sb[:, ob:ob + 1]),
                        inc=sS)
                    comb_free[(tp * 2 + par, ob)] = sS.n
        s_done = sS.n
        for par in range(2):
            a = 2 * tp + par
            gld = gx_ld[(tp, par, "g")]
            xld = gx_ld[(tp, par, "x")]
            P.wait("v", sGR[par], gld)
            P.wait("v", sXR[par], xld)
            P.wait("v", sS, s_done)
            P.wait("v", sOUT[par], stores_prev.get(par, 0))
            for ob in range(FBn):
                P.op("v", lambda e, ob=ob, par=par: e.tensor_mul(
                    pscr[:, ob * 256:(ob + 1) * 256],
                    s_tile[par][:, ob * 256:(ob + 1) * 256],
                    g_tile[par][:, ob * 256:(ob + 1) * 256]))
            for ob in range(FBn):
                P.op("v", lambda e, ob=ob, par=par: e.tensor_add(
                    xr_tile[par][:, ob * 256:(ob + 1) * 256],
                    xr_tile[par][:, ob * 256:(ob + 1) * 256],
                    pscr[:, ob * 256:(ob + 1) * 256]),
                    inc=(sV if ob == FBn - 1 else None))
            comb_v = sV.n
            comb_v_free[par] = comb_v
            P.wait("y", sV, comb_v)
            st = dma("y", sOUT[par],
                     out_d[:, :].rearrange("(fb p) c -> p fb c", p=128)[:, :, a * 256:(a + 1) * 256],
                     xr_tile[par][:, :].rearrange("p (fb c) -> p fb c", fb=FBn))
            stores_prev[par] = st
            if a == 0 and "s_head0" in dbg:
                P.wait("y", sS, s_done)
                dma("y", sDBG, dbg_d["s_head0"][:, :], s_tile[0][:, :])
        bd_free_t[par_t] = sT.n
        ojq_free_t[par_t] = sT.n

    P.wait("y", sDBG, sDBG.n)
    P.wait("y", sGS, sGS.n)
    for s_ in sOUT:
        P.wait("y", s_, s_.n)
    P.wait("y", sV, attT_done)
    P.wait("y", sV, rows_done)
    P.wait("y", sV, gbrep_done)
    P.validate()
    with nc.Block() as block:
        P.replay(block)
    es3.close()
    es0.close()
    return nc


BF_ML = ml_dtypes.bfloat16


def make_consts():
    idn = np.eye(128, dtype=np.float32)
    onesr = np.ones((1, 512), np.float32)
    onec = np.ones((1, 128), np.float32)
    sel2 = np.zeros((2, 128), np.float32)
    sel2[0, 0:64] = 1.0
    sel2[1, 64:128] = 1.0
    p0d = np.zeros((128, 128), np.float32)
    for p in range(128):
        for f in range(p % 64, 128, 64):
            p0d[p, f] = 1.0
    p0d = np.tile(p0d, (1, 4))
    return idn, onesr, onec, sel2, p0d


def host_inputs(inp, core):
    n, h = core // 2, core % 2
    x = np.ascontiguousarray(np.asarray(inp["L0_attn_x"], np.float32)[n])
    F_rows = np.array([a * 512 + h * 256 + q for a in range(16) for q in range(256)])
    perm_da = np.array([(i % 16) * 64 + i // 16 for i in range(D)])
    Wq = np.asarray(inp["L0_attn_Wq"], np.float32)
    Wk = np.asarray(inp["L0_attn_Wk"], np.float32)
    Wo = np.asarray(inp["L0_attn_Wo"], np.float32)
    wo_dup = np.empty((128, 16 * D), np.float32)
    for j in range(16):
        blk = Wo[64 * j:64 * (j + 1), :]
        wo_dup[0:64, j * D:(j + 1) * D] = blk
        wo_dup[64:128, j * D:(j + 1) * D] = blk
    bo = np.asarray(inp["L0_attn_bo"], np.float32)
    bo_p = np.ascontiguousarray(bo.reshape(8, 128).T)
    gate_w = np.asarray(inp["L0_ffn_gate_w"], np.float32)
    b1 = np.asarray(inp["L0_ffn_b1"], np.float32)
    b3 = np.asarray(inp["L0_ffn_b3"], np.float32)
    b1p = np.ascontiguousarray(b1.reshape(8, 11, 128).transpose(2, 0, 1).reshape(128, 88))
    b3p = np.ascontiguousarray(b3.reshape(8, 11, 128).transpose(2, 0, 1).reshape(128, 88))
    sb1_ = np.asarray(inp["L0_ffn_sb1"], np.float32).reshape(11, 128).T
    sb3_ = np.asarray(inp["L0_ffn_sb3"], np.float32).reshape(11, 128).T
    idn, onesr, onec, sel2, p0d = make_consts()
    pFT = np.zeros((64, 32), np.float32)
    for k in range(32):
        pFT[4 * (k // 2) + 2 * h + (k % 2), k] = 1.0
    pVT = np.zeros((64, 32), np.float32)
    for k in range(32):
        pVT[32 * h + k, k] = 1.0
    selbk = np.zeros((16, 8 * 128), np.float32)
    for t in range(8):
        for p in range(128):
            selbk[2 * t + p // 64, t * 128 + p] = 1.0
    d = {
        "x_tm": x,
        "x_fm_V": np.ascontiguousarray(x[h * 4096:(h + 1) * 4096].T),
        "x_fm_F": np.ascontiguousarray(x[F_rows].T),
        "wq": np.ascontiguousarray(Wq[:, perm_da]),
        "wk": np.ascontiguousarray(Wk[:, perm_da]),
        "wv": np.asarray(inp["L0_attn_Wv"], np.float32),
        "wo": wo_dup,
        "bo_p": np.ascontiguousarray(bo_p),
        "gwt": np.ascontiguousarray(gate_w.T),
        "gbr": np.asarray(inp["L0_ffn_gate_b"], np.float32).reshape(1, 8),
        "w1d": np.asarray(inp["L0_ffn_w1"], np.float32),
        "w3d": np.asarray(inp["L0_ffn_w3"], np.float32),
        "w2d": np.asarray(inp["L0_ffn_w2"], np.float32),
        "b1p": b1p, "b3p": b3p,
        "b2a": np.asarray(inp["L0_ffn_b2"], np.float32),
        "sw1": np.asarray(inp["L0_ffn_sw1"], np.float32),
        "sw3": np.asarray(inp["L0_ffn_sw3"], np.float32),
        "sw2": np.asarray(inp["L0_ffn_sw2"], np.float32),
        "sb1p": np.ascontiguousarray(sb1_), "sb3p": np.ascontiguousarray(sb3_),
        "sb2r": np.asarray(inp["L0_ffn_sb2"], np.float32).reshape(1, D),
        "idn": idn, "onesr": onesr, "onec": onec, "sel2": sel2, "p0d": p0d,
        "pFT": pFT, "pVT": pVT, "selbk": selbk,
    }
    return d, F_rows


def kernel(**inputs):
    nc = build_nc()
    in_maps = []
    frows = []
    for c in range(8):
        m, fr = host_inputs(inputs, c)
        in_maps.append(m)
        frows.append(fr)
    res = run_bass_kernel_spmd(nc, in_maps, core_ids=list(range(8)))
    out = np.empty((4, S, D), np.float32)
    for c in range(8):
        n = c // 2
        out[n, frows[c], :] = res.results[c]["out"].T
    return out


# version 07
# speedup vs baseline: 1.2686x; 1.0111x over previous
import contextlib
import numpy as np
import ml_dtypes
import concourse.bass as bass
import concourse.mybir as mybir
from concourse.bass_utils import run_bass_kernel_spmd

F32 = mybir.dt.float32
BF16 = mybir.dt.bfloat16
AF = mybir.ActivationFunctionType
ALU = mybir.AluOpType
AX = mybir.AxisListType

D = 1024
S = 8192
HEADS, HD = 16, 64
H = 1408
FBn = 8
HBn = 11
NT = 64
RING_X = 4
RING_XN = 20
RUN = 16
NGRP, NCH, CHT = 4, 2, 512
NEXP = 9

ENGINES = ("t", "v", "s", "g", "y")


class Sem:
    def __init__(self, handle, name):
        self.h = handle
        self.name = name
        self.n = 0


class Plan:

    def __init__(self):
        self.q = {e: [] for e in ENGINES}

    def wait(self, eng, sem, val=None):
        v = sem.n if val is None else val
        if v > 0:
            self.q[eng].append(("w", sem, v))

    def op(self, eng, fn, inc=None, inc_by=1):
        if inc is not None:
            inc.n += inc_by
        self.q[eng].append(("o", fn, inc, inc_by))

    def validate(self):
        counts = {}
        idx = {e: 0 for e in ENGINES}
        progress = True
        while progress:
            progress = False
            for e in ENGINES:
                while idx[e] < len(self.q[e]):
                    item = self.q[e][idx[e]]
                    if item[0] == "w":
                        if counts.get(item[1].name, 0) < item[2]:
                            break
                    else:
                        if item[2] is not None:
                            counts[item[2].name] = counts.get(item[2].name, 0) + item[3]
                    idx[e] += 1
                    progress = True
        stuck = {e: (idx[e], len(self.q[e])) for e in ENGINES if idx[e] < len(self.q[e])}
        if stuck:
            detail = {}
            for e in stuck:
                item = self.q[e][idx[e]]
                detail[e] = (item[1].name, item[2], counts.get(item[1].name, 0))
            raise RuntimeError(f"plan deadlock: {stuck} waiting on {detail}")

    def replay(self, block):
        deco = {"t": block.tensor, "v": block.vector, "s": block.scalar,
                "g": block.gpsimd, "y": block.sync}
        for e in ENGINES:
            items = self.q[e]
            if not items:
                continue

            def run(engobj, items=items):
                for item in items:
                    if item[0] == "w":
                        engobj.wait_ge(item[1].h, item[2])
                    else:
                        _, fn, inc, by = item
                        ins = fn(engobj)
                        if inc is not None:
                            ins.then_inc(inc.h, by)
            deco[e](run)


def build_nc(dbg=(), NT=NT, RUN=RUN, RING_XN=RING_XN, RING_X=RING_X, serial=0, nexp=NEXP, tap_e=0, tap_hb=0, experts=None):
    nc = bass.Bass("TRN2", target_bir_lowering=False, detect_race_conditions=False)
    dbg = set(dbg)

    din = {}
    def dt_in(name, shape):
        din[name] = nc.dram_tensor(name, list(shape), F32, kind="ExternalInput")
        return din[name]

    x_tm = dt_in("x_tm", (NT * 128, D))
    x_fm_V = dt_in("x_fm_V", (D, 4096))
    x_fm_F = dt_in("x_fm_F", (D, 4096))
    wq = dt_in("wq", (D, D)); wk = dt_in("wk", (D, D)); wv = dt_in("wv", (D, D))
    wo = dt_in("wo", (128, 16 * D))
    bo_p = dt_in("bo_p", (128, 8))
    gwt = dt_in("gwt", (D, 8)); gbr = dt_in("gbr", (1, 8))
    w1d = dt_in("w1d", (8, D, H)); w3d = dt_in("w3d", (8, D, H)); w2d = dt_in("w2d", (8, H, D))
    b1p = dt_in("b1p", (128, 88)); b3p = dt_in("b3p", (128, 88)); b2a = dt_in("b2a", (8, D))
    sw1 = dt_in("sw1", (D, H)); sw3 = dt_in("sw3", (D, H)); sw2 = dt_in("sw2", (H, D))
    sb1p = dt_in("sb1p", (128, 11)); sb3p = dt_in("sb3p", (128, 11)); sb2r = dt_in("sb2r", (1, D))
    idn = dt_in("idn", (128, 128))
    onesr = dt_in("onesr", (1, 512))
    onec = dt_in("onec", (1, 128))
    sel2 = dt_in("sel2", (2, 128))
    pFT = dt_in("pFT", (NT, 32)); pVT = dt_in("pVT", (NT, 32))
    selbk = dt_in("selbk", (16, 8 * 128))
    p0d = dt_in("p0d", (128, 512))

    out_d = nc.dram_tensor("out", [D, 4096], F32, kind="ExternalOutput")
    g_scr = nc.dram_tensor("g_scr", [D, 4096], BF16, kind="Internal")

    dbg_d = {}
    def dbg_out(name, shape, dtype=F32):
        if name in dbg:
            dbg_d[name] = nc.dram_tensor("dbg_" + name, list(shape), dtype, kind="ExternalOutput")

    dbg_out("stats_m", (128, NT)); dbg_out("stats_r", (128, NT))
    dbg_out("stats_s2", (128, NT)); dbg_out("stats_var", (128, NT)); dbg_out("stats_std", (128, NT))
    dbg_out("G", (128, FBn * D)); dbg_out("E", (16, 16)); dbg_out("att", (16, 16))
    dbg_out("xn0", (128, D), BF16)
    dbg_out("v_jq", (128, FBn * 4096), BF16)
    dbg_out("attv", (128, 4096), BF16)
    dbg_out("g_grp0", (128, FBn * 1024), BF16)
    dbg_out("rw0", (8, 512), BF16)
    dbg_out("lg0", (8, 512)); dbg_out("lgtm0", (128, 32)); dbg_out("sctm0", (128, 32))
    dbg_out("gbrep", (128, 8)); dbg_out("biastm0", (128, 32)); dbg_out("mskdtm0", (128, 32))
    dbg_out("h00", (128, 11 * 512), BF16); dbg_out("y0", (128, FBn * D))
    dbg_out("ye0", (128, FBn * D)); dbg_out("rep0", (128, 512), BF16)
    dbg_out("w13dump", (128, 2 * 8 * H), BF16)
    dbg_out("t100", (128, 512)); dbg_out("t300", (128, 512))
    dbg_out("rwtm0", (128, 32))
    dbg_out("s_head0", (128, 8 * 256), BF16)

    es0 = contextlib.ExitStack()
    sem = lambda n: Sem(es0.enter_context(nc.semaphore(n)), n)
    sINIT = sem("sINIT")
    sQK = sem("sQK")
    sX = [sem(f"sX{i}") for i in range(4)]
    sW13 = [sem(f"sW13{i}") for i in range(2)]
    sW2 = sem("sW2")
    sGS = sem("sGS")
    sDBG = sem("sDBG")
    sOUT = [sem(f"sOUT{i}") for i in range(2)]
    sGR = [sem(f"sGR{i}") for i in range(2)]
    sXR = [sem(f"sXR{i}") for i in range(2)]
    sT = sem("sT"); sV = sem("sV"); sS = sem("sS"); sG = sem("sG")

    sb0 = lambda n, sh, d=F32: es0.enter_context(nc.sbuf_tensor(n, list(sh), d))
    stats_m = sb0("stats_m", (128, NT))
    stats_r = sb0("stats_r", (128, NT))
    stats_mr = sb0("stats_mr", (128, NT))
    r_row = sb0("r_row", (NT, 128), BF16)
    mr_row = sb0("mr_row", (NT, 128), BF16)
    attT = sb0("attT", (16, 16), BF16)
    rF_row = sb0("rF_row", (32, 128), BF16)
    mrF_row = sb0("mrF_row", (32, 128), BF16)
    rV_row = sb0("rV_row", (32, 128), BF16)
    mrV_row = sb0("mrV_row", (32, 128), BF16)
    pF_sb = sb0("pF_sb", (NT, 32), BF16)
    pV_sb = sb0("pV_sb", (NT, 32), BF16)
    selbk_sb = sb0("selbk_sb", (16, 8 * 128), BF16)
    id_b = sb0("id_b", (128, 128), BF16)
    id_f = sb0("id_f", (128, 128))
    ones_b = sb0("ones_b", (1, 512), BF16)
    onec_b = sb0("onec_b", (1, 128), BF16)
    sel2_b = sb0("sel2_b", (2, 128), BF16)
    p0d_b = sb0("p0d_b", (128, 512), BF16)
    bo_sb = sb0("bo_sb", (128, 8))
    gwt_sb = sb0("gwt_sb", (128, 64), BF16)
    gb_rep = sb0("gb_rep", (128, 8))
    b1_sb = sb0("b1_sb", (128, 88)); b3_sb = sb0("b3_sb", (128, 88))
    sb1_sb = sb0("sb1_sb", (128, 11)); sb3_sb = sb0("sb3_sb", (128, 11))
    b2_sb = sb0("b2_sb", (8, D), BF16)
    sb2_sb = sb0("sb2_sb", (1, D), BF16)
    c_inv1024 = sb0("c_inv1024", (128, 1))
    c_eps = sb0("c_eps", (128, 1))
    c_0125 = sb0("c_0125", (128, 1))
    c_big = sb0("c_big", (128, 1))
    c_m0125 = sb0("c_m0125", (128, 1))
    junk = sb0("junk", (128, 1))
    c_m1 = sb0("c_m1", (128, 1))

    P = Plan()

    def vgap():
        P.op("v", lambda e: e.tensor_scalar(junk[:, :], c_eps[:, :], 1.0, None, ALU.mult))

    def join_all():
        for eng in ("t", "v", "s"):
            P.wait(eng, sT, sT.n)
            P.wait(eng, sV, sV.n)
            P.wait(eng, sS, sS.n)

    def dma(eng_letter, sem_, dst, src):
        P.op(eng_letter, lambda e, d=dst, s=src: e.dma_start(out=d, in_=s), inc=sem_, inc_by=16)
        return sem_.n

    dma("g", sINIT, id_b[:, :], idn[:, :])
    dma("g", sINIT, id_f[:, :], idn[:, :])
    dma("g", sINIT, ones_b[:, :], onesr[:, :])
    dma("g", sINIT, onec_b[:, :], onec[:, :])
    dma("g", sINIT, sel2_b[:, :], sel2[:, :])
    dma("g", sINIT, p0d_b[:, :], p0d[:, :])
    dma("g", sINIT, bo_sb[:, :], bo_p[:, :])
    dma("g", sINIT, gwt_sb[:, :].rearrange("p (kb e) -> p kb e", kb=8),
        gwt[:, :].rearrange("(kb p) e -> p kb e", p=128))
    dma("g", sINIT, b1_sb[:, :], b1p[:, :])
    dma("g", sINIT, b3_sb[:, :], b3p[:, :])
    dma("g", sINIT, sb1_sb[:, :], sb1p[:, :])
    dma("g", sINIT, sb3_sb[:, :], sb3p[:, :])
    dma("g", sINIT, b2_sb[:, :], b2a[:, :])
    dma("g", sINIT, sb2_sb[:, :], sb2r[:, :])
    init_wq = sINIT.n

    P.op("v", lambda e: e.memset(c_inv1024[:, :], 1.0 / 1024.0))
    P.op("v", lambda e: e.memset(c_eps[:, :], 1e-5))
    P.op("v", lambda e: e.memset(c_0125[:, :], 0.125))
    P.op("v", lambda e: e.memset(c_big[:, :], 1e9))
    P.op("v", lambda e: e.memset(c_m0125[:, :], -0.125))
    P.op("v", lambda e: e.memset(c_m1[:, :], -1.0), inc=sV)
    init_v = sV.n

    gb_row = sb0("gb_row", (1, 8), BF16)
    dma("g", sINIT, gb_row[:, :], gbr[:, :])
    dma("g", sINIT, pF_sb[:, :], pFT[:, :])
    dma("g", sINIT, pV_sb[:, :], pVT[:, :])
    dma("g", sINIT, selbk_sb[:, :], selbk[:, :])
    init_wq = sINIT.n

    es1 = contextlib.ExitStack()
    sb1 = lambda n, sh, d=F32: es1.enter_context(nc.sbuf_tensor(n, list(sh), d))
    xt_ring = [sb1(f"xt{i}", (128, D)) for i in range(RING_X)]
    stats_s2 = sb1("stats_s2", (128, NT))
    stats_s1 = sb1("stats_s1", (128, NT))
    stats_var = sb1("stats_var", (128, NT))
    stats_std = sb1("stats_std", (128, NT))
    sq_buf = sb1("sq", (128, D))
    xn_ring = [sb1(f"xn{i}", (128, D), BF16) for i in range(RING_XN)]
    G_sb = sb1("G_sb", (128, FBn * D))
    s1c = sb1("s1c", (128, 1)); mc = sb1("mc", (128, 1))
    s2c = sb1("s2c", (128, 1)); m2c = sb1("m2c", (128, 1))
    varc = sb1("varc", (128, 1)); stdc = sb1("stdc", (128, 1))
    pg_cms = [nc.psum_tensor(f"pg{i}", [128, D], F32) for i in range(4)]
    pg = [cm.__enter__() for cm in pg_cms]

    P.op("v", lambda e: e.memset(stats_s2[:, :], 0.0))
    P.op("v", lambda e: e.memset(stats_s1[:, :], 0.0), inc=sV)
    s2_zeroed = sV.n

    apply_done = [0] * NT
    bmm_done = [0] * NT
    ghalf_evac = {}

    for i in range(NT):
        run, pos = divmod(i, RUN)
        if i >= RING_X:
            P.wait("y", sV, apply_done[i - RING_X])
        sXs = sX[i % RING_X]
        ld_tgt = dma("y", sXs, xt_ring[i % RING_X][:, :], x_tm[128 * i:128 * (i + 1), :])
        xt = xt_ring[i % RING_X]

        P.wait("s", sXs, ld_tgt)
        P.wait("s", sV, init_v)
        P.wait("s", sV, s2_zeroed)
        P.op("s", lambda e, xt=xt, i=i: e.activation(sq_buf[:, :], xt[:, :], AF.Square,
                                                     accum_out=stats_s2[:, i:i + 1]))
        P.op("s", lambda e, xt=xt, i=i: e.activation(sq_buf[:, :], xt[:, :], AF.Identity,
                                                     accum_out=stats_s1[:, i:i + 1]), inc=sS)
        sq_done = sS.n
        P.wait("v", sS, sq_done)
        P.op("v", lambda e, i=i: e.tensor_scalar(stats_m[:, i:i + 1], stats_s1[:, i:i + 1],
                                                 c_inv1024[:, :], None, ALU.mult))
        P.op("v", lambda e, i=i: e.tensor_scalar(s2c[:, :], stats_s2[:, i:i + 1],
                                                 c_inv1024[:, :], None, ALU.mult))
        P.op("v", lambda e, i=i: e.tensor_mul(m2c[:, :], stats_m[:, i:i + 1], stats_m[:, i:i + 1]))
        vgap()
        P.op("v", lambda e, i=i: e.tensor_sub(stats_var[:, i:i + 1], s2c[:, :], m2c[:, :]), inc=sV)
        var_done = sV.n
        P.wait("s", sV, var_done)
        P.op("s", lambda e, i=i: e.activation(stats_std[:, i:i + 1], stats_var[:, i:i + 1],
                                              AF.Sqrt, bias=c_eps[:, :]), inc=sS)
        std_done = sS.n
        P.wait("v", sS, std_done)
        P.op("v", lambda e, i=i: e.reciprocal(stats_r[:, i:i + 1], stats_std[:, i:i + 1]))
        vgap()
        xn = xn_ring[i % RING_XN]
        if i >= RING_XN:
            P.wait("v", sT, bmm_done[i - RING_XN])
        P.op("v", lambda e, xt=xt, xn=xn, i=i: e.tensor_scalar(
            xn[:, :], xt[:, :], stats_m[:, i:i + 1], stats_r[:, i:i + 1],
            ALU.subtract, ALU.mult), inc=sV)
        apply_done[i] = sV.n
        if i == 0 and "xn0" in dbg:
            P.wait("y", sV, apply_done[0])
            dma("y", sDBG, dbg_d["xn0"][:, :], xn_ring[0][:, :])

        P.wait("t", sV, apply_done[i])
        if pos == 0 and run > 0:
            P.wait("t", sV, ghalf_evac[(run - 1, 1)])
        for mt_loc in range(4):
            mt = mt_loc
            for nch in range(2):
                P.op("t", lambda e, xn=xn, mt=mt, nch=nch, pos=pos, mt_loc=mt_loc:
                     e.matmul(pg[mt_loc][:, nch * 512:(nch + 1) * 512],
                              xn[:, mt * 128:(mt + 1) * 128],
                              xn[:, nch * 512:(nch + 1) * 512],
                              start=(pos == 0), stop=(pos == RUN - 1)),
                     inc=(sT if (mt_loc == 3 and nch == 1 and pos == RUN - 1) else None))
        if pos == RUN - 1:
            amms_done = sT.n
            P.wait("v", sT, amms_done)
            for mt_loc in range(4):
                if run == 0:
                    P.op("v", lambda e, mt_loc=mt_loc: e.tensor_copy(
                        G_sb[:, mt_loc * D:(mt_loc + 1) * D], pg[mt_loc][:, :]),
                        inc=(sV if mt_loc == 3 else None))
                else:
                    P.op("v", lambda e, mt_loc=mt_loc: e.tensor_add(
                        G_sb[:, mt_loc * D:(mt_loc + 1) * D],
                        G_sb[:, mt_loc * D:(mt_loc + 1) * D], pg[mt_loc][:, :]),
                        inc=(sV if mt_loc == 3 else None))
            ghalf_evac[(run, 0)] = sV.n
            P.wait("t", sV, ghalf_evac[(run, 0)])
            for j in range(run * RUN, run * RUN + RUN):
                xnj = xn_ring[j % RING_XN]
                for mt_loc in range(4):
                    mt = 4 + mt_loc
                    for nch in range(2):
                        last = (j == run * RUN + RUN - 1 and mt_loc == 3 and nch == 1)
                        P.op("t", lambda e, xn=xnj, mt=mt, nch=nch, j=j, mt_loc=mt_loc, run=run:
                             e.matmul(pg[mt_loc][:, nch * 512:(nch + 1) * 512],
                                      xn[:, mt * 128:(mt + 1) * 128],
                                      xn[:, nch * 512:(nch + 1) * 512],
                                      start=(j == run * RUN), stop=(j == run * RUN + RUN - 1)),
                             inc=(sT if last else None))
            bmms_done = sT.n
            for j in range(run * RUN, run * RUN + RUN):
                bmm_done[j] = bmms_done
            P.wait("v", sT, bmms_done)
            for mt_loc in range(4):
                if run == 0:
                    P.op("v", lambda e, mt_loc=mt_loc: e.tensor_copy(
                        G_sb[:, (4 + mt_loc) * D:(5 + mt_loc) * D], pg[mt_loc][:, :]),
                        inc=(sV if mt_loc == 3 else None))
                else:
                    P.op("v", lambda e, mt_loc=mt_loc: e.tensor_add(
                        G_sb[:, (4 + mt_loc) * D:(5 + mt_loc) * D],
                        G_sb[:, (4 + mt_loc) * D:(5 + mt_loc) * D], pg[mt_loc][:, :]),
                        inc=(sV if mt_loc == 3 else None))
            ghalf_evac[(run, 1)] = sV.n

    g_done = sV.n

    P.op("v", lambda e: e.tensor_mul(stats_mr[:, :], stats_m[:, :], stats_r[:, :]))
    r_bf = sb1("r_bf", (128, NT), BF16)
    mr_bf = sb1("mr_bf", (128, NT), BF16)
    P.op("v", lambda e: e.tensor_copy(r_bf[:, :], stats_r[:, :]))
    P.op("v", lambda e: e.tensor_copy(mr_bf[:, :], stats_mr[:, :]), inc=sV)
    rows_src = sV.n
    pg_cms[3].__exit__(None, None, None)
    pg_bf = es1.enter_context(nc.psum_tensor("pg_bf", [128, 1024], BF16))
    P.wait("t", sV, rows_src)
    P.op("t", lambda e: e.transpose(pg_bf[0:NT, 0:128], r_bf[:, :], id_b[:, :]), inc=sT)
    P.op("t", lambda e: e.transpose(pg_bf[0:NT, 128:256], mr_bf[:, :], id_b[:, :]), inc=sT)
    rows_t = sT.n
    P.wait("v", sT, rows_t)
    P.op("v", lambda e: e.tensor_copy(r_row[:, :], pg_bf[0:NT, 0:128]))
    P.op("v", lambda e: e.tensor_copy(mr_row[:, :], pg_bf[0:NT, 128:256]), inc=sV)
    rows_done = sV.n

    P.wait("t", sINIT, init_wq)
    P.op("t", lambda e: e.matmul(pg[0][0:128, 0:8], onec_b[:, :], gb_row[:, :],
                                 start=True, stop=True), inc=sT)
    gbrep_t = sT.n
    P.wait("v", sT, gbrep_t)
    P.op("v", lambda e: e.tensor_copy(gb_rep[:, :], pg[0][0:128, 0:8]), inc=sV)
    gbrep_done = sV.n

    if "stats_m" in dbg:
        P.wait("y", sV, g_done)
        dma("y", sDBG, dbg_d["stats_m"][:, :], stats_m[:, :])
        dma("y", sDBG, dbg_d["stats_r"][:, :], stats_r[:, :])
    if "stats_s2" in dbg:
        P.wait("y", sV, g_done)
        P.wait("y", sS, sS.n)
        dma("y", sDBG, dbg_d["stats_s2"][:, :], stats_s2[:, :])
        dma("y", sDBG, dbg_d["stats_var"][:, :], stats_var[:, :])
        dma("y", sDBG, dbg_d["stats_std"][:, :], stats_std[:, :])
    if "G" in dbg:
        P.wait("y", sV, g_done)
        dma("y", sDBG, dbg_d["G"][:, :], G_sb[:, :])

    G_hi = sb1("G_hi", (128, FBn * D), BF16)
    G_lo = sb1("G_lo", (128, FBn * D), BF16)
    wq_sb = sb1("wq_sb", (128, FBn * D), BF16)
    wk_sb = sb1("wk_sb", (128, FBn * D), BF16)
    m1_hi = sb1("m1_hi", (128, FBn * D), BF16)
    m1_lo = sb1("m1_lo", (128, FBn * D), BF16)

    dma("g", sQK, wq_sb[:, :].rearrange("p (kb c) -> p kb c", kb=8),
        wq[:, :].rearrange("(kb p) c -> p kb c", p=128))
    dma("g", sQK, wk_sb[:, :].rearrange("p (kb c) -> p kb c", kb=8),
        wk[:, :].rearrange("(kb p) c -> p kb c", p=128))
    wqk_ld = sQK.n

    P.wait("s", sV, g_done)
    P.wait("v", sV, g_done)
    for fb in range(FBn):
        P.op("s", lambda e, fb=fb: e.activation(G_hi[:, fb * D:(fb + 1) * D],
                                                G_sb[:, fb * D:(fb + 1) * D], AF.Identity),
             inc=sS)
    ghi_done = sS.n
    P.wait("v", sS, ghi_done)
    for fb in range(FBn):
        P.op("v", lambda e, fb=fb: e.tensor_sub(G_lo[:, fb * D:(fb + 1) * D],
                                                G_sb[:, fb * D:(fb + 1) * D],
                                                G_hi[:, fb * D:(fb + 1) * D]),
             inc=(sV if fb == FBn - 1 else None))
    glo_done = sV.n

    m1_evac = {}
    P.wait("t", sV, glo_done)
    P.wait("t", sQK, wqk_ld)
    for it in range(FBn):
        pgm = pg[0] if it % 2 == 0 else pg[2]
        if it >= 2:
            P.wait("t", sV, m1_evac[it - 2])
        for nch in range(2):
            n_mm = 0
            for srcbuf in (G_hi, G_lo):
                for jt in range(FBn):
                    n_mm += 1
                    last = (n_mm == 16)
                    P.op("t", lambda e, srcbuf=srcbuf, jt=jt, it=it, nch=nch, pgm=pgm,
                         st_=(n_mm == 1), sp_=last:
                         e.matmul(pgm[:, nch * 512:(nch + 1) * 512],
                                  srcbuf[:, jt * D + it * 128: jt * D + (it + 1) * 128],
                                  wk_sb[:, jt * D + nch * 512: jt * D + (nch + 1) * 512],
                                  start=st_, stop=sp_),
                         inc=(sT if last else None))
        m1_t = sT.n
        P.wait("s", sT, m1_t)
        P.op("s", lambda e, it=it, pgm=pgm: e.activation(
            m1_hi[:, it * D:(it + 1) * D], pgm[:, :], AF.Identity), inc=sS)
        m1hi_s = sS.n
        P.wait("v", sS, m1hi_s)
        P.op("v", lambda e, it=it, pgm=pgm: e.tensor_sub(
            m1_lo[:, it * D:(it + 1) * D], pgm[:, :], m1_hi[:, it * D:(it + 1) * D]),
            inc=sV)
        m1_evac[it] = sV.n
    m1_done = sV.n

    pe = pg[0]
    P.wait("t", sV, m1_done)
    n_mm = 0
    total_mm = FBn * HD * 2
    for it in range(FBn):
        for d in range(HD):
            for srcbuf in (m1_hi, m1_lo):
                n_mm += 1
                P.op("t", lambda e, it=it, d=d, srcbuf=srcbuf,
                     st_=(n_mm == 1), sp_=(n_mm == total_mm):
                     e.matmul(pe[0:16, 0:16],
                              wq_sb[:, it * D + d * 16: it * D + (d + 1) * 16],
                              srcbuf[:, it * D + d * 16: it * D + (d + 1) * 16],
                              start=st_, stop=sp_),
                     inc=(sT if n_mm == total_mm else None))
    e_done_t = sT.n

    negmax = sb1("negmax", (16, 1)); nb = sb1("nb", (16, 1))
    exp_sb = sb1("exp_sb", (16, 16)); ssum = sb1("ssum", (16, 1)); rcs = sb1("rcs", (16, 1))
    att_b = sb1("att_b", (16, 16), BF16)
    if "E" in dbg:
        e_cp = sb1("e_cp", (16, 16))
        P.wait("v", sT, e_done_t)
        P.op("v", lambda e: e.tensor_copy(e_cp[:, :], pe[0:16, 0:16]), inc=sV)
        P.wait("y", sV, sV.n)
        dma("y", sDBG, dbg_d["E"][:, :], e_cp[:, :])
    P.wait("v", sT, e_done_t)
    P.op("v", lambda e: e.reduce_max(negmax[:, :], pe[0:16, 0:16], AX.X))
    vgap()
    P.op("v", lambda e: e.tensor_scalar(nb[:, :], negmax[:, :], c_m0125[0:16, :], None, ALU.mult),
         inc=sV)
    nb_done = sV.n
    P.wait("s", sV, nb_done)
    P.op("s", lambda e: e.activation(exp_sb[:, :], pe[0:16, 0:16], AF.Exp,
                                     bias=nb[:, :], scale=0.125), inc=sS)
    exp_done = sS.n
    P.wait("v", sS, exp_done)
    P.op("v", lambda e: e.reduce_sum(ssum[:, :], exp_sb[:, :], AX.X))
    vgap()
    P.op("v", lambda e: e.reciprocal(rcs[:, :], ssum[:, :]))
    vgap()
    P.op("v", lambda e: e.tensor_scalar(att_b[:, :], exp_sb[:, :], rcs[:, :], None, ALU.mult),
         inc=sV)
    att_done = sV.n
    P.wait("t", sV, att_done)
    P.op("t", lambda e: e.transpose(pg_bf[0:16, 128:144], att_b[:, :], id_b[0:16, 0:16]),
         inc=sT)
    attT_t = sT.n
    P.wait("v", sT, attT_t)
    P.op("v", lambda e: e.tensor_copy(attT[:, :], pg_bf[0:16, 128:144]), inc=sV)
    attT_done = sV.n

    if "att" in dbg:
        att_f = sb1("att_f", (16, 16))
        P.wait("v", sV, att_done)
        P.op("v", lambda e: e.tensor_copy(att_f[:, :], att_b[:, :]), inc=sV)
        P.wait("y", sV, sV.n)
        dma("y", sDBG, dbg_d["att"][:, :], att_f[:, :])


    P.wait("t", sV, rows_done)
    P.wait("t", sV, m1_done)
    P.wait("t", sINIT, init_wq)
    psel = pg[2]
    P.op("t", lambda e: e.matmul(psel[0:32, 0:128], pF_sb[:, :], r_row[:, :], start=True, stop=True))
    P.op("t", lambda e: e.matmul(psel[0:32, 128:256], pF_sb[:, :], mr_row[:, :], start=True, stop=True))
    P.op("t", lambda e: e.matmul(psel[0:32, 256:384], pV_sb[:, :], r_row[:, :], start=True, stop=True))
    P.op("t", lambda e: e.matmul(psel[0:32, 384:512], pV_sb[:, :], mr_row[:, :], start=True, stop=True),
         inc=sT)
    sel_t = sT.n
    P.wait("v", sT, sel_t)
    P.op("v", lambda e: e.tensor_copy(rF_row[:, :], psel[0:32, 0:128]))
    P.op("v", lambda e: e.tensor_copy(mrF_row[:, :], psel[0:32, 128:256]))
    P.op("v", lambda e: e.tensor_copy(rV_row[:, :], psel[0:32, 256:384]))
    P.op("v", lambda e: e.tensor_copy(mrV_row[:, :], psel[0:32, 384:512]), inc=sV)
    selrows_done = sV.n

    ph1 = {sT: sT.n, sV: sV.n, sS: sS.n}
    for eng in ENGINES:
        for s_, v_ in ph1.items():
            P.wait(eng, s_, v_)
    es1.close()
    for cm in (pg_cms[2], pg_cms[1], pg_cms[0]):
        cm.__exit__(None, None, None)

    es2 = contextlib.ExitStack()
    sb2 = lambda n, sh, d=F32: es2.enter_context(nc.sbuf_tensor(n, list(sh), d))
    w13 = [sb2(f"w13_{i}", (128, 2 * 8 * H), BF16) for i in range(2)]
    W3OFF = 8 * H
    w2b = sb2("w2b", (128, HBn * D), BF16)
    xng = sb2("xng", (128, FBn * D), BF16)
    yg = sb2("yg", (128, FBn * D))
    xf_ring = [sb2(f"xf{i}", (128, 512)) for i in range(4)]
    t1b = [sb2(f"t1b{i}", (128, 512)) for i in range(2)]
    t3b = [sb2(f"t3b{i}", (128, 512)) for i in range(2)]
    hbuf = sb2("hbuf", (128, HBn * 512), BF16)
    rep_b = [sb2(f"rep{i}", (128, 512), BF16) for i in range(2)]
    rw_fm = [sb2(f"rwfm{i}", (8, 512), BF16) for i in range(2)]
    lg_sb = sb2("lg_sb", (8, 512))
    lg_tm = sb2("lg_tm", (128, 32))
    ex_tm = sb2("ex_tm", (128, 32))
    sc_tm = sb2("sc_tm", (128, 32))
    bias_tm = sb2("bias_tm", (128, 32))
    mskd_tm = sb2("mskd_tm", (128, 32))
    rw_tm = sb2("rw_tm", (128, 32))
    mxc = sb2("mxc", (128, 1)); nmc = sb2("nmc", (128, 1))
    smc = sb2("smc", (128, 1)); rpc = sb2("rpc", (128, 1))
    m1c2 = sb2("m1c2", (128, 1)); m2c2 = sb2("m2c2", (128, 1))
    mskc = sb2("mskc", (128, 8)); mkc = sb2("mkc", (128, 8)); keepc = sb2("keepc", (128, 8))
    pa_cms = [nc.psum_tensor(f"pa{i}", [128, 512], F32) for i in range(4)]
    pa = [cm.__enter__() for cm in pa_cms]
    pw_cms = [nc.psum_tensor(f"pw{i}", [128, 512], F32) for i in range(2)]
    pw = [cm.__enter__() for cm in pw_cms]
    pm_cms = [nc.psum_tensor(f"pm{i}", [128, 512], F32) for i in range(2)]
    pm = [cm.__enter__() for cm in pm_cms]

    xf_free_v = [0] * 4
    xn_ready = {}
    rwfm_s = {}
    rep_s = [0, 0]
    gdma_y = 0
    xfl_pref = {}
    gsilu_s = 0
    a_end_t = {}
    w2_end_t = {}
    w1_ld = {}; w3_ld = {}; w2_ld = {}
    pa_free = [("sS", 0)] * 4
    pw_free = [0, 0]
    pm0_free = ("sS", 0)
    pm1_free = ("sV", 0)
    hmul_last = 0
    t1_free = [0, 0]
    t3_free = [("sS", 0), ("sS", 0)]

    def wsrc(eng, tag_cnt):
        tag, cnt = tag_cnt
        P.wait(eng, {"sS": sS, "sV": sV, "sT": sT}[tag], cnt)

    for g in range(NGRP):
        for ch in range(NCH):
            c0 = g * 1024 + ch * 512
            wsrc("t", pm0_free); wsrc("t", pm1_free)
            P.wait("t", sV, selrows_done)
            for k4 in range(4):
                row = 8 * g + 4 * ch + k4
                P.op("t", lambda e, row=row, k4=k4: e.matmul(
                    pm[0][:, k4 * 128:(k4 + 1) * 128],
                    bass.AP(id_b, row, [[128, 32], [0, 128]]), rF_row[0:32, :],
                    start=True, stop=True))
                P.op("t", lambda e, row=row, k4=k4: e.matmul(
                    pm[1][:, k4 * 128:(k4 + 1) * 128],
                    bass.AP(id_b, row, [[128, 32], [0, 128]]), mrF_row[0:32, :],
                    start=True, stop=True),
                    inc=(sT if k4 == 3 else None))
            ab_t = sT.n
            if g > 0 and ch == 0:
                P.wait("v", sGS, gdma_y)
            for fb in range(FBn):
                slot = (ch * FBn + fb) % 4
                if (g, ch, fb) in xfl_pref:
                    ld_t = xfl_pref[(g, ch, fb)]
                else:
                    P.wait("y", sV, xf_free_v[slot])
                    ld_t = dma("y", sX[slot], xf_ring[slot][:, :],
                               x_fm_F[fb * 128:(fb + 1) * 128, c0:c0 + 512])
                P.wait("v", sX[slot], ld_t)
                P.wait("v", sT, ab_t)
                xf = xf_ring[slot]
                P.op("v", lambda e, xf=xf: e.tensor_mul(xf[:, :], xf[:, :], pm[0][:, :]))
                P.op("v", lambda e, xf=xf, fb=fb, ch=ch: e.tensor_sub(
                    xng[:, fb * D + ch * 512: fb * D + ch * 512 + 512], xf[:, :], pm[1][:, :]),
                    inc=sV)
                xf_free_v[slot] = sV.n
            xn_ready[(g, ch)] = sV.n
            P.wait("t", sV, xn_ready[(g, ch)])
            for kb in range(FBn):
                P.op("t", lambda e, kb=kb, ch=ch: e.matmul(
                    pm[0][0:8, 0:512], gwt_sb[:, kb * 8:(kb + 1) * 8],
                    xng[:, kb * D + ch * 512: kb * D + ch * 512 + 512],
                    start=(kb == 0), stop=(kb == FBn - 1)),
                    inc=(sT if kb == FBn - 1 else None))
            gate_t = sT.n
            P.wait("s", sT, gate_t)
            P.op("s", lambda e: e.activation(lg_sb[:, :], pm[0][0:8, 0:512], AF.Identity), inc=sS)
            lg_s = sS.n
            if g == 0 and ch == 0 and "lg0" in dbg:
                P.wait("y", sS, lg_s)
                dma("y", sDBG, dbg_d["lg0"][:, :], lg_sb[:, :])
            pm0_free = ("sS", lg_s)
            P.wait("t", sS, lg_s)
            P.wait("t", sV, xn_ready[(g, ch)])
            for k4 in range(4):
                P.op("t", lambda e, k4=k4: e.transpose(
                    pm[1][0:128, k4 * 8:(k4 + 1) * 8], lg_sb[0:8, k4 * 128:(k4 + 1) * 128],
                    id_f[0:8, 0:8]),
                    inc=(sT if k4 == 3 else None))
            lgT_t = sT.n
            P.wait("v", sT, lgT_t)
            P.op("v", lambda e: e.tensor_copy(lg_tm[:, :], pm[1][:, 0:32]), inc=sV)
            if g == 0 and ch == 0 and "lgtm0" in dbg:
                P.wait("y", sV, sV.n)
                dma("y", sDBG, dbg_d["lgtm0"][:, :], lg_tm[:, :])
            for b4 in range(4):
                sl = slice(b4 * 8, (b4 + 1) * 8)
                P.op("v", lambda e, sl=sl: e.reduce_max(mxc[:, :], lg_tm[:, sl], AX.X))
                vgap()
                P.op("v", lambda e: e.tensor_scalar(nmc[:, :], mxc[:, :], c_m1[:, :], None, ALU.mult),
                     inc=sV)
                nm_v = sV.n
                P.wait("s", sV, nm_v)
                P.op("s", lambda e, sl=sl: e.activation(ex_tm[:, sl], lg_tm[:, sl], AF.Exp,
                                                        bias=nmc[:, :]), inc=sS)
                ex_s = sS.n
                P.wait("v", sS, ex_s)
                P.op("v", lambda e, sl=sl: e.reduce_sum(smc[:, :], ex_tm[:, sl], AX.X))
                vgap()
                P.op("v", lambda e: e.reciprocal(rpc[:, :], smc[:, :]))
                vgap()
                P.op("v", lambda e, sl=sl: e.tensor_scalar(sc_tm[:, sl], ex_tm[:, sl],
                                                           rpc[:, :], None, ALU.mult))
                vgap()
                P.op("v", lambda e, sl=sl: e.tensor_add(bias_tm[:, sl], sc_tm[:, sl], gb_rep[:, :]))
                vgap()
                P.op("v", lambda e, sl=sl: e.reduce_max(m1c2[:, :], bias_tm[:, sl], AX.X))
                vgap()
                P.op("v", lambda e, sl=sl: e.tensor_scalar(mskc[:, :], bias_tm[:, sl],
                                                           m1c2[:, :], None, ALU.is_ge))
                vgap()
                P.op("v", lambda e: e.tensor_scalar(mkc[:, :], mskc[:, :], c_big[:, :], None, ALU.mult))
                vgap()
                P.op("v", lambda e, sl=sl: e.tensor_sub(mskd_tm[:, sl], bias_tm[:, sl], mkc[:, :]))
                vgap()
                P.op("v", lambda e, sl=sl: e.reduce_max(m2c2[:, :], mskd_tm[:, sl], AX.X))
                vgap()
                P.op("v", lambda e, sl=sl: e.tensor_scalar(keepc[:, :], bias_tm[:, sl],
                                                           m2c2[:, :], None, ALU.is_ge))
                vgap()
                P.op("v", lambda e, sl=sl: e.tensor_mul(rw_tm[:, sl], sc_tm[:, sl], keepc[:, :]),
                     inc=(sV if b4 == 3 else None))
            rw_v = sV.n
            if g == 0 and ch == 0 and "sctm0" in dbg:
                P.wait("y", sV, rw_v)
                dma("y", sDBG, dbg_d["sctm0"][:, :], sc_tm[:, :])
                dma("y", sDBG, dbg_d["gbrep"][:, :], gb_rep[:, :])
                dma("y", sDBG, dbg_d["biastm0"][:, :], bias_tm[:, :])
                dma("y", sDBG, dbg_d["mskdtm0"][:, :], mskd_tm[:, :])
                dma("y", sDBG, dbg_d["rwtm0"][:, :], rw_tm[:, :])
            P.wait("t", sV, rw_v)
            for k4 in range(4):
                P.op("t", lambda e, k4=k4: e.transpose(
                    pm[1][0:8, k4 * 128:(k4 + 1) * 128], rw_tm[:, k4 * 8:(k4 + 1) * 8],
                    id_f[:, :]),
                    inc=(sT if k4 == 3 else None))
            rwT_t = sT.n
            P.wait("s", sT, rwT_t)
            P.op("s", lambda e, ch=ch: e.activation(rw_fm[ch][:, :], pm[1][0:8, 0:512], AF.Identity),
                 inc=sS)
            rwfm_s[(g, ch)] = sS.n
            pm1_free = ("sS", sS.n)
            if g == 0 and ch == 0 and "rw0" in dbg:
                P.wait("y", sS, rwfm_s[(g, ch)])
                dma("y", sDBG, dbg_d["rw0"][:, :], rw_fm[0][:, :])

        elist = list(range(nexp)) if experts is None else experts
        for eidx in elist:
            L = g * NEXP + eidx
            shared = (eidx == 8)
            if L >= 2:
                P.wait("g", sT, a_end_t.get(L - 2, max(list(a_end_t.values()) or [0])))
            srcs = (w1d[eidx], w3d[eidx]) if not shared else (sw1[:, :], sw3[:, :])
            for wi, srcw in enumerate(srcs):
                ap = srcw[:, :] if not shared else srcw
                ap = ap.rearrange("(kb p) h -> p kb h", p=128)
                dstbuf = w13[L % 2]
                off = 0 if wi == 0 else W3OFF
                P.op("g", lambda e, ap=ap, dstbuf=dstbuf, off=off: e.dma_start(
                    out=dstbuf[:, off:off + 8 * H].rearrange("p (kb h) -> p kb h", kb=8),
                    in_=ap), inc=sW13[L % 2], inc_by=16)
            w3_ld[L] = sW13[L % 2].n
            if L >= 1:
                P.wait("g", sT, w2_end_t.get(L - 1, max(list(w2_end_t.values()) or [0])))
            srcw2 = w2d[eidx][:, :] if not shared else sw2[:, :]
            ap2 = srcw2.rearrange("(kb p) dd -> p kb dd", p=128)
            P.op("g", lambda e, ap2=ap2: e.dma_start(
                out=w2b[:, :].rearrange("p (kb dd) -> p kb dd", kb=HBn), in_=ap2),
                inc=sW2, inc_by=16)
            w2_ld[L] = sW2.n
            if not shared:
                for ch in range(NCH):
                    P.wait("t", sS, rwfm_s[(g, ch)])
                    wsrc("t", pm0_free)
                    P.op("t", lambda e, ch=ch, eidx=eidx: e.matmul(
                        pm[0][:, :], bass.AP(id_b, eidx, [[128, 8], [0, 128]]),
                        rw_fm[ch][0:8, :], start=True, stop=True), inc=sT)
                    rept = sT.n
                    P.wait("s", sT, rept)
                    P.wait("s", sV, hmul_last)
                    P.op("s", lambda e, ch=ch: e.activation(rep_b[ch][:, :], pm[0][:, :],
                                                            AF.Identity), inc=sS)
                    rep_s[ch] = sS.n
                    pm0_free = ("sS", sS.n)
            for ch in range(NCH):
                if g == 0 and eidx == tap_e and ch == 0 and "w13dump" in dbg:
                    P.wait("y", sW13[L % 2], w3_ld[L])
                    dma("y", sDBG, dbg_d["w13dump"][:, :], w13[L % 2][:, :])
                    P.wait("t", sDBG, sDBG.n)
                for hb in range(HBn):
                    p2 = hb % 2
                    if serial >= 1:
                        join_all()
                    P.wait("t", sW13[L % 2], w3_ld[L])
                    P.wait("t", sV, xn_ready[(g, ch)])
                    wsrc("t", pa_free[2 * p2])
                    wsrc("t", pa_free[2 * p2 + 1])
                    for kb in range(FBn):
                        P.op("t", lambda e, kb=kb, hb=hb, ch=ch, p2=p2, L=L: e.matmul(
                            pa[2 * p2][:, :], w13[L % 2][:, kb * H + hb * 128: kb * H + (hb + 1) * 128],
                            xng[:, kb * D + ch * 512: kb * D + ch * 512 + 512],
                            start=(kb == 0), stop=(kb == FBn - 1)),
                            inc=(sT if kb == FBn - 1 else None))
                    a1_t = sT.n
                    for kb in range(FBn):
                        P.op("t", lambda e, kb=kb, hb=hb, ch=ch, p2=p2, L=L: e.matmul(
                            pa[2 * p2 + 1][:, :],
                            w13[L % 2][:, W3OFF + kb * H + hb * 128: W3OFF + kb * H + (hb + 1) * 128],
                            xng[:, kb * D + ch * 512: kb * D + ch * 512 + 512],
                            start=(kb == 0), stop=(kb == FBn - 1)),
                            inc=(sT if kb == FBn - 1 else None))
                    a3_t = sT.n
                    bcol = (sb1_sb if shared else b1_sb)[:, (hb if shared else eidx * HBn + hb):
                                                         (hb if shared else eidx * HBn + hb) + 1]
                    bcol3 = (sb3_sb if shared else b3_sb)[:, (hb if shared else eidx * HBn + hb):
                                                          (hb if shared else eidx * HBn + hb) + 1]
                    P.wait("s", sT, a1_t)
                    P.wait("s", sV, t1_free[p2])
                    P.op("s", lambda e, p2=p2, bcol=bcol: e.activation(
                        t1b[p2][:, :], pa[2 * p2][:, :], AF.Identity, bias=bcol), inc=sS)
                    t1_s = sS.n
                    pa_free[2 * p2] = ("sS", t1_s)
                    P.wait("v", sS, t1_s)
                    P.wait("v", sT, a3_t)
                    wsrc("v", t3_free[p2])
                    P.op("v", lambda e, p2=p2, bcol3=bcol3: e.scalar_tensor_tensor(
                        t3b[p2][:, :], pa[2 * p2 + 1][:, :], bcol3, t1b[p2][:, :],
                        ALU.add, ALU.mult), inc=sV)
                    prod_v = sV.n
                    if g == 0 and eidx == tap_e and ch == 0 and hb == tap_hb and "t100" in dbg:
                        P.wait("y", sV, prod_v)
                        dma("y", sDBG, dbg_d["t100"][:, :], t1b[0][:, :])
                        dma("y", sDBG, dbg_d["t300"][:, :], t3b[0][:, :])
                        P.wait("s", sDBG, sDBG.n)
                    t1_free[p2] = prod_v
                    pa_free[2 * p2 + 1] = ("sV", prod_v)
                    P.wait("s", sV, prod_v)
                    if shared:
                        P.op("s", lambda e, p2=p2, hb=hb: e.activation(
                            hbuf[:, hb * 512:(hb + 1) * 512], t3b[p2][:, :], AF.Silu), inc=sS)
                        t3_free[p2] = ("sS", sS.n)
                    else:
                        P.op("s", lambda e, p2=p2: e.activation(t3b[p2][:, :], t3b[p2][:, :],
                                                                AF.Silu), inc=sS)
                        silu_s = sS.n
                        P.wait("v", sS, silu_s)
                        P.wait("v", sS, rep_s[ch])
                        P.op("v", lambda e, p2=p2, hb=hb, ch=ch: e.tensor_mul(
                            hbuf[:, hb * 512:(hb + 1) * 512], t3b[p2][:, :], rep_b[ch][:, :]),
                            inc=sV)
                        t3_free[p2] = ("sV", sV.n)
                        hmul_last = sV.n
                if g == 0 and eidx == tap_e and ch == 0 and "h00" in dbg:
                    P.wait("y", sV, hmul_last)
                    P.wait("y", sS, sS.n)
                    dma("y", sDBG, dbg_d["h00"][:, :], hbuf[:, :])
                    if "rep0" in dbg:
                        dma("y", sDBG, dbg_d["rep0"][:, :], rep_b[0][:, :])
                    P.wait("v", sDBG, sDBG.n)
                    P.wait("s", sDBG, sDBG.n)
                if ch == NCH - 1:
                    a_end_t[L] = sT.n
                P.wait("t", sW2, w2_ld[L])
                if shared:
                    P.wait("t", sS, sS.n)
                else:
                    P.wait("t", sV, hmul_last)
                for db in range(FBn):
                    pwb = pw[db % 2]
                    if serial >= 1:
                        join_all()
                    P.wait("t", sV, pw_free[db % 2])
                    first = True
                    if eidx == 0:
                        P.op("t", lambda e, db=db, ch=ch: e.matmul(
                            pw[db % 2][:, :], b2_sb[:, db * 128:(db + 1) * 128],
                            rw_fm[ch][:, :], start=True, stop=False))
                        first = False
                    if shared:
                        P.op("t", lambda e, db=db: e.matmul(
                            pw[db % 2][:, :], sb2_sb[:, db * 128:(db + 1) * 128],
                            ones_b[:, :], start=True, stop=False))
                        first = False
                    for kb in range(HBn):
                        P.op("t", lambda e, kb=kb, db=db, first=first: e.matmul(
                            pw[db % 2][:, :], w2b[:, kb * D + db * 128: kb * D + (db + 1) * 128],
                            hbuf[:, kb * 512:(kb + 1) * 512],
                            start=(first and kb == 0), stop=(kb == HBn - 1)),
                            inc=(sT if kb == HBn - 1 else None))
                    w2db_t = sT.n
                    P.wait("v", sT, w2db_t)
                    if eidx == elist[0]:
                        if g > 0 and db == 0 and ch == 0:
                            P.wait("v", sS, gsilu_s)
                        P.op("v", lambda e, db=db, ch=ch: e.tensor_copy(
                            yg[:, db * D + ch * 512: db * D + ch * 512 + 512], pw[db % 2][:, :]),
                            inc=sV)
                    else:
                        P.op("v", lambda e, db=db, ch=ch: e.tensor_add(
                            yg[:, db * D + ch * 512: db * D + ch * 512 + 512],
                            yg[:, db * D + ch * 512: db * D + ch * 512 + 512], pw[db % 2][:, :]),
                            inc=sV)
                    pw_free[db % 2] = sV.n
                if g == 0 and eidx == tap_e and ch == NCH - 1 and "ye0" in dbg:
                    P.wait("y", sV, sV.n)
                    dma("y", sDBG, dbg_d["ye0"][:, :], yg[:, :])
                    P.wait("t", sDBG, sDBG.n)
                if ch == NCH - 1:
                    w2_end_t[L] = sT.n
        if g + 1 < NGRP:
            for fb in range(4):
                slot = fb % 4
                P.wait("y", sV, xf_free_v[slot])
                c1 = (g + 1) * 1024
                xfl_pref[(g + 1, 0, fb)] = dma(
                    "y", sX[slot], xf_ring[slot][:, :],
                    x_fm_F[fb * 128:(fb + 1) * 128, c1:c1 + 512])
        if g == 0 and "y0" in dbg:
            P.wait("y", sV, sV.n)
            dma("y", sDBG, dbg_d["y0"][:, :], yg[:, :])
            P.wait("s", sDBG, sDBG.n)
        P.wait("s", sV, sV.n)
        P.wait("s", sT, w2_end_t[g * NEXP + elist[-1]])
        for fb in range(FBn):
            P.op("s", lambda e, fb=fb: e.activation(
                xng[:, fb * D:(fb + 1) * D], yg[:, fb * D:(fb + 1) * D], AF.Silu),
                inc=(sS if fb == FBn - 1 else None))
        gsilu_s = sS.n
        P.wait("y", sS, gsilu_s)
        P.op("y", lambda e, g=g: e.dma_start(
            out=g_scr[:, :].rearrange("(fb p) c -> p fb c", p=128)[:, :, g * 1024:(g + 1) * 1024],
            in_=xng[:, :].rearrange("p (fb c) -> p fb c", fb=FBn)),
            inc=sGS, inc_by=16)
        gdma_y = sGS.n
        if g == 0 and "g_grp0" in dbg:
            P.wait("y", sS, gsilu_s)
            dma("y", sDBG, dbg_d["g_grp0"][:, :], xng[:, :])

    ffn_done = {"sT": sT.n, "sV": sV.n, "sS": sS.n}


    for eng in ENGINES:
        for s_, v_ in ffn_done.items():
            P.wait(eng, {"sT": sT, "sV": sV, "sS": sS}[s_], v_)
    es2.close()
    for cm in (pm_cms[1], pm_cms[0], pw_cms[1], pw_cms[0],
               pa_cms[3], pa_cms[2], pa_cms[1], pa_cms[0]):
        cm.__exit__(None, None, None)

    es3 = contextlib.ExitStack()
    sb3 = lambda n, sh, d=F32: es3.enter_context(nc.sbuf_tensor(n, list(sh), d))
    wv_sb = sb3("wv_sb", (128, FBn * D), BF16)
    wo_sb = sb3("wo_sb", (128, 16 * D), BF16)
    v_jq = sb3("v_jq", (128, FBn * 4096), BF16)
    xnv = [sb3(f"xnv{i}", (128, FBn * 512), BF16) for i in range(2)]
    xv_ring = [sb3(f"xv{i}", (128, 512)) for i in range(2)]
    out_jq = [sb3(f"ojq{i}", (128, 4096), BF16) for i in range(2)]
    bd_sb = [sb3(f"bd{i}", (128, FBn * 128), BF16) for i in range(2)]
    s_tile = [sb3(f"stl{i}", (128, FBn * 256), BF16) for i in range(2)]
    g_tile = [sb3(f"gtl{i}", (128, FBn * 256), BF16) for i in range(2)]
    xr_tile = [sb3(f"xrt{i}", (128, FBn * 256)) for i in range(2)]
    pscr = sb3("pscr", (128, FBn * 256))
    pv_cms = [nc.psum_tensor(f"pv{i}", [128, 512], F32) for i in range(2)]
    pv = [cm.__enter__() for cm in pv_cms]
    pav_cms = [nc.psum_tensor(f"pav{i}", [128, 512], F32) for i in range(2)]
    pav = [cm.__enter__() for cm in pav_cms]
    ps_cms = [nc.psum_tensor(f"ps{i}", [128, 512], F32) for i in range(4)]
    ps_ = [cm.__enter__() for cm in ps_cms]

    dma("g", sW13[0], wv_sb[:, :].rearrange("p (kb c) -> p kb c", kb=FBn),
        wv[:, :].rearrange("(kb p) c -> p kb c", p=128))
    wv_ld = sW13[0].n
    dma("g", sW13[1], wo_sb[:, :], wo[:, :])
    wo_ld = sW13[1].n

    xnv_free_t = [0, 0]
    xv_free_v = [0] * 4
    vp_free = [0, 0]
    vch_evac = {}
    comb_v_free = {}
    stores_prev = {}
    for c in range(8):
        par = c % 2
        P.wait("t", sV, selrows_done)
        if c >= 1:
            P.wait("t", sV, vch_evac[c - 1])
        for k4 in range(4):
            row = 4 * c + k4
            P.op("t", lambda e, row=row, k4=k4: e.matmul(
                ps_[0][:, k4 * 128:(k4 + 1) * 128],
                bass.AP(id_b, row, [[128, 32], [0, 128]]), rV_row[0:32, :],
                start=True, stop=True))
            P.op("t", lambda e, row=row, k4=k4: e.matmul(
                ps_[1][:, k4 * 128:(k4 + 1) * 128],
                bass.AP(id_b, row, [[128, 32], [0, 128]]), mrV_row[0:32, :],
                start=True, stop=True),
                inc=(sT if k4 == 3 else None))
        ab_t = sT.n
        P.wait("v", sT, ab_t)
        if c >= 2:
            P.wait("v", sT, xnv_free_t[par])
        for fb in range(FBn):
            slot = (c * FBn + fb) % 2
            P.wait("y", sV, xv_free_v[slot])
            ld_t = dma("y", sX[slot], xv_ring[slot][:, :],
                       x_fm_V[fb * 128:(fb + 1) * 128, c * 512:(c + 1) * 512])
            P.wait("v", sX[slot], ld_t)
            xv = xv_ring[slot]
            P.op("v", lambda e, xv=xv: e.tensor_mul(xv[:, :], xv[:, :], ps_[0][:, :]))
            P.op("v", lambda e, xv=xv, fb=fb, par=par: e.tensor_sub(
                xnv[par][:, fb * 512:(fb + 1) * 512], xv[:, :], ps_[1][:, :]), inc=sV)
            xv_free_v[slot] = sV.n
        xn_v = sV.n
        P.wait("t", sV, xn_v)
        P.wait("t", sW13[0], wv_ld)
        for mb in range(FBn):
            pb = pv[mb % 2]
            if c * FBn + mb >= 2:
                P.wait("t", sV, vp_free[mb % 2])
            for kb in range(FBn):
                P.op("t", lambda e, kb=kb, mb=mb, par=par, pb=pb: e.matmul(
                    pb[:, :], wv_sb[:, kb * D + mb * 128: kb * D + (mb + 1) * 128],
                    xnv[par][:, kb * 512:(kb + 1) * 512],
                    start=(kb == 0), stop=(kb == FBn - 1)),
                    inc=(sT if kb == FBn - 1 else None))
            mm_t = sT.n
            P.wait("v", sT, mm_t)
            dst = bass.AP(v_jq, mb * 4096 + c * 32, [[FBn * 4096, 128], [256, 16], [1, 32]])
            srcp = bass.AP(pb, 0, [[512, 128], [1, 16], [16, 32]])
            P.op("v", lambda e, dst=dst, srcp=srcp: e.tensor_copy(dst, srcp), inc=sV)
            vp_free[mb % 2] = sV.n
        vch_evac[c] = sV.n
        xnv_free_t[par] = sT.n
    v_done = sV.n
    if "v_jq" in dbg:
        P.wait("y", sV, v_done)
        dma("y", sDBG, dbg_d["v_jq"][:, :], v_jq[:, :])

    bd_free_t = [0, 0]
    ojq_free_t = [0, 0]
    st_done = {}
    comb_free = {}
    gx_ld = {}
    pav_free_v = 0
    for tp in range(8):
        par_t = tp % 2
        for par in range(2):
            a = 2 * tp + par
            gx_ld[(tp, par, "g")] = dma(
                "y", sGR[par], g_tile[par][:, :].rearrange("p (fb c) -> p fb c", fb=FBn),
                g_scr[:, :].rearrange("(fb p) c -> p fb c", p=128)[:, :, a * 256:(a + 1) * 256])
            gx_ld[(tp, par, "x")] = dma(
                "y", sXR[par], xr_tile[par][:, :].rearrange("p (fb c) -> p fb c", fb=FBn),
                x_fm_F[:, :].rearrange("(fb p) c -> p fb c", p=128)[:, :, a * 256:(a + 1) * 256])
        P.wait("t", sV, attT_done)
        P.wait("t", sT, bd_free_t[par_t])
        P.wait("t", sV, pav_free_v)
        P.wait("t", sINIT, init_wq)
        rhs = bass.AP(attT, 2 * tp, [[16, 16], [1, 2], [0, 64]])
        for t in range(8):
            P.op("t", lambda e, t=t, rhs=rhs: e.matmul(
                pav[t // 4][:, (t % 4) * 128:(t % 4 + 1) * 128],
                selbk_sb[:, t * 128:(t + 1) * 128], rhs,
                start=True, stop=True),
                inc=(sT if t % 4 == 3 else None))
        rep_t = sT.n
        P.wait("v", sT, rep_t)
        P.op("v", lambda e, par_t=par_t: e.tensor_mul(
            bd_sb[par_t][:, 0:512], pav[0][:, :], p0d_b[:, :]))
        P.op("v", lambda e, par_t=par_t: e.tensor_mul(
            bd_sb[par_t][:, 512:1024], pav[1][:, :], p0d_b[:, :]), inc=sV)
        bd_v = sV.n
        P.wait("t", sV, bd_v)
        P.wait("t", sV, v_done)
        P.wait("t", sT, ojq_free_t[par_t])
        for cc in range(8):
            pb = pav[cc % 2]
            for t in range(8):
                P.op("t", lambda e, t=t, cc=cc, pb=pb, par_t=par_t: e.matmul(
                    pb[:, :], bd_sb[par_t][:, t * 128:(t + 1) * 128],
                    v_jq[:, t * 4096 + cc * 512: t * 4096 + (cc + 1) * 512],
                    start=(t == 0), stop=(t == 7)),
                    inc=(sT if t == 7 else None))
            avt = sT.n
            P.wait("v", sT, avt)
            P.op("v", lambda e, cc=cc, pb=pb, par_t=par_t: e.tensor_copy(
                out_jq[par_t][:, cc * 512:(cc + 1) * 512], pb[:, :]), inc=sV)
        ojq_v = sV.n
        pav_free_v = ojq_v
        if tp == 0 and "attv" in dbg:
            P.wait("y", sV, ojq_v)
            dma("y", sDBG, dbg_d["attv"][:, :], out_jq[0][:, :])
        P.wait("t", sV, ojq_v)
        P.wait("t", sW13[1], wo_ld)
        for half in range(2):
            for par in range(2):
                a_loc = par
                for ob in range(half * 4, half * 4 + 4):
                    P.wait("t", sS, comb_free.get((tp * 2 + par - 4, ob), 0))
                    for j in range(16):
                        P.op("t", lambda e, j=j, ob=ob, par=par, par_t=par_t: e.matmul(
                            ps_[ob % 4][:, par * 256:(par + 1) * 256],
                            wo_sb[par * 64:(par + 1) * 64, j * D + ob * 128:(j * D + (ob + 1) * 128)],
                            out_jq[par_t][par * 64:(par + 1) * 64, j * 256:(j + 1) * 256],
                            start=(j == 0), stop=(j == 15),
                            tile_position=(par * 64, 0)),
                            inc=(sT if j == 15 else None))
                    st_done[(par, ob)] = sT.n
            for par in range(2):
                for ob in range(half * 4, half * 4 + 4):
                    P.wait("s", sT, st_done[(par, ob)])
                    P.wait("s", sV, comb_v_free.get(par, 0))
                    P.op("s", lambda e, ob=ob, par=par: e.activation(
                        s_tile[par][:, ob * 256:(ob + 1) * 256],
                        ps_[ob % 4][:, par * 256:(par + 1) * 256],
                        AF.Identity, bias=bo_sb[:, ob:ob + 1]),
                        inc=sS)
                    comb_free[(tp * 2 + par, ob)] = sS.n
        s_done = sS.n
        for par in range(2):
            a = 2 * tp + par
            gld = gx_ld[(tp, par, "g")]
            xld = gx_ld[(tp, par, "x")]
            P.wait("v", sGR[par], gld)
            P.wait("v", sXR[par], xld)
            P.wait("v", sS, s_done)
            P.wait("v", sOUT[par], stores_prev.get(par, 0))
            for ob in range(FBn):
                P.op("v", lambda e, ob=ob, par=par: e.tensor_mul(
                    pscr[:, ob * 256:(ob + 1) * 256],
                    s_tile[par][:, ob * 256:(ob + 1) * 256],
                    g_tile[par][:, ob * 256:(ob + 1) * 256]))
            for ob in range(FBn):
                P.op("v", lambda e, ob=ob, par=par: e.tensor_add(
                    xr_tile[par][:, ob * 256:(ob + 1) * 256],
                    xr_tile[par][:, ob * 256:(ob + 1) * 256],
                    pscr[:, ob * 256:(ob + 1) * 256]),
                    inc=(sV if ob == FBn - 1 else None))
            comb_v = sV.n
            comb_v_free[par] = comb_v
            P.wait("y", sV, comb_v)
            st = dma("y", sOUT[par],
                     out_d[:, :].rearrange("(fb p) c -> p fb c", p=128)[:, :, a * 256:(a + 1) * 256],
                     xr_tile[par][:, :].rearrange("p (fb c) -> p fb c", fb=FBn))
            stores_prev[par] = st
            if a == 0 and "s_head0" in dbg:
                P.wait("y", sS, s_done)
                dma("y", sDBG, dbg_d["s_head0"][:, :], s_tile[0][:, :])
        bd_free_t[par_t] = sT.n
        ojq_free_t[par_t] = sT.n

    P.wait("y", sDBG, sDBG.n)
    P.wait("y", sGS, sGS.n)
    for s_ in sOUT:
        P.wait("y", s_, s_.n)
    P.wait("y", sV, attT_done)
    P.wait("y", sV, rows_done)
    P.wait("y", sV, gbrep_done)
    P.validate()
    with nc.Block() as block:
        P.replay(block)
    es3.close()
    es0.close()
    return nc


BF_ML = ml_dtypes.bfloat16


def make_consts():
    idn = np.eye(128, dtype=np.float32)
    onesr = np.ones((1, 512), np.float32)
    onec = np.ones((1, 128), np.float32)
    sel2 = np.zeros((2, 128), np.float32)
    sel2[0, 0:64] = 1.0
    sel2[1, 64:128] = 1.0
    p0d = np.zeros((128, 128), np.float32)
    for p in range(128):
        for f in range(p % 64, 128, 64):
            p0d[p, f] = 1.0
    p0d = np.tile(p0d, (1, 4))
    return idn, onesr, onec, sel2, p0d


def host_inputs(inp, core):
    n, h = core // 2, core % 2
    x = np.ascontiguousarray(np.asarray(inp["L0_attn_x"], np.float32)[n])
    F_rows = np.array([a * 512 + h * 256 + q for a in range(16) for q in range(256)])
    perm_da = np.array([(i % 16) * 64 + i // 16 for i in range(D)])
    Wq = np.asarray(inp["L0_attn_Wq"], np.float32)
    Wk = np.asarray(inp["L0_attn_Wk"], np.float32)
    Wo = np.asarray(inp["L0_attn_Wo"], np.float32)
    wo_dup = np.empty((128, 16 * D), np.float32)
    for j in range(16):
        blk = Wo[64 * j:64 * (j + 1), :]
        wo_dup[0:64, j * D:(j + 1) * D] = blk
        wo_dup[64:128, j * D:(j + 1) * D] = blk
    bo = np.asarray(inp["L0_attn_bo"], np.float32)
    bo_p = np.ascontiguousarray(bo.reshape(8, 128).T)
    gate_w = np.asarray(inp["L0_ffn_gate_w"], np.float32)
    b1 = np.asarray(inp["L0_ffn_b1"], np.float32)
    b3 = np.asarray(inp["L0_ffn_b3"], np.float32)
    b1p = np.ascontiguousarray(b1.reshape(8, 11, 128).transpose(2, 0, 1).reshape(128, 88))
    b3p = np.ascontiguousarray(b3.reshape(8, 11, 128).transpose(2, 0, 1).reshape(128, 88))
    sb1_ = np.asarray(inp["L0_ffn_sb1"], np.float32).reshape(11, 128).T
    sb3_ = np.asarray(inp["L0_ffn_sb3"], np.float32).reshape(11, 128).T
    idn, onesr, onec, sel2, p0d = make_consts()
    pFT = np.zeros((64, 32), np.float32)
    for k in range(32):
        pFT[4 * (k // 2) + 2 * h + (k % 2), k] = 1.0
    pVT = np.zeros((64, 32), np.float32)
    for k in range(32):
        pVT[32 * h + k, k] = 1.0
    selbk = np.zeros((16, 8 * 128), np.float32)
    for t in range(8):
        for p in range(128):
            selbk[2 * t + p // 64, t * 128 + p] = 1.0
    d = {
        "x_tm": x,
        "x_fm_V": np.ascontiguousarray(x[h * 4096:(h + 1) * 4096].T),
        "x_fm_F": np.ascontiguousarray(x[F_rows].T),
        "wq": np.ascontiguousarray(Wq[:, perm_da]),
        "wk": np.ascontiguousarray(Wk[:, perm_da]),
        "wv": np.asarray(inp["L0_attn_Wv"], np.float32),
        "wo": wo_dup,
        "bo_p": np.ascontiguousarray(bo_p),
        "gwt": np.ascontiguousarray(gate_w.T),
        "gbr": np.asarray(inp["L0_ffn_gate_b"], np.float32).reshape(1, 8),
        "w1d": np.asarray(inp["L0_ffn_w1"], np.float32),
        "w3d": np.asarray(inp["L0_ffn_w3"], np.float32),
        "w2d": np.asarray(inp["L0_ffn_w2"], np.float32),
        "b1p": b1p, "b3p": b3p,
        "b2a": np.asarray(inp["L0_ffn_b2"], np.float32),
        "sw1": np.asarray(inp["L0_ffn_sw1"], np.float32),
        "sw3": np.asarray(inp["L0_ffn_sw3"], np.float32),
        "sw2": np.asarray(inp["L0_ffn_sw2"], np.float32),
        "sb1p": np.ascontiguousarray(sb1_), "sb3p": np.ascontiguousarray(sb3_),
        "sb2r": np.asarray(inp["L0_ffn_sb2"], np.float32).reshape(1, D),
        "idn": idn, "onesr": onesr, "onec": onec, "sel2": sel2, "p0d": p0d,
        "pFT": pFT, "pVT": pVT, "selbk": selbk,
    }
    return d, F_rows


def kernel(**inputs):
    nc = build_nc()
    in_maps = []
    frows = []
    for c in range(8):
        m, fr = host_inputs(inputs, c)
        in_maps.append(m)
        frows.append(fr)
    res = run_bass_kernel_spmd(nc, in_maps, core_ids=list(range(8)))
    out = np.empty((4, S, D), np.float32)
    for c in range(8):
        n = c // 2
        out[n, frows[c], :] = res.results[c]["out"].T
    return out
